# version 13
# speedup vs baseline: 1.1631x; 1.0088x over previous
import math
import numpy as np

import concourse.bass as bass
import concourse.mybir as mybir
from concourse.bass_utils import run_bass_kernel_spmd
from concourse.alu_op_type import AluOpType as OP

F32 = mybir.dt.float32
BF16 = mybir.dt.bfloat16
AF = mybir.ActivationFunctionType

ENGINES = ("tensor", "vector", "scalar", "gpsimd", "sync")
DMA_POOL = 8


def _dma_base(engine):
    if engine == "gpsimd":
        return "gpsimd_dma"
    if engine == "scalar":
        return "scalar_dma"
    return engine


def _chan_engine(chan):
    base = chan.split("#")[0]
    return {"gpsimd_dma": "gpsimd", "scalar_dma": "scalar"}.get(base, base)



class _Op:
    __slots__ = ("engine", "fn", "reads", "writes", "idx", "inc", "count_after",
                 "deps", "tag", "_dma", "chan")

    def __init__(self, engine, fn, reads, writes, tag=""):
        self.engine = engine
        self.fn = fn
        self.reads = reads
        self.writes = writes
        self.idx = -1
        self.inc = 0
        self.count_after = 0
        self.deps = []
        self.tag = tag


class Prog:
    TENSOR_SAME_ENGINE_SYNC = False
    OTHER_SAME_ENGINE_SYNC = True
    PRUNE = True

    def __init__(self):
        self.ops = []
        self._last_write = {}
        self._readers = {}
        self._barrier_deps = []
        self._force_inc = set()
        self._dma_counts = {}

    def add(self, engine, fn, reads=(), writes=(), dma=False, tag=""):
        op = _Op(engine, fn, tuple(reads), tuple(writes), tag)
        op.idx = len(self.ops)
        op._dma = dma
        if dma:
            base = _dma_base(engine)
            i = self._dma_counts.get(base, 0)
            self._dma_counts[base] = i + 1
            op.chan = f"{base}#{i % DMA_POOL}"
        else:
            op.chan = engine
        deps = set(self._barrier_deps)
        for k in op.reads:
            w = self._last_write.get(k)
            if w is not None:
                deps.add(w)
        for k in op.writes:
            w = self._last_write.get(k)
            if w is not None:
                deps.add(w)
            for r in self._readers.get(k, ()):
                deps.add(r)
        deps.discard(op.idx)
        op.deps = sorted(deps)
        for k in op.reads:
            self._readers.setdefault(k, []).append(op.idx)
        for k in op.writes:
            self._last_write[k] = op.idx
            self._readers[k] = []
        self.ops.append(op)
        return op

    def rekey(self, new_key, *old_keys):
        allv = []
        for o in old_keys:
            w = self._last_write.pop(o, None)
            if w is not None:
                allv.append(w)
            allv += self._readers.pop(o, [])
        allv = sorted(set(allv))
        if allv:
            self._readers[new_key] = allv
            self._last_write[new_key] = allv[-1]

    def barrier(self):
        last = {}
        for op in self.ops:
            last[op.chan] = op.idx
        self._barrier_deps = sorted(last.values())
        self._force_inc.update(self._barrier_deps)
        self._last_write = {}
        self._readers = {}

    def t(self, fn, reads=(), writes=(), tag=""):
        return self.add("tensor", fn, reads, writes, tag=tag)

    def v(self, fn, reads=(), writes=(), tag=""):
        return self.add("vector", fn, reads, writes, tag=tag)

    def s(self, fn, reads=(), writes=(), tag=""):
        return self.add("scalar", fn, reads, writes, tag=tag)

    def g(self, fn, reads=(), writes=(), tag=""):
        return self.add("gpsimd", fn, reads, writes, tag=tag)

    def dma(self, queue, fn, reads=(), writes=(), tag=""):
        return self.add(queue, fn, reads, writes, dma=True, tag=tag)

    def seal(self):
        is_target = [False] * len(self.ops)
        for i in self._force_inc:
            is_target[i] = True
        for op in self.ops:
            for d in op.deps:
                dep = self.ops[d]
                if dep.engine == op.engine:
                    if dep.engine == "tensor" and not self.TENSOR_SAME_ENGINE_SYNC:
                        continue
                    if dep.engine != "tensor" and not self.OTHER_SAME_ENGINE_SYNC:
                        continue
                is_target[d] = True
        self.channels = sorted({op.chan for op in self.ops})
        counts = {c: 0 for c in self.channels}
        for op in self.ops:
            if is_target[op.idx] or op._dma:
                op.inc = 16 if op._dma else 1
                counts[op.chan] += op.inc
            op.count_after = counts[op.chan]
        self.final_counts = counts
        vc = {e: {c: 0 for c in self.channels} for e in ENGINES}
        op_vc = [None] * len(self.ops)
        self.op_waits = [None] * len(self.ops)
        for op in self.ops:
            waits = []
            myvc = vc[op.engine]
            for d in sorted(op.deps, key=lambda i: -self.ops[i].count_after):
                dep = self.ops[d]
                if dep.engine == op.engine:
                    if dep.engine == "tensor" and not self.TENSOR_SAME_ENGINE_SYNC:
                        continue
                    if dep.engine != "tensor" and not self.OTHER_SAME_ENGINE_SYNC:
                        continue
                assert dep.inc > 0, (dep.tag, op.tag)
                if self.PRUNE and myvc[dep.chan] >= dep.count_after:
                    continue
                waits.append((dep.chan, dep.count_after))
                dvc = op_vc[d]
                for e2, c2 in dvc.items():
                    if c2 > myvc[e2]:
                        myvc[e2] = c2
                if dep.count_after > myvc[dep.chan]:
                    myvc[dep.chan] = dep.count_after
            dedup = {}
            for (c, v) in waits:
                dedup[c] = max(dedup.get(c, 0), v)
            waits = sorted(dedup.items())
            self.op_waits[op.idx] = waits
            done_vc = dict(myvc)
            done_vc[op.chan] = max(done_vc[op.chan], op.count_after)
            op_vc[op.idx] = done_vc
        return self

    def emit(self, nc):
        import contextlib
        with contextlib.ExitStack() as stack:
            sems = {c: stack.enter_context(nc.semaphore(f"S_{c}".replace("#", "_")))
                    for c in self.channels}
            block = stack.enter_context(nc.Block())
            per_engine = {e: [] for e in ENGINES}
            for op in self.ops:
                per_engine[op.engine].append(op)

            def make_section(engine):
                ops = per_engine[engine]

                def section(eng):
                    for op in ops:
                        if op._dma and op.count_after - op.inc > 0:
                            eng.wait_ge(sems[op.chan], op.count_after - op.inc)
                        for (we, wv) in self.op_waits[op.idx]:
                            eng.wait_ge(sems[we], wv)
                        instr = op.fn(eng)
                        if op.inc:
                            assert instr is not None, f"op {op.tag} needs inc but returned None"
                            instr.then_inc(sems[op.chan], op.inc)
                    for c in self.channels:
                        if self.final_counts[c] and _chan_engine(c) == engine:
                            eng.wait_ge(sems[c], self.final_counts[c])
                return section

            for e in ENGINES:
                if per_engine[e]:
                    getattr(block, e)(make_section(e))
        return self

    def make_const_pool(self, nc, sbuf_tensor, dtype=F32):
        pool = {}

        def constval(x):
            x = float(x)
            if x == 0.0:
                return 0.0
            if x not in pool:
                i = len(pool)
                assert i < sbuf_tensor.shape[1]
                ap = sbuf_tensor[:, i:i + 1]
                self.v(lambda e, ap=ap, x=x: e.memset(ap, x), writes=["cpool"],
                       tag=f"const_{x}")
                nc.const_aps.register(dtype, x, ap)
                pool[x] = x
            return x
        return constval

    def stats(self):
        n_by_eng = {e: 0 for e in ENGINES}
        n_waits = 0
        n_incs = 0
        for op in self.ops:
            n_by_eng[op.engine] += 1
            n_waits += len(self.op_waits[op.idx])
            n_incs += 1 if op.inc else 0
        return dict(n_ops=len(self.ops), per_engine=n_by_eng, waits=n_waits,
                    incs=n_incs, final=self.final_counts)



D = 1024
NCORES = 8
AX = mybir.AxisListType

F32SZ = 4


def ceil_div(a, b):
    return (a + b - 1) // b


class FMView:

    def __init__(self, t, off, nkp, W):
        self.t, self.off, self.nkp, self.W = t, off, nkp, W

    def sl(self, kp, c0, n, r0=0, nr=128):
        assert kp < self.nkp and c0 + n <= self.W, (kp, c0, n, self.nkp, self.W)
        return self.t[r0:r0 + nr, self.off + kp * self.W + c0:
                      self.off + kp * self.W + c0 + n]


class MK:

    def __init__(self, nc, p, T, weights_np):
        self.nc, self.p, self.T = nc, p, T
        self.PT = T // 128
        self.weights_np = weights_np
        self.dram = {}
        self._stack = None
        self.uid = 0

    def u(self, tag):
        self.uid += 1
        return f"{tag}_{self.uid}"

    def din(self, name, shape, dtype=F32):
        ap = self.nc.dram_tensor(name, list(shape), dtype, kind="ExternalInput").ap()
        self.dram[name] = ap
        return ap

    def dout(self, name, shape, dtype=F32):
        ap = self.nc.dram_tensor(name, list(shape), dtype, kind="ExternalOutput").ap()
        self.dram[name] = ap
        return ap

    def dscratch(self, name, shape, dtype=F32):
        ap = self.nc.dram_tensor(name, list(shape), dtype, kind="Internal").ap()
        self.dram[name] = ap
        return ap

    def alloc_pools(self, stack):
        nc, T = self.nc, self.T
        sb = lambda name, shape, dt=BF16: stack.enter_context(nc.sbuf_tensor(name, shape, dt))
        ps = lambda name, shape, dt=F32: stack.enter_context(nc.psum_tensor(name, shape, dt))
        self.A = sb("pool_A", [128, max(8 * T, 32768)])
        self.B = sb("pool_B", [128, max(8 * T, 32768)])
        self.M = sb("pool_M", [128, 16448])
        self.WT = sb("pool_WT", [128, 8192])
        self.F32P = sb("pool_F32", [128, 2560], F32)
        self.TRIG = sb("pool_TRIG", [128, 4096])
        self.E = sb("pool_E", [128, 1024])
        self.TOKB = sb("pool_TOKB", [128, 2048])
        self.STAT = sb("pool_STAT", [128, 16], F32)
        self.BCOL = sb("pool_BCOL", [128, 48], F32)
        self.ROWS = sb("pool_ROWS", [1, 1024], F32)
        self.CP = sb("pool_CP", [128, 16], F32)
        self.IDB = sb("ident_bf", [128, 128])
        self.IDF = sb("ident_f32", [128, 128], F32)
        self.ONESF = sb("ones_f32", [1, 128], F32)
        self.ONE11F = sb("one11_f32", [1, 1], F32)
        self.TRI = sb("tri_bf", [128, 128])
        self.PF = [ps(f"psF{i}", [128, 1024], F32) for i in range(3)]
        self.PBF = [ps(f"psB{i}", [128, 1024], BF16) for i in range(2)]
        self.constval = self.p.make_const_pool(nc, self.CP)
        self._bcol_cursor = 0

    def bank(self, i):
        t = self.PF[i // 2]
        half = i % 2
        return t[:, half * 512:(half + 1) * 512], f"psF{i // 2}_{half}"

    def bank_pair(self, i):
        return self.PF[i], (f"psF{i}_0", f"psF{i}_1")

    def load_consts(self, consts_dram):
        p = self.p
        p.dma("sync", lambda e: e.dma_start(out=self.IDF[:, :], in_=consts_dram[0:128, :]),
              writes=["IDF"], tag="ld_idf")
        p.dma("gpsimd", lambda e: e.dma_start(out=self.IDB[:, :], in_=consts_dram[0:128, :]),
              writes=["IDB"], tag="ld_idb")
        p.dma("gpsimd", lambda e: e.dma_start(out=self.TRI[:, :], in_=consts_dram[128:256, :]),
              writes=["TRI"], tag="ld_tri")
        p.v(lambda e: e.memset(self.ONESF[:, :], 1.0), writes=["ONESF"], tag="ms_onesf")
        p.v(lambda e: e.memset(self.ONE11F[:, :], 1.0), writes=["ONE11F"], tag="ms_one11")

    def bias_col(self, wname, n=None, scale=None):
        p, nc = self.p, self.nc
        vec = self.dram[wname]
        Dv = vec.shape[0] if n is None else n
        ncols = ceil_div(Dv, 128)
        c0 = self._bcol_cursor
        assert c0 + ncols <= 48, "BCOL overflow"
        self._bcol_cursor += ncols
        dst = self.BCOL[:, c0:c0 + ncols]
        key = f"bcol_{wname}_{c0}"
        psum, pkey = self.bank(4)
        for h0 in range(0, Dv, 512):
            hn = min(512, Dv - h0)
            rk = f"rows_bias_{wname}_{h0}"
            p.dma("sync", lambda e, h0=h0, hn=hn: e.dma_start(
                out=self.ROWS[0:1, 0:hn], in_=vec[h0:h0 + hn].rearrange("(one n) -> one n", one=1)),
                writes=["ROWS"], tag=self.u("ld_bias"))
            nch = ceil_div(hn, 128)
            for j in range(nch):
                jn = min(128, hn - j * 128)
                p.t(lambda e, j=j, jn=jn: e.matmul(
                    psum[0:jn, j:j + 1], self.ROWS[0:1, j * 128:j * 128 + jn],
                    self.ONE11F[:, :], start=True, stop=True),
                    reads=["ROWS", "ONE11F"], writes=[pkey], tag=self.u("r2c"))
            base = h0 // 128
            if scale is None:
                p.v(lambda e, nch=nch, base=base: e.tensor_copy(
                    dst[:, base:base + nch], psum[:, 0:nch]),
                    reads=[pkey], writes=[key, "BCOL"], tag=self.u("ev_bias"))
            else:
                p.v(lambda e, nch=nch, base=base: e.tensor_scalar(
                    dst[:, base:base + nch], psum[:, 0:nch], float(scale), None, op0=OP.mult),
                    reads=[pkey], writes=[key, "BCOL"], tag=self.u("ev_bias"))
        return dst, key

    def load_weight_fm(self, wname, queue="gpsimd", dst=None, dstoff=0, key=None,
                       rowslice=None, colslice=None, tag=None, expert=None,
                       src_key=None):
        p = self.p
        w = self.dram[wname]
        if expert is not None:
            w = w[expert]
        R, C = w.shape
        if colslice is not None:
            c0, cn = colslice
        else:
            c0, cn = 0, C
        if rowslice is not None:
            r0, rn = rowslice
            w = w[r0:r0 + rn, :]
            R = rn
        nkp = R // 128
        key = key or f"wfm_{wname}_{dstoff}_{c0}"
        view = FMView(dst, dstoff, nkp, cn)
        src = w[:, c0:c0 + cn].rearrange("(kp p) c -> p kp c", p=128)
        ap_out = dst[:, dstoff:dstoff + nkp * cn].rearrange(
            "p (kp c) -> p kp c", kp=nkp)
        rd = [src_key] if src_key else []
        p.dma(queue, lambda e: e.dma_start(out=ap_out, in_=src),
              reads=rd, writes=[key], tag=tag or self.u(f"ldw_{wname}"))
        return view, key

    @staticmethod
    def fm_tile_ap(view, t0, n):
        full = view.t[:, view.off:view.off + view.nkp * view.W]
        r = full.rearrange("p (j w) -> p j w", j=view.nkp)
        return r[:, :, t0:t0 + n]

    def ln_transpose(self, src_key, src_dram, dst, dst_key, eps=1e-5,
                     affine=None, tok_f32=None, prefix="ln", tiles=None, dst_c0=0,
                     hook=None, xt_db_ok=True):
        p, T = self.p, self.T
        cv = self.constval
        eps_c = cv(eps)
        t0r, t1r = (0, self.PT) if tiles is None else tiles
        xt_db = tok_f32 is None and hook is None and xt_db_ok
        for t in range(t0r, t1r):
            tl = t - t0r
            sl = t % 2
            xt = self.F32P[:, sl * 1024:(sl + 1) * 1024] if xt_db                 else self.F32P[:, 0:1024]
            xk = f"{prefix}_xt{sl}" if xt_db else f"{prefix}_xt"
            st = self.STAT[:, sl * 8:(sl + 1) * 8]
            sk = f"{prefix}_st{sl}"
            xn = self.TOKB[:, sl * 1024:(sl + 1) * 1024]
            xnk = f"{prefix}_xn{sl}"
            junk = self.E[:, 0:1024]
            pbf = self.PBF[sl]
            pbk = f"psBF{sl}"
            p.dma("sync", lambda e, t=t, xt=xt: e.dma_start(
                out=xt, in_=src_dram[t * 128:(t + 1) * 128, :]),
                reads=[src_key], writes=[xk], tag=self.u(f"{prefix}_ldx"))
            p.v(lambda e, xt=xt, st=st: e.tensor_reduce(
                st[:, 0:1], xt, axis=AX.X, op=OP.add),
                reads=[xk], writes=[sk], tag=self.u(f"{prefix}_sum"))
            p.v(lambda e, st=st: e.tensor_scalar(
                st[:, 1:2], st[:, 0:1], -1.0 / D, None, op0=OP.mult),
                reads=[sk], writes=[sk], tag=self.u(f"{prefix}_nm"))
            p.s(lambda e, xt=xt, st=st, junk=junk: e.activation(
                junk, xt, AF.Square, bias=st[:, 1:2], accum_out=st[:, 2:3]),
                reads=[xk, sk], writes=["ln_junk", sk], tag=self.u(f"{prefix}_sq"))
            p.s(lambda e, st=st: e.activation(
                st[:, 3:4], st[:, 2:3], AF.Ln, scale=1.0 / D, bias=eps_c),
                reads=[sk, "cpool"], writes=[sk], tag=self.u(f"{prefix}_lnv"))
            p.s(lambda e, st=st: e.activation(
                st[:, 4:5], st[:, 3:4], AF.Exp, scale=-0.5),
                reads=[sk], writes=[sk], tag=self.u(f"{prefix}_rstd"))
            p.v(lambda e, st=st: e.tensor_tensor(
                st[:, 5:6], st[:, 1:2], st[:, 4:5], op=OP.mult),
                reads=[sk], writes=[sk], tag=self.u(f"{prefix}_nmr"))
            p.s(lambda e, xt=xt, st=st, xn=xn: e.activation(
                xn, xt, AF.Identity, scale=st[:, 4:5], bias=st[:, 5:6]),
                reads=[xk, sk], writes=[xnk], tag=self.u(f"{prefix}_norm"))
            for j in range(8):
                p.t(lambda e, xn=xn, pbf=pbf, j=j: e.transpose(
                    pbf[:, j * 128:(j + 1) * 128], xn[:, j * 128:(j + 1) * 128],
                    self.IDB[:, :]),
                    reads=[xnk, "IDB"], writes=[pbk], tag=self.u(f"{prefix}_tr"))
            tkey = f"{dst_key}.t{dst_c0 // 128 + tl}"
            if affine is None:
                dap = self.fm_tile_ap(dst, dst_c0 + tl * 128, 128)
                sap = pbf[:, 0:1024].rearrange("p (j r) -> p j r", j=8)
                p.v(lambda e, sap=sap, dap=dap: e.tensor_copy(dap, sap),
                    reads=[pbk], writes=[dst_key, tkey], tag=self.u(f"{prefix}_ev"))
            else:
                gcol, bcol, akeys = affine
                for j in range(8):
                    p.v(lambda e, pbf=pbf, j=j, tl=tl: e.tensor_scalar(
                        dst.sl(j, dst_c0 + tl * 128, 128), pbf[:, j * 128:(j + 1) * 128],
                        gcol[:, j:j + 1], bcol[:, j:j + 1], op0=OP.mult, op1=OP.add),
                        reads=[pbk] + akeys, writes=[dst_key, tkey],
                        tag=self.u(f"{prefix}_eva"))
            if tok_f32 is not None or hook is not None:
                tf = self.F32P[:, 1024:2048]
                tfk = f"{prefix}_tf"
                if affine is None:
                    p.s(lambda e, xt=xt, st=st, tf=tf: e.activation(
                        tf, xt, AF.Identity, scale=st[:, 4:5], bias=st[:, 5:6]),
                        reads=[xk, sk], writes=[tfk], tag=self.u(f"{prefix}_tok"))
                else:
                    raise NotImplementedError
                if tok_f32 is not None:
                    tdram, tkey = tok_f32
                    p.dma("scalar", lambda e, t=t, tf=tf: e.dma_start(
                        out=tdram[t * 128:(t + 1) * 128, :], in_=tf),
                        reads=[tfk], writes=[tkey], tag=self.u(f"{prefix}_sttok"))
                if hook is not None:
                    hook(t, tf, tfk, xk)

    def bias_row(self, wname, lo, n, key="ROWS", tag=None):
        vec = self.dram[wname]
        self.p.dma("sync", lambda e: e.dma_start(
            out=self.ROWS[0:1, 0:n], in_=vec[lo:lo + n].rearrange("(one n) -> one n", one=1)),
            writes=[key], tag=tag or self.u(f"brow_{wname}"))
        return self.ROWS[0:1, 0:n]

    @staticmethod
    def evens(ap2d):
        r = ap2d.rearrange("p (c two) -> p c two", two=2)
        return r[:, :, 0:1], r[:, :, 1:2]

    @staticmethod
    def as3(ap2d):
        return ap2d.rearrange("p (c one) -> p c one", one=1)


def ph_l0_attn(mk, xin, xin_key, xout, xout_key):
    p, T, PT = mk.p, mk.T, mk.PT
    u = mk.u
    SINV = 1.0 / math.sqrt(16.0 + 512.0 + 512.0)
    lnT = FMView(mk.A, 0, 8, T)
    ctqT = FMView(mk.B, 0, 4, T)
    ctkvT = FMView(mk.B, 4 * T, 4, T)
    ktr_dram = mk.dram["l0_ktr"]
    attnT_dram = mk.dram["l0_attnT"]
    rc_dram, rs_dram = mk.dram["rope_cos"], mk.dram["rope_sin"]

    mk._bcol_cursor = 0
    b_dq, k_dq = mk.bias_col("L0_attn_Wd_q_b")
    b_dkv, k_dkv = mk.bias_col("L0_attn_Wd_kv_b")
    b_uq, k_uq = mk.bias_col("L0_attn_Wu_q_b")
    b_uk, k_uk = mk.bias_col("L0_attn_Wu_k_b")

    mk.ln_transpose(xin_key, xin, lnT, "l0.lnT", prefix="l0ln")
    p.barrier()

    wq, k_wq = mk.load_weight_fm("L0_attn_Wd_q_w", dst=mk.WT, dstoff=0)
    wkv, k_wkv = mk.load_weight_fm("L0_attn_Wd_kv_w", dst=mk.WT, dstoff=4096)
    for which, wv, wk, bc, bk, dstv, dstk in (
        ("q", wq, k_wq, b_dq, k_dq, ctqT, "l0.ctqT"),
        ("kv", wkv, k_wkv, b_dkv, k_dkv, ctkvT, "l0.ctkvT"),
    ):
        for m in range(4):
            for tb in range(0, T, 512):
                ps, pk = mk.bank(m % 2)
                for kp in range(8):
                    p.t(lambda e, ps=ps, wv=wv, m=m, kp=kp, tb=tb: e.matmul(
                        ps, wv.sl(kp, m * 128, 128), lnT.sl(kp, tb, 512),
                        start=(kp == 0), stop=(kp == 7)),
                        reads=[wk] + [f"l0.lnT.t{tb // 128 + i}" for i in range(4)], writes=[pk], tag=u(f"l0ct{which}mm"))
                p.s(lambda e, ps=ps, dstv=dstv, m=m, tb=tb, bc=bc: e.activation(
                    dstv.sl(m, tb, 512), ps, AF.Identity, bias=bc[:, m:m + 1]),
                    reads=[pk, bk], writes=[dstk], tag=u(f"l0ct{which}ev"))
    p.barrier()

    wkr, k_wkr = mk.load_weight_fm("L0_attn_Wk_r_w", dst=mk.WT, dstoff=0, key="l0.wkr")
    for t in range(PT):
        stg = mk.TOKB[:, 0:1024]
        for half in range(2):
            ps, pk = mk.bank(half)
            mk.bias_row("L0_attn_Wk_r_b", half * 512, 512, key="ROWS", tag=u("l0bkr"))
            p.t(lambda e, ps=ps: e.matmul(
                ps, mk.ONESF[:, 0:128], mk.ROWS[0:1, 0:512],
                start=True, stop=False),
                reads=["ROWS", "ONESF"], writes=[pk], tag=u("l0krb"))
            for kp in range(8):
                p.t(lambda e, ps=ps, kp=kp, t=t, half=half: e.matmul(
                    ps, lnT.sl(kp, t * 128, 128), wkr.sl(kp, half * 512, 512),
                    start=False, stop=(kp == 7)),
                    reads=["l0.wkr", f"l0.lnT.t{t}"], writes=[pk], tag=u("l0krmm"))
            cs = mk.TRIG[:, half * 512:half * 512 + 256]
            sn = mk.TRIG[:, 1024 + half * 512:1024 + half * 512 + 256]
            ck, sk2 = f"l0.cs{half}", f"l0.sn{half}"
            p.dma("gpsimd", lambda e, t=t, half=half, cs=cs: e.dma_start(
                out=cs, in_=rc_dram[t * 128:(t + 1) * 128, half * 256:half * 256 + 256]),
                writes=[ck], tag=u("l0cs"))
            p.dma("gpsimd", lambda e, t=t, half=half, sn=sn: e.dma_start(
                out=sn, in_=rs_dram[t * 128:(t + 1) * 128, half * 256:half * 256 + 256]),
                writes=[sk2], tag=u("l0sn"))
            x0, x1 = MK.evens(ps)
            t0 = MK.as3(mk.F32P[:, half * 512:half * 512 + 256])
            t1 = MK.as3(mk.F32P[:, half * 512 + 256:half * 512 + 512])
            sh = stg[:, half * 512:(half + 1) * 512]
            s_even, s_odd = MK.evens(sh)
            csap = MK.as3(cs)
            snap = MK.as3(sn)
            tk0, tk1, sgk = f"l0.t0{half}", f"l0.t1{half}", f"l0.stg{half}"
            p.v(lambda e, x0=x0, csap=csap, t0=t0: e.tensor_tensor(t0, x0, csap, op=OP.mult),
                reads=[pk, ck], writes=[tk0], tag=u("l0r0"))
            p.v(lambda e, x1=x1, snap=snap, t1=t1: e.tensor_tensor(t1, x1, snap, op=OP.mult),
                reads=[pk, sk2], writes=[tk1], tag=u("l0r1"))
            p.v(lambda e, t0=t0, t1=t1, s_even=s_even: e.tensor_tensor(
                s_even, t0, t1, op=OP.subtract),
                reads=[tk0, tk1], writes=[sgk], tag=u("l0r2"))
            p.v(lambda e, x1=x1, csap=csap, t0=t0: e.tensor_tensor(t0, x1, csap, op=OP.mult),
                reads=[pk, ck], writes=[tk0], tag=u("l0r3"))
            p.v(lambda e, x0=x0, snap=snap, t1=t1: e.tensor_tensor(t1, x0, snap, op=OP.mult),
                reads=[pk, sk2], writes=[tk1], tag=u("l0r4"))
            p.v(lambda e, t0=t0, t1=t1, s_odd=s_odd: e.tensor_tensor(s_odd, t0, t1, op=OP.add),
                reads=[tk0, tk1], writes=[sgk], tag=u("l0r5"))
        pbf = mk.PBF[t % 2]
        pbk = f"psBF{t % 2}"
        for j in range(8):
            p.t(lambda e, stg=stg, pbf=pbf, j=j: e.transpose(
                pbf[:, j * 128:(j + 1) * 128], stg[:, j * 128:(j + 1) * 128],
                mk.IDB[:, :]),
                reads=["l0.stg0", "l0.stg1", "IDB"], writes=[pbk], tag=u("l0ktr_tr"))
        st2 = mk.TOKB[:, 1024:2048]
        p.v(lambda e, pbf=pbf, st2=st2: e.tensor_copy(st2, pbf[:, :]),
            reads=[pbk], writes=["l0.st2"], tag=u("l0ktr_ev"))
        dap = ktr_dram[:, t * 128:(t + 1) * 128].rearrange("(j r) c -> r j c", r=128)
        sap2 = st2.rearrange("p (j c) -> p j c", j=8)
        p.dma("scalar", lambda e, sap2=sap2, dap=dap: e.dma_start(out=dap, in_=sap2),
              reads=["l0.st2"], writes=["l0.ktr_dram"], tag=u("l0ktr_st"))
    p.barrier()

    M = mk.M
    qT = M[:, 0:T]
    kT = M[:, T:2 * T]
    VP = M[:, 2 * T:2 * T + PT * 130]
    QR = M[:, 2 * T + PT * 130:3 * T + PT * 130]
    p.v(lambda e: e.memset(VP[:, :], 1.0), writes=["l0.VP"], tag=u("l0vones"))
    import os as _os
    _NP2 = int(_os.environ.get('L0PAIRS', '8'))
    _order = list(range(_NP2)) if not _os.environ.get('L0REV') else list(range(_NP2))[::-1]
    for h2 in _order:
        p.barrier()
        wuq, k_wuq = mk.load_weight_fm("L0_attn_Wu_q_w", dst=mk.WT, dstoff=0,
                                       colslice=(h2 * 128, 128), key="l0.wuq")
        wuk, k_wuk = mk.load_weight_fm("L0_attn_Wu_k_w", dst=mk.WT, dstoff=512,
                                       colslice=(h2 * 128, 128), key="l0.wuk")
        wuv, k_wuv = mk.load_weight_fm("L0_attn_Wu_v_w", dst=mk.WT, dstoff=1024,
                                       colslice=(h2 * 128, 128), key="l0.wuv")
        wqr, k_wqr = mk.load_weight_fm("L0_attn_Wq_r_w", dst=mk.WT, dstoff=1536,
                                       colslice=(h2 * 128, 128), key="l0.wqr")
        cosP = mk.TRIG[:, 0:2048]
        sinP = mk.TRIG[:, 2048:4096]
        csrc = rc_dram[:, h2 * 64:(h2 + 1) * 64].rearrange("(t p) c -> p t c", p=128)
        ssrc = rs_dram[:, h2 * 64:(h2 + 1) * 64].rearrange("(t p) c -> p t c", p=128)
        ntr = T // 128
        cosP3 = cosP[:, 0:ntr * 64].rearrange("p (t c) -> p t c", t=ntr)
        sinP3 = sinP[:, 0:ntr * 64].rearrange("p (t c) -> p t c", t=ntr)
        p.dma("gpsimd", lambda e, csrc=csrc, cosP3=cosP3: e.dma_start(out=cosP3, in_=csrc),
              writes=["l0.cosP"], tag=u("l0cosP"))
        p.dma("gpsimd", lambda e, ssrc=ssrc, sinP3=sinP3: e.dma_start(out=sinP3, in_=ssrc),
              writes=["l0.sinP"], tag=u("l0sinP"))
        for tb in range(0, T, 512):
            ps, pkb = mk.bank(0)
            for kp in range(4):
                p.t(lambda e, ps=ps, kp=kp, tb=tb, wuq=wuq: e.matmul(
                    ps, wuq.sl(kp, 0, 128), ctqT.sl(kp, tb, 512),
                    start=(kp == 0), stop=(kp == 3)),
                    reads=[k_wuq, "l0.ctqT"], writes=[pkb], tag=u("l0qc"))
            p.s(lambda e, ps=ps, tb=tb, h2=h2: e.activation(
                qT[:, tb:tb + 512], ps, AF.Identity, bias=b_uq[:, h2:h2 + 1]),
                reads=[pkb, k_uq], writes=["l0.qT"], tag=u("l0qcev"))
        mk.bias_row("L0_attn_Wq_r_b", h2 * 128, 128, key="ROWS", tag=u("l0bqr"))
        for t in range(PT):
            ps, pkb = mk.bank(2 + (t % 2))
            p.t(lambda e, ps=ps: e.matmul(
                ps[:, 0:128], mk.ONESF[:, 0:128], mk.ROWS[0:1, 0:128],
                start=True, stop=False),
                reads=["ROWS", "ONESF"], writes=[pkb], tag=u("l0qrb"))
            for kp in range(4):
                p.t(lambda e, ps=ps, kp=kp, t=t, wqr=wqr: e.matmul(
                    ps[:, 0:128], ctqT.sl(kp, t * 128, 128), wqr.sl(kp, 0, 128),
                    start=False, stop=(kp == 3)),
                    reads=[k_wqr, "l0.ctqT"], writes=[pkb], tag=u("l0qrmm"))
            p.v(lambda e, ps=ps, t=t: e.tensor_copy(QR[:, t * 128:(t + 1) * 128], ps[:, 0:128]),
                reads=[pkb], writes=["l0.QR"], tag=u("l0qrev"))
        GSEG = min(2048, T)
        for g in range(T // GSEG):
            seg = QR[:, g * GSEG:(g + 1) * GSEG]
            x0, x1 = MK.evens(seg)
            cseg = MK.as3(cosP[:, g * GSEG // 2:(g + 1) * GSEG // 2])
            sseg = MK.as3(sinP[:, g * GSEG // 2:(g + 1) * GSEG // 2])
            t0 = MK.as3(mk.F32P[:, 0:GSEG // 2])
            t1 = MK.as3(mk.F32P[:, 1024:1024 + GSEG // 2])
            stg = mk.TOKB[:, 0:GSEG]
            s_even, s_odd = MK.evens(stg)
            p.v(lambda e, x0=x0, cseg=cseg, t0=t0: e.tensor_tensor(t0, x0, cseg, op=OP.mult),
                reads=["l0.QR", "l0.cosP"], writes=["l0.t0"], tag=u("l0R0"))
            p.v(lambda e, x1=x1, sseg=sseg, t1=t1: e.tensor_tensor(t1, x1, sseg, op=OP.mult),
                reads=["l0.QR", "l0.sinP"], writes=["l0.t1"], tag=u("l0R1"))
            p.v(lambda e, t0=t0, t1=t1, s_even=s_even: e.tensor_tensor(
                s_even, t0, t1, op=OP.subtract),
                reads=["l0.t0", "l0.t1"], writes=["l0.rstg"], tag=u("l0R2"))
            p.v(lambda e, x1=x1, cseg=cseg, t0=t0: e.tensor_tensor(t0, x1, cseg, op=OP.mult),
                reads=["l0.QR", "l0.cosP"], writes=["l0.t0"], tag=u("l0R3"))
            p.v(lambda e, x0=x0, sseg=sseg, t1=t1: e.tensor_tensor(t1, x0, sseg, op=OP.mult),
                reads=["l0.QR", "l0.sinP"], writes=["l0.t1"], tag=u("l0R4"))
            p.v(lambda e, t0=t0, t1=t1, s_odd=s_odd: e.tensor_tensor(s_odd, t0, t1, op=OP.add),
                reads=["l0.t0", "l0.t1"], writes=["l0.rstg"], tag=u("l0R5"))
            for tt in range(GSEG // 128):
                t = g * (GSEG // 128) + tt
                pbf = mk.PBF[tt % 2]
                pbk = f"psBF{tt % 2}"
                p.t(lambda e, stg=stg, tt=tt, pbf=pbf: e.transpose(
                    pbf[:, 0:128], stg[:, tt * 128:(tt + 1) * 128], mk.IDB[:, :]),
                    reads=["l0.rstg", "IDB"], writes=[pbk], tag=u("l0Rtr"))
                p.v(lambda e, pbf=pbf, t=t: e.tensor_tensor(
                    qT[:, t * 128:(t + 1) * 128], qT[:, t * 128:(t + 1) * 128],
                    pbf[:, 0:128], op=OP.add),
                    reads=[pbk, "l0.qT"], writes=["l0.qT"], tag=u("l0Radd"))
        ktr_sb = QR
        p.rekey("l0.ktrsb", "l0.QR")
        p.dma("sync", lambda e, h2=h2: e.dma_start(
            out=ktr_sb[:, :], in_=ktr_dram[h2 * 128:(h2 + 1) * 128, :]),
            reads=["l0.ktr_dram"], writes=["l0.ktrsb"], tag=u("l0ktrld"))
        for tb in range(0, T, 512):
            ps, pkb = mk.bank(1)
            for kp in range(4):
                p.t(lambda e, ps=ps, kp=kp, tb=tb, wuk=wuk: e.matmul(
                    ps, wuk.sl(kp, 0, 128), ctkvT.sl(kp, tb, 512),
                    start=(kp == 0), stop=(kp == 3)),
                    reads=[k_wuk, "l0.ctkvT"], writes=[pkb], tag=u("l0kc"))
            p.s(lambda e, ps=ps, tb=tb, h2=h2: e.activation(
                kT[:, tb:tb + 512], ps, AF.Identity, bias=b_uk[:, h2:h2 + 1]),
                reads=[pkb, k_uk], writes=["l0.kT"], tag=u("l0kcev"))
            p.v(lambda e, tb=tb: e.tensor_tensor(
                kT[:, tb:tb + 512], kT[:, tb:tb + 512], ktr_sb[:, tb:tb + 512], op=OP.add),
                reads=["l0.kT", "l0.ktrsb"], writes=["l0.kT"], tag=u("l0kadd"))
        p.dma("sync", lambda e, h2=h2: e.dma_start(
            out=mk.ROWS[0:1, 512:640],
            in_=mk.dram["L0_attn_Wu_v_b"][h2 * 128:(h2 + 1) * 128].rearrange("(one n) -> one n", one=1)),
            writes=["ROWS2"], tag=u("l0buv"))
        for t in range(PT):
            ps, pkb = mk.bank(4)
            p.t(lambda e, ps=ps: e.matmul(
                ps[:, 0:128], mk.ONESF[:, 0:128], mk.ROWS[0:1, 512:640],
                start=True, stop=False),
                reads=["ROWS2", "ONESF"], writes=[pkb], tag=u("l0vb"))
            for kp in range(4):
                p.t(lambda e, ps=ps, kp=kp, t=t, wuv=wuv: e.matmul(
                    ps[:, 0:128], ctkvT.sl(kp, t * 128, 128), wuv.sl(kp, 0, 128),
                    start=False, stop=(kp == 3)),
                    reads=[k_wuv, "l0.ctkvT"], writes=[pkb], tag=u("l0vmm"))
            dap = VP[:, t * 130:(t + 1) * 130].rearrange("p (two c) -> p two c", two=2)[:, :, 0:64]
            sap = ps[:, 0:128].rearrange("p (two c) -> p two c", two=2)
            p.v(lambda e, dap=dap, sap=sap: e.tensor_copy(dap, sap),
                reads=[pkb], writes=["l0.VP"], tag=u("l0vev"))
        if getattr(mk, "DEBUG_L0", False):
            for nm, buf, rk in (("dbg_qT", qT, "l0.qT"), ("dbg_kT", kT, "l0.kT"),
                                ("dbg_VP", VP, "l0.VP")):
                d = mk.dram[nm]
                p.dma("scalar", lambda e, d=d, buf=buf, h2=h2: e.dma_start(
                    out=d[h2 * 128:(h2 + 1) * 128, 0:buf.shape[1]], in_=buf),
                    reads=[rk], writes=[nm], tag=u("dbg"))
        pend = []
        pend = []
        for qb in range(0, T, 512):
            def emit_S(kc, h, qb=qb):
                par = kc % 2
                ps_s, pk_s = mk.bank(h * 2 + par)
                p.t(lambda e, ps_s=ps_s, h=h, kc=kc, qb=qb: e.matmul(
                    ps_s, kT[h * 64:(h + 1) * 64, kc * 128:(kc + 1) * 128],
                    qT[h * 64:(h + 1) * 64, qb:qb + 512], start=True, stop=True),
                    reads=["l0.kT", "l0.qT"], writes=[pk_s], tag=u("l0S"))
            for h in range(2):
                emit_S(0, h)
            for kc in range(PT):
                if kc in (2, 4) and pend:
                    pend.pop(0)()
                if kc + 1 < PT:
                    for h in range(2):
                        emit_S(kc + 1, h)
                for h in range(2):
                    par = kc % 2
                    ps_s, pk_s = mk.bank(h * 2 + par)
                    if par == 0:
                        eb = mk.E[:, h * 512:(h + 1) * 512]
                    else:
                        eb = mk.TOKB[:, h * 512:(h + 1) * 512]
                    ekey = f"l0.E{h}_{par}"
                    p.s(lambda e, ps_s=ps_s, eb=eb: e.activation(
                        eb, ps_s, AF.Exp, scale=SINV),
                        reads=[pk_s], writes=[ekey], tag=u("l0exp"))
                    ps_av, pk_av = mk.bank(4 + h)
                    p.t(lambda e, ps_av=ps_av, h=h, kc=kc, eb=eb: e.matmul(
                        ps_av[0:65, :], VP[:, kc * 130 + h * 65:kc * 130 + (h + 1) * 65],
                        eb, start=(kc == 0), stop=(kc == PT - 1)),
                        reads=["l0.VP", ekey], writes=[pk_av], tag=u("l0AV"))
            for h in range(2):
                ps_av, pk_av = mk.bank(4 + h)
                avs = mk.F32P[0:65, h * 512:(h + 1) * 512]
                p.v(lambda e, ps_av=ps_av, avs=avs: e.tensor_copy(
                    avs, ps_av[0:65, 0:512]),
                    reads=[pk_av], writes=[f"l0.avs{h}"], tag=u("l0avse"))

            def mk_out(h, qb=qb, h2=h2):
                avs = mk.F32P[0:65, h * 512:(h + 1) * 512]
                rrow = mk.F32P[0:1, 1024 + h * 512:1024 + (h + 1) * 512]
                rb = mk.F32P[0:64, 2048:2560]
                stg = mk.TOKB[0:64, 1024 + h * 512:1024 + (h + 1) * 512]
                ps_bc, pk_bc = mk.bank(2 * h + 1)

                def emit():
                    p.v(lambda e, avs=avs, rrow=rrow: e.reciprocal(
                        rrow, avs[64:65, :]),
                        reads=[f"l0.avs{h}"], writes=[f"l0.rrow{h}"],
                        tag=u("l0rec"))
                    p.t(lambda e, ps_bc=ps_bc, rrow=rrow: e.matmul(
                        ps_bc[0:64, :], mk.ONESF[0:1, 0:64], rrow,
                        start=True, stop=True),
                        reads=[f"l0.rrow{h}", "ONESF"], writes=[pk_bc],
                        tag=u("l0bc"))
                    p.v(lambda e, ps_bc=ps_bc, rb=rb: e.tensor_copy(rb, ps_bc[0:64, :]),
                        reads=[pk_bc], writes=["l0.rb"], tag=u("l0rbev"))
                    p.v(lambda e, avs=avs, rb=rb, stg=stg: e.tensor_tensor(
                        stg, avs[0:64, :], rb, op=OP.mult),
                        reads=[f"l0.avs{h}", "l0.rb"], writes=[f"l0.ostg{h}"],
                        tag=u("l0onorm"))
                    drow = (2 * h2 + h) * 64
                    p.dma("scalar", lambda e, stg=stg, drow=drow, qb=qb: e.dma_start(
                        out=attnT_dram[drow:drow + 64, qb:qb + 512], in_=stg),
                        reads=[f"l0.ostg{h}"], writes=["l0.attnT"], tag=u("l0ost"))
                return emit
            pend.append(mk_out(0))
            pend.append(mk_out(1))
        while pend:
            pend.pop(0)()
    p.barrier()

    wo, k_wo = mk.load_weight_fm("L0_attn_Wo_w", dst=mk.WT, dstoff=0, key="l0.wo")
    for half in range(2):
        mk.bias_row("L0_attn_Wo_b", half * 512, 512, key="ROWS", tag=u("l0bo"))
        for t in range(PT):
            sl = t % 2
            atile = mk.M[:, sl * 1024:(sl + 1) * 1024]
            ak = f"l0.at{sl}"
            sap = attnT_dram[:, t * 128:(t + 1) * 128].rearrange("(j r) c -> r j c", r=128)
            at3 = atile.rearrange("p (j c) -> p j c", j=8)
            p.dma("sync", lambda e, at3=at3, sap=sap: e.dma_start(out=at3, in_=sap),
                  reads=["l0.attnT"], writes=[ak], tag=u("l0wold"))
            xt = mk.F32P[:, sl * 1024:sl * 1024 + 512]
            xk = f"l0.wx{sl}"
            p.dma("sync", lambda e, xt=xt, t=t, half=half: e.dma_start(
                out=xt, in_=xin[t * 128:(t + 1) * 128, half * 512:(half + 1) * 512]),
                reads=[xin_key], writes=[xk], tag=u("l0wox"))
            ps, pkb = mk.bank(2 * sl)
            p.t(lambda e, ps=ps: e.matmul(
                ps, mk.ONESF[:, 0:128], mk.ROWS[0:1, 0:512],
                start=True, stop=False),
                reads=["ROWS", "ONESF"], writes=[pkb], tag=u("l0wob"))
            for kp in range(8):
                p.t(lambda e, ps=ps, kp=kp, atile=atile, half=half, wo=wo: e.matmul(
                    ps, atile[:, kp * 128:(kp + 1) * 128], wo.sl(kp, half * 512, 512),
                    start=False, stop=(kp == 7)),
                    reads=[ak, "l0.wo"], writes=[pkb], tag=u("l0womm"))
            ot = mk.F32P[:, 2048:2560]
            ok = "l0.wo_ot"
            p.v(lambda e, ps=ps, xt=xt, ot=ot: e.tensor_tensor(
                ot, ps, xt, op=OP.add),
                reads=[pkb, xk], writes=[ok], tag=u("l0woadd"))
            p.dma("scalar", lambda e, ot=ot, t=t, half=half: e.dma_start(
                out=xout[t * 128:(t + 1) * 128, half * 512:(half + 1) * 512], in_=ot),
                reads=[ok], writes=[xout_key], tag=u("l0wost"))
        p.barrier()

WEIGHT_SPECS = [
    ("L0_attn_Wd_q_w", (1024, 512)), ("L0_attn_Wd_q_b", (512,)),
    ("L0_attn_Wu_q_w", (512, 1024)), ("L0_attn_Wu_q_b", (1024,)),
    ("L0_attn_Wq_r_w", (512, 1024)), ("L0_attn_Wq_r_b", (1024,)),
    ("L0_attn_Wk_r_w", (1024, 1024)), ("L0_attn_Wk_r_b", (1024,)),
    ("L0_attn_Wd_kv_w", (1024, 512)), ("L0_attn_Wd_kv_b", (512,)),
    ("L0_attn_Wu_k_w", (512, 1024)), ("L0_attn_Wu_k_b", (1024,)),
    ("L0_attn_Wu_v_w", (512, 1024)), ("L0_attn_Wu_v_b", (1024,)),
    ("L0_attn_Wo_w", (1024, 1024)), ("L0_attn_Wo_b", (1024,)),
    ("L0_ffn_w1", (4096, 1024)), ("L0_ffn_b1", (4096,)),
    ("L0_ffn_w2", (1024, 4096)), ("L0_ffn_b2", (1024,)),
    ("L1_attn_Wq", (1024, 1024)), ("L1_attn_bq", (1024,)),
    ("L1_attn_Wk", (1024, 1024)), ("L1_attn_bk", (1024,)),
    ("L1_attn_Wv", (1024, 1024)), ("L1_attn_bv", (1024,)),
    ("L1_attn_Wo", (1024, 1024)), ("L1_attn_bo", (1024,)),
    ("L1_attn_W1", (1024, 4096)), ("L1_attn_b1", (4096,)),
    ("L1_attn_W2", (4096, 1024)), ("L1_attn_b2", (1024,)),
    ("L1_attn_g1", (1024,)), ("L1_attn_be1", (1024,)),
    ("L1_attn_g2", (1024,)), ("L1_attn_be2", (1024,)),
    ("L1_ffn_router_w", (8, 1024)), ("L1_ffn_router_bias", (8,)),
    ("L1_ffn_sw1", (1024, 2048)), ("L1_ffn_sw3", (1024, 2048)),
    ("L1_ffn_sw2", (2048, 1024)),
    ("L1_ffn_ew1", (8, 1024, 1024)), ("L1_ffn_ew2", (8, 1024, 1024)),
    ("L2_attn_Wq", (1024, 1024)), ("L2_attn_bq", (1024,)),
    ("L2_attn_Wk", (1024, 1024)), ("L2_attn_bk", (1024,)),
    ("L2_attn_Wv", (1024, 1024)), ("L2_attn_bv", (1024,)),
    ("L2_attn_Wo", (1024, 1024)), ("L2_attn_bo", (1024,)),
    ("L2_ffn_gamma", (1024,)), ("L2_ffn_beta", (1024,)),
    ("L2_ffn_Wr", (1024, 8)), ("L2_ffn_br", (8,)),
    ("L2_ffn_W1", (8, 1024, 4096)), ("L2_ffn_b1", (8, 4096)),
    ("L2_ffn_W2", (8, 4096, 1024)), ("L2_ffn_b2", (8, 1024)),
    ("L3_attn_Wk", (1024, 1024)), ("L3_attn_bk", (1024,)),
    ("L3_attn_Wv", (1024, 1024)), ("L3_attn_bv", (1024,)),
    ("L3_attn_Wq", (1024, 1024)), ("L3_attn_bq", (1024,)),
    ("L3_attn_Wkp", (1024, 1024)), ("L3_attn_bkp", (1024,)),
    ("L3_attn_Wqp", (1024, 1024)), ("L3_attn_bqp", (1024,)),
    ("L3_attn_phase_scale", (1024,)), ("L3_attn_ln_g", (1024,)),
    ("L3_attn_ln_b", (1024,)),
    ("L3_attn_Wo", (1024, 1024)), ("L3_attn_bo", (1024,)),
    ("L3_ffn_Wg", (8, 1024)),
    ("L3_ffn_W1", (8, 2048, 1024)), ("L3_ffn_W2", (8, 2048, 1024)),
    ("L3_ffn_W3", (8, 1024, 2048)),
]

ALL_PHASES = ("l0a", "l0f", "l1a", "l1f", "l2a", "l2f", "l3a", "l3f")


def build_program(T, phases=ALL_PHASES, debug=False):
    import contextlib
    nc = bass.Bass("TRN2", target_bir_lowering=False)
    p = Prog()
    mk = MK(nc, p, T, None)
    mk.din("x", [T, D])
    for name, shape in WEIGHT_SPECS:
        mk.din(name, shape)
    consts = mk.din("consts", [256, 128])
    mk.din("rope_cos", [T, 512])
    mk.din("rope_sin", [T, 512])
    out = mk.dout("out", [T, D])
    mk.dscratch("l0_ktr", [D, T], BF16)
    mk.dscratch("l0_bw1", [4096, 1024], BF16)
    mk.dscratch("l0_bw2", [1024, 4096], BF16)
    mk.dscratch("l1_src", [T, D], F32)
    mk.dscratch("l1_x1", [T, D], F32)
    mk.dscratch("l1_acc", [T, D], F32)
    mk.dscratch("l1f_acc", [T, D], F32)
    mk.dscratch("l1f_gsh", [2048, T], BF16)
    mk.dscratch("l2_xblk", [T, D], F32)
    mk.dscratch("l2_c", [8, 1024], BF16)
    mk.dscratch("l3_wt", [6, 1024, 1024], BF16)
    mk.dscratch("l3_bo", [1, 1024], F32)
    mk.dscratch("l3f_w1", [8, 2048, 1024], BF16)
    mk.dscratch("l3f_w2", [8, 2048, 1024], BF16)
    mk.dscratch("l3f_w3", [8, 1024, 2048], BF16)
    mk.dscratch("l1f_ew1b", [8, 1024, 1024], BF16)
    mk.dscratch("l1f_ew2b", [8, 1024, 1024], BF16)
    mk.dscratch("l2f_w1b", [8, 1024, 4096], BF16)
    mk.dscratch("l2f_w2b", [8, 4096, 1024], BF16)
    if debug:
        mk.DEBUG_L0 = True
        mk.DEBUG_L2F = True
        mk.dscratch("dbg_w", [128, 256], F32)
        mk.dscratch("dbg_s", [128, 256], F32)
        mk.dscratch("dbg_lg", [128, 256], F32)
        mk.dscratch("dbg_rb", [128, 8], F32)
        mk.dscratch("dbg_bc", [128, 16], F32)
        mk.dscratch("dbg_brow", [1, 8], F32)
        mk.dscratch("dbg_qT", [1024, T], BF16)
        mk.dscratch("dbg_kT", [1024, T], BF16)
        mk.dscratch("dbg_VP", [1024, 33 * 130], BF16)
    mk.dscratch("l0_attnT", [D, T], BF16)
    xr = [mk.dscratch(f"xr{i}", [T, D]) for i in range(7)]
    with contextlib.ExitStack() as stack:
        mk.alloc_pools(stack)
        mk.load_consts(consts)
        cur, cur_key = mk.dram["x"], "x"
        chain = []
        fns = {"l0a": ph_l0_attn, "l0f": ph_l0_ffn, "l1a": ph_l1_attn, "l1f": ph_l1_ffn, "l2a": ph_l2_attn, "l2f": ph_l2_ffn, "l3a": ph_l3_attn, "l3f": ph_l3_ffn}
        for i, ph in enumerate(phases):
            last = (i == len(phases) - 1)
            dst = out if last else xr[i]
            dst_key = "out" if last else f"xr{i}"
            fns[ph](mk, cur, cur_key, dst, dst_key)
            cur, cur_key = dst, dst_key
        p.seal()
        p.emit(nc)
    return nc, mk, p


def make_consts(T):
    c = np.zeros((256, 128), np.float32)
    c[0:128, :] = np.eye(128, dtype=np.float32)
    lt = np.triu(np.ones((64, 64), np.float32))
    c[128:192, 0:64] = lt
    c[192:256, 64:128] = lt
    return c


def make_rope(T):
    ids = np.arange(256, dtype=np.float32)
    theta = np.float32(10000.0) ** (np.float32(-2.0) * ids[None, :512 // 2][0]) \
        if False else None
    i = np.arange(512, dtype=np.float32)
    theta = (np.float32(10000.0) ** (np.float32(-2.0) * i)) / np.float32(512.0)
    pos = np.arange(T, dtype=np.float32)[:, None]
    r = pos * theta[None, :]
    return np.cos(r).astype(np.float32), np.sin(r).astype(np.float32)


def kernel(**inputs):
    T = 4096
    x = np.ascontiguousarray(inputs["L0_attn_x"], dtype=np.float32)
    B = x.shape[0]
    nc, mk, p = build_program(T)
    cos, sin = make_rope(T)
    consts = make_consts(T)
    base = {}
    for name, shape in WEIGHT_SPECS:
        base[name] = np.ascontiguousarray(inputs[name], dtype=np.float32)
    base["consts"] = consts
    base["rope_cos"] = cos
    base["rope_sin"] = sin
    in_maps = [dict(base, x=np.ascontiguousarray(x[i])) for i in range(B)]
    res = run_bass_kernel_spmd(nc, in_maps, core_ids=list(range(B)))
    return np.stack([r["out"] for r in res.results], axis=0)


def ph_l0_ffn(mk, xin, xin_key, xout, xout_key):
    p, T, PT = mk.p, mk.T, mk.PT
    u = mk.u
    bw1_dram = mk.dram["l0_bw1"]
    bw2_dram = mk.dram["l0_bw2"]
    w1, w2 = mk.dram["L0_ffn_w1"], mk.dram["L0_ffn_w2"]
    mk._bcol_cursor = 0
    b1c, k_b1 = mk.bias_col("L0_ffn_b1")
    for rt in range(32):
        sl = rt % 2
        wt = mk.F32P[:, sl * 1024:(sl + 1) * 1024]
        sg = mk.TOKB[:, sl * 1024:(sl + 1) * 1024]
        st = mk.STAT[:, sl:sl + 1]
        kw, ks, kst, kb = f"f.wt{sl}", f"f.sg{sl}", f"f.st{sl}", f"f.bwt{sl}"
        p.dma("sync", lambda e, rt=rt, wt=wt: e.dma_start(
            out=wt, in_=w1[rt * 128:(rt + 1) * 128, :]),
            writes=[kw], tag=u("f.w1ld"))
        p.v(lambda e, wt=wt, st=st: e.tensor_reduce(
            st, wt, axis=AX.X, op=OP.add, apply_absolute_value=True),
            reads=[kw], writes=[kst], tag=u("f.abs"))
        p.s(lambda e, wt=wt, sg=sg: e.activation(sg, wt, AF.Sign),
            reads=[kw], writes=[ks], tag=u("f.sign"))
        bwt = mk.M[:, sl * 1024:(sl + 1) * 1024]
        p.v(lambda e, sg=sg, st=st, bwt=bwt: e.tensor_scalar(
            bwt, sg, st, 1.0 / 1024.0, op0=OP.mult, op1=OP.mult),
            reads=[ks, kst], writes=[kb], tag=u("f.bw"))
        p.dma("scalar", lambda e, rt=rt, bwt=bwt: e.dma_start(
            out=bw1_dram[rt * 128:(rt + 1) * 128, :], in_=bwt),
            reads=[kb], writes=["l0.bw1d"], tag=u("f.w1st"))
    for rt in range(8):
        st4 = mk.STAT[:, 8:12]
        for c in range(4):
            sl = c % 2
            wt = mk.F32P[:, sl * 1024:(sl + 1) * 1024]
            kw = f"f.wt{sl}"
            p.dma("sync", lambda e, rt=rt, c=c, wt=wt: e.dma_start(
                out=wt, in_=w2[rt * 128:(rt + 1) * 128, c * 1024:(c + 1) * 1024]),
                writes=[kw], tag=u("f.w2ld"))
            p.v(lambda e, wt=wt, st4=st4, c=c: e.tensor_reduce(
                st4[:, c:c + 1], wt, axis=AX.X, op=OP.add, apply_absolute_value=True),
                reads=[kw], writes=["f.st4"], tag=u("f.abs2"))
        st = mk.STAT[:, 12:13]
        p.v(lambda e, st4=st4, st=st: e.tensor_reduce(
            st, st4, axis=AX.X, op=OP.add),
            reads=["f.st4"], writes=["f.st"], tag=u("f.asum"))
        for c in range(4):
            sl = c % 2
            wt = mk.F32P[:, sl * 1024:(sl + 1) * 1024]
            sg = mk.TOKB[:, sl * 1024:(sl + 1) * 1024]
            bwt = mk.M[:, sl * 1024:(sl + 1) * 1024]
            kw, ks, kb = f"f.wt{sl}", f"f.sg{sl}", f"f.bwt{sl}"
            p.dma("sync", lambda e, rt=rt, c=c, wt=wt: e.dma_start(
                out=wt, in_=w2[rt * 128:(rt + 1) * 128, c * 1024:(c + 1) * 1024]),
                writes=[kw], tag=u("f.w2ld2"))
            p.s(lambda e, wt=wt, sg=sg: e.activation(sg, wt, AF.Sign),
                reads=[kw], writes=[ks], tag=u("f.sign2"))
            p.v(lambda e, sg=sg, st=st, bwt=bwt: e.tensor_scalar(
                bwt, sg, st, 1.0 / 4096.0, op0=OP.mult, op1=OP.mult),
                reads=[ks, "f.st"], writes=[kb], tag=u("f.bw2"))
            p.dma("scalar", lambda e, rt=rt, c=c, bwt=bwt: e.dma_start(
                out=bw2_dram[rt * 128:(rt + 1) * 128, c * 1024:(c + 1) * 1024], in_=bwt),
                reads=[kb], writes=["l0.bw2d"], tag=u("f.w2st"))
    p.barrier()
    bw1T = FMView(mk.A, 0, 8, 4096)
    for ks in range(8):
        p.dma("sync", lambda e, ks=ks: e.dma_start(
            out=mk.A[:, ks * 4096:(ks + 1) * 4096],
            in_=bw1_dram[:, ks * 128:(ks + 1) * 128], transpose=True),
            reads=["l0.bw1d"], writes=["f.bw1T"], tag=u("f.tr1"))
    bw2T = FMView(mk.B, 0, 32, 1024)
    for hs in range(32):
        p.dma("sync", lambda e, hs=hs: e.dma_start(
            out=mk.B[:, hs * 1024:(hs + 1) * 1024],
            in_=bw2_dram[:, hs * 128:(hs + 1) * 128], transpose=True),
            reads=["l0.bw2d"], writes=["f.bw2T"], tag=u("f.tr2"))
    p.barrier()
    NB = T // 256
    for blk in range(NB):
        sl = blk % 2
        lnb = FMView(mk.M, sl * 2048, 8, 256)
        lnk = f"f.lnb{sl}"
        mk.ln_transpose(xin_key, xin, lnb, lnk, prefix="fln",
                        tiles=(blk * 2, blk * 2 + 2), xt_db_ok=False)
        gT = FMView(mk.M, 4096, 32, 256)
        for m in range(32):
            ps, pkb = mk.bank(m % 2)
            for kp in range(8):
                p.t(lambda e, ps=ps, m=m, kp=kp, lnb=lnb: e.matmul(
                    ps[:, 0:256], bw1T.sl(kp, m * 128, 128), lnb.sl(kp, 0, 256),
                    start=(kp == 0), stop=(kp == 7)),
                    reads=["f.bw1T", lnk], writes=[pkb], tag=u("f.hmm"))
            p.s(lambda e, ps=ps, m=m: e.activation(
                gT.sl(m, 0, 256), ps[:, 0:256], AF.Gelu, bias=b1c[:, m:m + 1]),
                reads=[pkb, k_b1], writes=["f.gT"], tag=u("f.gelu"))
        for tl in range(2):
            t = blk * 2 + tl
            xt = mk.F32P[:, 1024:2048]
            xk = "f.xt"
            p.dma("sync", lambda e, xt=xt, t=t: e.dma_start(
                out=xt, in_=xin[t * 128:(t + 1) * 128, :]),
                reads=[xin_key], writes=[xk], tag=u("f.xld"))
            for half in range(2):
                mk.bias_row("L0_ffn_b2", half * 512, 512, key="ROWS", tag=u("f.b2"))
                ps, pkb = mk.bank(2 + half)
                p.t(lambda e, ps=ps: e.matmul(
                    ps, mk.ONESF[:, 0:128], mk.ROWS[0:1, 0:512], start=True, stop=False),
                    reads=["ROWS", "ONESF"], writes=[pkb], tag=u("f.ob"))
                for hp in range(32):
                    p.t(lambda e, ps=ps, hp=hp, tl=tl, half=half: e.matmul(
                        ps, gT.sl(hp, tl * 128, 128), bw2T.sl(hp, half * 512, 512),
                        start=False, stop=(hp == 31)),
                        reads=["f.gT", "f.bw2T"], writes=[pkb], tag=u("f.omm"))
                ot = mk.F32P[:, 2048 + 0:2048 + 512]
                ok = "f.ot"
                p.v(lambda e, ps=ps, xt=xt, ot=ot, half=half: e.tensor_tensor(
                    ot, ps, xt[:, half * 512:(half + 1) * 512], op=OP.add),
                    reads=[pkb, xk], writes=[ok], tag=u("f.oadd"))
                p.dma("scalar", lambda e, ot=ot, t=t, half=half: e.dma_start(
                    out=xout[t * 128:(t + 1) * 128, half * 512:(half + 1) * 512], in_=ot),
                    reads=[ok], writes=[xout_key], tag=u("f.ost"))
    p.barrier()


def ph_l1_attn(mk, xin, xin_key, xout, xout_key):
    p, T, PT = mk.p, mk.T, mk.PT
    u = mk.u
    for e_ in range(8):
        for srcn, dstn in (("L1_ffn_ew1", "l1f_ew1b"), ("L1_ffn_ew2", "l1f_ew2b"),
                           ("L2_ffn_W1", "l2f_w1b"), ("L2_ffn_W2", "l2f_w2b")):
            p.dma("gpsimd", lambda e, srcn=srcn, dstn=dstn, e_=e_: e.dma_start(
                out=mk.dram[dstn][e_], in_=mk.dram[srcn][e_]),
                writes=[f"cast.{dstn}"], tag=u("precast"))
    lnT = FMView(mk.A, 0, 8, T)
    src_dram = mk.dram["l1_src"]
    x1_dram = mk.dram["l1_x1"]
    acc_dram = mk.dram["l1_acc"]
    mk._bcol_cursor = 0
    bqc, k_bq = mk.bias_col("L1_attn_bq")
    mk.ln_transpose(xin_key, xin, lnT, "l1.lnT", prefix="l1ln",
                    tok_f32=(src_dram, "l1.src"))
    p.barrier()
    M = mk.M
    qT = M[:, 0:T]
    KP_ = M[:, T:2 * T]
    VP = M[:, 2 * T:2 * T + PT * 130]
    oT = FMView(mk.B, 0, 8, T)
    p.v(lambda e: e.memset(VP[:, :], 1.0), writes=["l1.VP"], tag=u("l1vones"))
    EPS = 1e-6
    for h2 in range(8):
        wq, k_wq = mk.load_weight_fm("L1_attn_Wq", dst=mk.WT, dstoff=0,
                                     colslice=(h2 * 128, 128), key="l1.wq")
        wk, k_wk = mk.load_weight_fm("L1_attn_Wk", dst=mk.WT, dstoff=1024,
                                     colslice=(h2 * 128, 128), key="l1.wk")
        wv, k_wv = mk.load_weight_fm("L1_attn_Wv", dst=mk.WT, dstoff=2048,
                                     colslice=(h2 * 128, 128), key="l1.wv")
        p.dma("sync", lambda e, h2=h2: e.dma_start(
            out=mk.ROWS[0:1, 128:256],
            in_=mk.dram["L1_attn_bv"][h2 * 128:(h2 + 1) * 128].rearrange(
                "(one n) -> one n", one=1)),
            writes=["ROWS2"], tag=u("l1bv"))
        for tb in range(0, T, 512):
            ps, pkb = mk.bank(tb // 512 % 2)
            for kp in range(8):
                p.t(lambda e, ps=ps, kp=kp, tb=tb, wq=wq: e.matmul(
                    ps, wq.sl(kp, 0, 128), lnT.sl(kp, tb, 512),
                    start=(kp == 0), stop=(kp == 7)),
                    reads=[k_wq] + [f"l1.lnT.t{tb // 128 + i}" for i in range(4)], writes=[pkb], tag=u("l1q"))
            par = tb // 512 % 2
            tmin = mk.F32P[:, par * 1024:par * 1024 + 512]
            texp = mk.F32P[:, par * 1024 + 512:par * 1024 + 1024]
            km, ke = f"l1.tmin{par}", f"l1.texp{par}"
            p.v(lambda e, ps=ps, tmin=tmin, h2=h2: e.tensor_scalar(
                tmin, ps, bqc[:, h2:h2 + 1], 0.0, op0=OP.add, op1=OP.min),
                reads=[pkb, k_bq], writes=[km], tag=u("l1qa"))
            p.s(lambda e, tmin=tmin, texp=texp: e.activation(texp, tmin, AF.Exp),
                reads=[km], writes=[ke], tag=u("l1qe"))
            p.v(lambda e, ps=ps, tb=tb, h2=h2: e.tensor_scalar(
                qT[:, tb:tb + 512], ps, bqc[:, h2:h2 + 1], 0.0,
                op0=OP.add, op1=OP.max),
                reads=[pkb, k_bq], writes=["l1.qT"], tag=u("l1qr"))
            p.v(lambda e, texp=texp, tb=tb: e.tensor_tensor(
                qT[:, tb:tb + 512], qT[:, tb:tb + 512], texp, op=OP.add),
                reads=[ke, "l1.qT"], writes=["l1.qT"], tag=u("l1qs"))
        mk.bias_row("L1_attn_bk", h2 * 128, 128, key="ROWS", tag=u("l1kbr"))
        for t in range(PT):
            ps, pkx = mk.bank(t % 4)
            p.t(lambda e, ps=ps: e.matmul(
                ps[:, 0:128], mk.ONESF[:, 0:128], mk.ROWS[0:1, 0:128],
                start=True, stop=False),
                reads=["ROWS", "ONESF"], writes=[pkx], tag=u("l1kb"))
            for kp in range(8):
                p.t(lambda e, ps=ps, kp=kp, t=t, wk=wk: e.matmul(
                    ps[:, 0:128], lnT.sl(kp, t * 128, 128), wk.sl(kp, 0, 128),
                    start=False, stop=(kp == 7)),
                    reads=[k_wk, f"l1.lnT.t{t}"], writes=[pkx], tag=u("l1kmm"))
            par = t % 2
            tmin = mk.F32P[:, par * 1024:par * 1024 + 128]
            texp = mk.F32P[:, par * 1024 + 512:par * 1024 + 640]
            km, ke = f"l1.tmin{par}", f"l1.texp{par}"
            p.v(lambda e, ps=ps, tmin=tmin: e.tensor_scalar(
                tmin, ps[:, 0:128], 0.0, None, op0=OP.min),
                reads=[pkx], writes=[km], tag=u("l1ka"))
            p.s(lambda e, tmin=tmin, texp=texp: e.activation(texp, tmin, AF.Exp),
                reads=[km], writes=[ke], tag=u("l1ke"))
            p.v(lambda e, ps=ps, t=t: e.tensor_scalar(
                KP_[:, t * 128:(t + 1) * 128], ps[:, 0:128], 0.0, None, op0=OP.max),
                reads=[pkx], writes=["l1.KP"], tag=u("l1kr"))
            p.v(lambda e, texp=texp, t=t: e.tensor_tensor(
                KP_[:, t * 128:(t + 1) * 128], KP_[:, t * 128:(t + 1) * 128],
                texp, op=OP.add),
                reads=[ke, "l1.KP"], writes=["l1.KP"], tag=u("l1ks"))
        for t in range(PT):
            ps, pkx = mk.bank(4 + t % 2)
            p.t(lambda e, ps=ps: e.matmul(
                ps[:, 0:128], mk.ONESF[:, 0:128], mk.ROWS[0:1, 128:256],
                start=True, stop=False),
                reads=["ROWS2", "ONESF"], writes=[pkx], tag=u("l1vb"))
            for kp in range(8):
                p.t(lambda e, ps=ps, kp=kp, t=t, wv=wv: e.matmul(
                    ps[:, 0:128], lnT.sl(kp, t * 128, 128), wv.sl(kp, 0, 128),
                    start=False, stop=(kp == 7)),
                    reads=[k_wv, f"l1.lnT.t{t}"], writes=[pkx], tag=u("l1vmm"))
            dap = VP[:, t * 130:(t + 1) * 130].rearrange(
                "p (two c) -> p two c", two=2)[:, :, 0:64]
            sap = ps[:, 0:128].rearrange("p (two c) -> p two c", two=2)
            p.v(lambda e, dap=dap, sap=sap: e.tensor_copy(dap, sap),
                reads=[pkx], writes=["l1.VP"], tag=u("l1vev"))
        kvb = mk.TOKB[:, 0:128]
        for h in range(2):
            ps_kv, pkv = mk.bank(h)
            for t in range(PT):
                p.t(lambda e, ps_kv=ps_kv, t=t, h=h: e.matmul(
                    ps_kv[0:64, 0:65],
                    KP_[:, t * 128 + h * 64:t * 128 + (h + 1) * 64],
                    VP[:, t * 130 + h * 65:t * 130 + (h + 1) * 65],
                    start=(t == 0), stop=(t == PT - 1)),
                    reads=["l1.KP", "l1.VP"], writes=[pkv], tag=u("l1kv"))
            p.v(lambda e, ps_kv=ps_kv, kvb=kvb, h=h: e.tensor_copy(
                kvb[h * 64:(h + 1) * 64, 0:65], ps_kv[0:64, 0:65]),
                reads=[pkv], writes=["l1.kvb"], tag=u("l1kve"))
        for t in range(PT):
            zc = mk.STAT[:, 0:2]
            stg = mk.TOKB[:, 1024:1152]
            for h in range(2):
                ps_o, pko = mk.bank(5 - h)
                p.t(lambda e, ps_o=ps_o, t=t, h=h, kvb=kvb: e.matmul(
                    ps_o[:, 0:65],
                    qT[h * 64:(h + 1) * 64, t * 128:(t + 1) * 128],
                    kvb[h * 64:(h + 1) * 64, 0:65],
                    start=True, stop=True),
                    reads=["l1.qT", "l1.kvb"], writes=[pko], tag=u("l1o"))
                p.v(lambda e, ps_o=ps_o, zc=zc, h=h: e.tensor_scalar(
                    zc[:, h:h + 1], ps_o[:, 64:65], EPS, None, op0=OP.add),
                    reads=[pko], writes=["l1.zc"], tag=u("l1za"))
                p.v(lambda e, zc=zc, h=h: e.reciprocal(zc[:, h:h + 1], zc[:, h:h + 1]),
                    reads=["l1.zc"], writes=["l1.zc"], tag=u("l1zr"))
                p.s(lambda e, ps_o=ps_o, stg=stg, zc=zc, h=h: e.activation(
                    stg[:, h * 64:(h + 1) * 64], ps_o[:, 0:64],
                    AF.Identity, scale=zc[:, h:h + 1]),
                    reads=[pko, "l1.zc"], writes=["l1.ostg"], tag=u("l1on"))
            pbf = mk.PBF[t % 2]
            pbk = f"psBF{t % 2}"
            p.t(lambda e, stg=stg, pbf=pbf: e.transpose(
                pbf[:, 0:128], stg, mk.IDB[:, :]),
                reads=["l1.ostg", "IDB"], writes=[pbk], tag=u("l1otr"))
            p.v(lambda e, pbf=pbf, t=t, h2=h2: e.tensor_copy(
                oT.sl(h2, t * 128, 128), pbf[:, 0:128]),
                reads=[pbk], writes=["l1.oT"], tag=u("l1oev"))
    p.barrier()

    wo, k_wo = mk.load_weight_fm("L1_attn_Wo", dst=mk.WT, dstoff=0, key="l1.wo")
    x1T = FMView(mk.A, 0, 8, T)
    cv = mk.constval
    eps_c = cv(1e-5)
    for nm, banki, kk in (("L1_attn_g1", 1, "l1.g1b"), ("L1_attn_be1", 2, "l1.be1b")):
        pg, (pga, pgb) = mk.bank_pair(banki)
        for half in range(2):
            mk.bias_row(nm, half * 512, 512, key="ROWS", tag=u("l1g"))
            p.t(lambda e, pg=pg, half=half: e.matmul(
                pg[:, half * 512:(half + 1) * 512], mk.ONESF[:, 0:128],
                mk.ROWS[0:1, 0:512], start=True, stop=True),
                reads=["ROWS", "ONESF"], writes=[kk, pga if half == 0 else pgb],
                tag=u("l1gb"))
    g1b, _ = mk.bank_pair(1)
    be1b, _ = mk.bank_pair(2)
    for t in range(PT):
        ps, (pk0, pk1) = mk.bank_pair(0)
        z = mk.F32P[:, 0:1024]
        for half in range(2):
            mk.bias_row("L1_attn_bo", half * 512, 512, key="ROWS", tag=u("l1bo"))
            psl = ps[:, half * 512:(half + 1) * 512]
            pkx = pk0 if half == 0 else pk1
            p.t(lambda e, psl=psl: e.matmul(
                psl, mk.ONESF[:, 0:128], mk.ROWS[0:1, 0:512], start=True, stop=False),
                reads=["ROWS", "ONESF"], writes=[pkx], tag=u("l1wob"))
            for kp in range(8):
                p.t(lambda e, psl=psl, kp=kp, t=t, half=half, wo=wo: e.matmul(
                    psl, oT.sl(kp, t * 128, 128), wo.sl(kp, half * 512, 512),
                    start=False, stop=(kp == 7)),
                    reads=["l1.oT", "l1.wo"], writes=[pkx], tag=u("l1womm"))
        srct = mk.F32P[:, 1024:2048]
        p.dma("sync", lambda e, srct=srct, t=t: e.dma_start(
            out=srct, in_=src_dram[t * 128:(t + 1) * 128, :]),
            reads=["l1.src"], writes=["l1.srct"], tag=u("l1srcld"))
        p.v(lambda e, z=z, ps=ps, srct=srct: e.tensor_tensor(
            z[:, 0:512], ps[:, 0:512], srct[:, 0:512], op=OP.add),
            reads=[pk0, "l1.srct"], writes=["l1.z"], tag=u("l1z0"))
        p.v(lambda e, z=z, ps=ps, srct=srct: e.tensor_tensor(
            z[:, 512:1024], ps[:, 512:1024], srct[:, 512:1024], op=OP.add),
            reads=[pk1, "l1.srct"], writes=["l1.z"], tag=u("l1z1"))
        st = mk.STAT[:, 0:8]
        junk = mk.TOKB[:, 0:1024]
        p.v(lambda e, z=z, st=st: e.tensor_reduce(st[:, 0:1], z, axis=AX.X, op=OP.add),
            reads=["l1.z"], writes=["l1.st"], tag=u("l1s0"))
        p.v(lambda e, st=st: e.tensor_scalar(st[:, 1:2], st[:, 0:1], -1.0 / D, None, op0=OP.mult),
            reads=["l1.st"], writes=["l1.st"], tag=u("l1s1"))
        p.s(lambda e, z=z, st=st, junk=junk: e.activation(
            junk, z, AF.Square, bias=st[:, 1:2], accum_out=st[:, 2:3]),
            reads=["l1.z", "l1.st"], writes=["l1.junk", "l1.st"], tag=u("l1s2"))
        p.s(lambda e, st=st: e.activation(st[:, 3:4], st[:, 2:3], AF.Ln, scale=1.0 / D, bias=eps_c),
            reads=["l1.st", "cpool"], writes=["l1.st"], tag=u("l1s3"))
        p.s(lambda e, st=st: e.activation(st[:, 4:5], st[:, 3:4], AF.Exp, scale=-0.5),
            reads=["l1.st"], writes=["l1.st"], tag=u("l1s4"))
        p.v(lambda e, st=st: e.tensor_tensor(st[:, 5:6], st[:, 1:2], st[:, 4:5], op=OP.mult),
            reads=["l1.st"], writes=["l1.st"], tag=u("l1s5"))
        n = mk.F32P[:, 1024:2048]
        p.s(lambda e, z=z, st=st, n=n: e.activation(
            n, z, AF.Identity, scale=st[:, 4:5], bias=st[:, 5:6]),
            reads=["l1.z", "l1.st"], writes=["l1.n"], tag=u("l1s6"))
        p.v(lambda e, n=n, z=z, g1b=g1b: e.tensor_tensor(z, n, g1b[:, :], op=OP.mult),
            reads=["l1.n", "l1.g1b"], writes=["l1.z"], tag=u("l1a1"))
        p.v(lambda e, z=z, be1b=be1b: e.tensor_tensor(z, z, be1b[:, :], op=OP.add),
            reads=["l1.z", "l1.be1b"], writes=["l1.z"], tag=u("l1a2"))
        p.dma("scalar", lambda e, z=z, t=t: e.dma_start(
            out=x1_dram[t * 128:(t + 1) * 128, :], in_=z),
            reads=["l1.z"], writes=["l1.x1"], tag=u("l1x1st"))
        xb = mk.TOKB[:, 1024:2048]
        p.v(lambda e, z=z, xb=xb: e.tensor_copy(xb, z),
            reads=["l1.z"], writes=["l1.xb"], tag=u("l1cast"))
        pbf = mk.PBF[t % 2]
        pbk = f"psBF{t % 2}"
        for j in range(8):
            p.t(lambda e, xb=xb, pbf=pbf, j=j: e.transpose(
                pbf[:, j * 128:(j + 1) * 128], xb[:, j * 128:(j + 1) * 128],
                mk.IDB[:, :]),
                reads=["l1.xb", "IDB"], writes=[pbk], tag=u("l1xtr"))
        dap = mk.fm_tile_ap(x1T, t * 128, 128)
        sap = pbf[:, 0:1024].rearrange("p (j r) -> p j r", j=8)
        p.v(lambda e, sap=sap, dap=dap: e.tensor_copy(dap, sap),
            reads=[pbk], writes=["l1.x1T"], tag=u("l1xev"))
    p.barrier()

    mk._bcol_cursor = 0
    b1c, k_b1 = mk.bias_col("L1_attn_b1")
    for hh in range(2):
        w1h, k_w1 = mk.load_weight_fm("L1_attn_W1", dst=mk.B, dstoff=0,
                                      colslice=(hh * 2048, 2048), key="l1.w1h")
        w2h, k_w2 = mk.load_weight_fm("L1_attn_W2", dst=mk.B, dstoff=16384,
                                      rowslice=(hh * 2048, 2048), key="l1.w2h")
        gT = FMView(mk.M, 0, 16, 512)
        for tb in range(0, T, 512):
            for m in range(16):
                ps, pkb = mk.bank(m % 2)
                for kp in range(8):
                    p.t(lambda e, ps=ps, m=m, kp=kp, tb=tb, w1h=w1h: e.matmul(
                        ps, w1h.sl(kp, m * 128, 128), x1T.sl(kp, tb, 512),
                        start=(kp == 0), stop=(kp == 7)),
                        reads=[k_w1, "l1.x1T"], writes=[pkb], tag=u("l1hmm"))
                p.s(lambda e, ps=ps, m=m, hh=hh: e.activation(
                    gT.sl(m, 0, 512), ps, AF.Gelu, bias=b1c[:, hh * 16 + m:hh * 16 + m + 1]),
                    reads=[pkb, k_b1], writes=["l1.gT"], tag=u("l1gelu"))
            for tl in range(4):
                t = tb // 128 + tl
                for hd in range(2):
                    ps, pkb = mk.bank(4 + hd)
                    if hh == 0:
                        mk.bias_row("L1_attn_b2", hd * 512, 512, key="ROWS", tag=u("l1b2"))
                        p.t(lambda e, ps=ps: e.matmul(
                            ps, mk.ONESF[:, 0:128], mk.ROWS[0:1, 0:512],
                            start=True, stop=False),
                            reads=["ROWS", "ONESF"], writes=[pkb], tag=u("l1yb"))
                    for hp in range(16):
                        p.t(lambda e, ps=ps, hp=hp, tl=tl, hd=hd, w2h=w2h, hh=hh: e.matmul(
                            ps, gT.sl(hp, tl * 128, 128), w2h.sl(hp, hd * 512, 512),
                            start=(hh == 1 and hp == 0), stop=(hp == 15)),
                            reads=["l1.gT", k_w2], writes=[pkb], tag=u("l1ymm"))
                    stg = mk.F32P[:, 2048 + 0:2048 + 512]
                    p.v(lambda e, ps=ps, stg=stg: e.tensor_copy(stg, ps),
                        reads=[pkb], writes=["l1.ystg"], tag=u("l1yev"))
                    if hh == 0:
                        p.dma("scalar", lambda e, stg=stg, t=t, hd=hd: e.dma_start(
                            out=acc_dram[t * 128:(t + 1) * 128, hd * 512:(hd + 1) * 512],
                            in_=stg),
                            reads=["l1.ystg"], writes=[f"l1.acc_{t}_{hd}"], tag=u("l1yst"))
                    else:
                        p.dma("gpsimd", lambda e, stg=stg, t=t, hd=hd: e.dma_start(
                            out=acc_dram[t * 128:(t + 1) * 128, hd * 512:(hd + 1) * 512],
                            in_=stg, accum_op=OP.add),
                            reads=["l1.ystg"], writes=[f"l1.acc_{t}_{hd}"], tag=u("l1yac"))
        p.barrier()

    for nm, banki, kk in (("L1_attn_g2", 1, "l1.g2b"), ("L1_attn_be2", 2, "l1.be2b")):
        pg, (pga, pgb) = mk.bank_pair(banki)
        for half in range(2):
            mk.bias_row(nm, half * 512, 512, key="ROWS", tag=u("l1g2"))
            p.t(lambda e, pg=pg, half=half: e.matmul(
                pg[:, half * 512:(half + 1) * 512], mk.ONESF[:, 0:128],
                mk.ROWS[0:1, 0:512], start=True, stop=True),
                reads=["ROWS", "ONESF"], writes=[kk, pga if half == 0 else pgb],
                tag=u("l1g2b"))
    g2b, _ = mk.bank_pair(1)
    be2b, _ = mk.bank_pair(2)
    for t in range(PT):
        z = mk.F32P[:, 0:1024]
        n = mk.F32P[:, 1024:2048]
        st = mk.STAT[:, 0:8]
        junk = mk.TOKB[:, 0:1024]
        p.dma("sync", lambda e, z=z, t=t: e.dma_start(
            out=z, in_=x1_dram[t * 128:(t + 1) * 128, :]),
            reads=["l1.x1"], writes=["l1.z"], tag=u("l1eld1"))
        p.dma("sync", lambda e, n=n, t=t: e.dma_start(
            out=n, in_=acc_dram[t * 128:(t + 1) * 128, :]),
            reads=[f"l1.acc_{t}_0", f"l1.acc_{t}_1"], writes=["l1.n"], tag=u("l1eld2"))
        p.v(lambda e, z=z, n=n: e.tensor_tensor(z, z, n, op=OP.add),
            reads=["l1.z", "l1.n"], writes=["l1.z"], tag=u("l1ez"))
        p.v(lambda e, z=z, st=st: e.tensor_reduce(st[:, 0:1], z, axis=AX.X, op=OP.add),
            reads=["l1.z"], writes=["l1.st"], tag=u("l1e0"))
        p.v(lambda e, st=st: e.tensor_scalar(st[:, 1:2], st[:, 0:1], -1.0 / D, None, op0=OP.mult),
            reads=["l1.st"], writes=["l1.st"], tag=u("l1e1"))
        p.s(lambda e, z=z, st=st, junk=junk: e.activation(
            junk, z, AF.Square, bias=st[:, 1:2], accum_out=st[:, 2:3]),
            reads=["l1.z", "l1.st"], writes=["l1.junk", "l1.st"], tag=u("l1e2"))
        p.s(lambda e, st=st: e.activation(st[:, 3:4], st[:, 2:3], AF.Ln, scale=1.0 / D, bias=eps_c),
            reads=["l1.st", "cpool"], writes=["l1.st"], tag=u("l1e3"))
        p.s(lambda e, st=st: e.activation(st[:, 4:5], st[:, 3:4], AF.Exp, scale=-0.5),
            reads=["l1.st"], writes=["l1.st"], tag=u("l1e4"))
        p.v(lambda e, st=st: e.tensor_tensor(st[:, 5:6], st[:, 1:2], st[:, 4:5], op=OP.mult),
            reads=["l1.st"], writes=["l1.st"], tag=u("l1e5"))
        p.s(lambda e, z=z, st=st, n=n: e.activation(
            n, z, AF.Identity, scale=st[:, 4:5], bias=st[:, 5:6]),
            reads=["l1.z", "l1.st"], writes=["l1.n"], tag=u("l1e6"))
        p.v(lambda e, n=n, z=z, g2b=g2b: e.tensor_tensor(z, n, g2b[:, :], op=OP.mult),
            reads=["l1.n", "l1.g2b"], writes=["l1.z"], tag=u("l1e7"))
        p.v(lambda e, z=z, be2b=be2b: e.tensor_tensor(z, z, be2b[:, :], op=OP.add),
            reads=["l1.z", "l1.be2b"], writes=["l1.z"], tag=u("l1e8"))
        p.dma("sync", lambda e, n=n, t=t: e.dma_start(
            out=n, in_=xin[t * 128:(t + 1) * 128, :]),
            reads=[xin_key], writes=["l1.n"], tag=u("l1eld3"))
        p.v(lambda e, z=z, n=n: e.tensor_tensor(z, z, n, op=OP.add),
            reads=["l1.z", "l1.n"], writes=["l1.z"], tag=u("l1e9"))
        p.dma("scalar", lambda e, z=z, t=t: e.dma_start(
            out=xout[t * 128:(t + 1) * 128, :], in_=z),
            reads=["l1.z"], writes=[xout_key], tag=u("l1est"))
    p.barrier()


def ph_l1_ffn(mk, xin, xin_key, xout, xout_key):
    p, T, PT = mk.p, mk.T, mk.PT
    u = mk.u
    F = mk.F32P
    lnT = FMView(mk.A, 0, 8, T)
    acc_dram = mk.dram["l1f_acc"]
    gsh_dram = mk.dram["l1f_gsh"]
    NE8 = PT * 8
    rwT = F[:, 2048:2112]
    mk.load_rwT("L1_ffn_router_w", rwT, "lf.rwT", "lf")
    p.barrier()
    logits = F[:, 2112:2112 + NE8]
    hook = mk.router_tile_hook("lf", rwT, "lf.rwT", logits, "lf.logits")
    mk.ln_transpose(xin_key, xin, lnT, "lf.lnT", prefix="lfln", hook=hook)
    p.barrier()
    s_ = F[:, 0:NE8]
    sel = F[:, 256:256 + NE8]
    oh1 = F[:, 512:512 + NE8]
    oh2 = F[:, 768:768 + NE8]
    tmp = F[:, 1024:1024 + NE8]
    iotaB = F[:, 1280:1280 + NE8]
    comb = F[:, 1536:1536 + NE8]
    m1 = F[:, 1792:1792 + PT]
    m2 = F[:, 1824:1824 + PT]
    idx = F[:, 1856:1856 + PT]
    s1 = F[:, 1888:1888 + PT]
    s2 = F[:, 1920:1920 + PT]
    den = F[:, 1952:1952 + PT]
    inv = F[:, 1984:1984 + PT]
    msk = F[:, 2016:2016 + PT]
    p.g(lambda e: e.iota(iotaB, pattern=[[0, PT], [1, 8]], base=0,
                         channel_multiplier=0, allow_small_or_imprecise_dtypes=True),
        writes=["lf.iota"], tag=u("lf.iota"))
    p.s(lambda e: e.activation(s_, logits, AF.Sigmoid),
        reads=["lf.logits"], writes=["lf.s"], tag=u("lf.sig"))
    mk.bias_row("L1_ffn_router_bias", 0, 8, key="ROWS", tag=u("lf.rb"))
    psK, pkK = mk.bank(3)
    p.t(lambda e, psK=psK: e.matmul(
        psK[:, 0:8], mk.ONESF[:, 0:128], mk.ROWS[0:1, 0:8], start=True, stop=True),
        reads=["ROWS", "ONESF"], writes=[pkK], tag=u("lf.rbb"))
    rb = mk.STAT[:, 8:16]
    p.v(lambda e, psK=psK, rb=rb: e.tensor_copy(rb, psK[:, 0:8]),
        reads=[pkK], writes=["lf.rbb"], tag=u("lf.rbe"))
    mk.bTT(sel.rearrange("p (t e) -> p t e", t=PT),
           s_.rearrange("p (t e) -> p t e", t=PT),
           rb.rearrange("p (one e) -> p one e", one=1), OP.add,
           reads=["lf.s", "lf.rbb"], writes=["lf.sel"], tag=u("lf.selc"))
    mk.top2("lf", sel, "lf.sel", PT, dict(
        oh1=(oh1, "lf.oh1"), oh2=(oh2, "lf.oh2"), m1=(m1, "lf.m1"),
        m2=(m2, "lf.m2"), tmp=(tmp, "lf.tmp"), idx=(idx, "lf.idx"),
        iotaB=(iotaB, "lf.iota")))
    v3 = lambda ap: ap.rearrange("p (t e) -> p t e", t=PT)
    v2 = lambda ap: ap.rearrange("p (t one) -> p t one", one=1)
    p.v(lambda e: e.tensor_tensor(tmp, s_, oh1, op=OP.mult),
        reads=["lf.s", "lf.oh1"], writes=["lf.tmp"], tag=u("lf.g1m"))
    p.v(lambda e: e.tensor_reduce(v2(s1), v3(tmp), axis=AX.X, op=OP.add),
        reads=["lf.tmp"], writes=["lf.s1"], tag=u("lf.g1r"))
    p.v(lambda e: e.tensor_tensor(tmp, s_, oh2, op=OP.mult),
        reads=["lf.s", "lf.oh2"], writes=["lf.tmp"], tag=u("lf.g2m"))
    p.v(lambda e: e.tensor_reduce(v2(s2), v3(tmp), axis=AX.X, op=OP.add),
        reads=["lf.tmp"], writes=["lf.s2"], tag=u("lf.g2r"))
    p.v(lambda e: e.tensor_tensor(den, s1, s2, op=OP.add),
        reads=["lf.s1", "lf.s2"], writes=["lf.den"], tag=u("lf.den"))
    p.v(lambda e: e.tensor_scalar(inv, den, 1e-9, None, op0=OP.add),
        reads=["lf.den"], writes=["lf.inv"], tag=u("lf.inva"))
    p.v(lambda e: e.reciprocal(inv, inv),
        reads=["lf.inv"], writes=["lf.inv"], tag=u("lf.invr"))
    p.v(lambda e: e.tensor_scalar(msk, den, 1e-9, None, op0=OP.is_gt),
        reads=["lf.den"], writes=["lf.msk"], tag=u("lf.msk"))
    for sk_, skk in ((s1, "lf.s1"), (s2, "lf.s2")):
        p.v(lambda e, sk_=sk_: e.tensor_tensor(sk_, sk_, inv, op=OP.mult),
            reads=[skk, "lf.inv"], writes=[skk], tag=u("lf.gk1"))
        p.v(lambda e, sk_=sk_: e.tensor_scalar(sk_, sk_, 0.5, None, op0=OP.subtract),
            reads=[skk], writes=[skk], tag=u("lf.gk2"))
        p.v(lambda e, sk_=sk_: e.tensor_tensor(sk_, sk_, msk, op=OP.mult),
            reads=[skk, "lf.msk"], writes=[skk], tag=u("lf.gk3"))
        p.v(lambda e, sk_=sk_: e.tensor_scalar(sk_, sk_, 0.5, None, op0=OP.add),
            reads=[skk], writes=[skk], tag=u("lf.gk4"))
    mk.bTT(v3(tmp), v3(oh1), v2(s1), OP.mult,
           reads=["lf.oh1", "lf.s1"], writes=["lf.tmp"], tag=u("lf.c1"))
    mk.bTT(v3(comb), v3(oh2), v2(s2), OP.mult,
           reads=["lf.oh2", "lf.s2"], writes=["lf.comb"], tag=u("lf.c2"))
    p.v(lambda e: e.tensor_tensor(comb, comb, tmp, op=OP.add),
        reads=["lf.comb", "lf.tmp"], writes=["lf.comb"], tag=u("lf.c3"))
    p.barrier()
    sw1, k_sw1 = mk.load_weight_fm("L1_ffn_sw1", dst=mk.B, dstoff=0, key="lf.sw1")
    sw3, k_sw3 = mk.load_weight_fm("L1_ffn_sw3", dst=mk.B, dstoff=16384, key="lf.sw3")
    aT = FMView(mk.M, 0, 16, 512)
    bT = FMView(mk.M, 8192, 16, 512)
    NBLK = T // 512
    for blk3 in range(0, NBLK, 1):
        pass
    for tb in range(0, T, 512):
        for which, wv, wk, dstv, dstk, act in (
                (0, sw1, k_sw1, aT, "lf.aT", AF.Silu), (1, sw3, k_sw3, bT, "lf.bT", None)):
            for m in range(16):
                ps, pkb = mk.bank((m + which) % 2)
                for kp in range(8):
                    p.t(lambda e, ps=ps, m=m, kp=kp, tb=tb, wv=wv: e.matmul(
                        ps, wv.sl(kp, m * 128, 128), lnT.sl(kp, tb, 512),
                        start=(kp == 0), stop=(kp == 7)),
                        reads=[wk] + [f"lf.lnT.t{tb // 128 + i}" for i in range(4)], writes=[pkb], tag=u("lf.smm"))
                if act is None:
                    p.v(lambda e, ps=ps, m=m, dstv=dstv: e.tensor_copy(
                        dstv.sl(m, 0, 512), ps),
                        reads=[pkb], writes=[dstk], tag=u("lf.sev"))
                else:
                    p.s(lambda e, ps=ps, m=m, dstv=dstv, act=act: e.activation(
                        dstv.sl(m, 0, 512), ps, act),
                        reads=[pkb], writes=[dstk], tag=u("lf.sact"))
        for ch in range(2):
            p.v(lambda e, ch=ch: e.tensor_tensor(
                mk.M[:, ch * 4096:(ch + 1) * 4096], mk.M[:, ch * 4096:(ch + 1) * 4096],
                mk.M[:, 8192 + ch * 4096:8192 + (ch + 1) * 4096], op=OP.mult),
                reads=["lf.aT", "lf.bT"], writes=[f"lf.g{ch}", "lf.aT"], tag=u("lf.gmul"))
            dap = gsh_dram[ch * 1024:(ch + 1) * 1024,
                           tb:tb + 512].rearrange("(j r) c -> r j c", r=128)
            sap = mk.M[:, ch * 4096:(ch + 1) * 4096].rearrange("p (j c) -> p j c", j=8)
            p.dma("scalar", lambda e, dap=dap, sap=sap: e.dma_start(out=dap, in_=sap),
                  reads=[f"lf.g{ch}", "lf.aT"], writes=["lf.gsh"], tag=u("lf.gst"))
    p.barrier()
    sw2, k_sw2 = mk.load_weight_fm("L1_ffn_sw2", dst=mk.B, dstoff=0, key="lf.sw2")
    for t in range(PT):
        sl_ = t % 2
        gt = mk.M[:, sl_ * 2048:(sl_ + 1) * 2048]
        gk = f"lf.gt{sl_}"
        sap = gsh_dram[:, t * 128:(t + 1) * 128].rearrange("(j r) c -> r j c", r=128)
        g3 = gt.rearrange("p (j c) -> p j c", j=16)
        p.dma("sync", lambda e, g3=g3, sap=sap: e.dma_start(out=g3, in_=sap),
              reads=["lf.gsh"], writes=[gk], tag=u("lf.gld"))
        for half in range(2):
            ps, pkb = mk.bank(4 + half)
            for hp in range(16):
                p.t(lambda e, ps=ps, hp=hp, half=half, gt=gt, sw2=sw2: e.matmul(
                    ps, gt[:, hp * 128:(hp + 1) * 128], sw2.sl(hp, half * 512, 512),
                    start=(hp == 0), stop=(hp == 15)),
                    reads=[gk, k_sw2], writes=[pkb], tag=u("lf.somm"))
            pr = t % 2
            stg = (F[:, pr * 1024 + half * 512:pr * 1024 + (half + 1) * 512]
                   if pr == 0 else
                   (F[:, 1024:1536] if half == 0 else F[:, 2048:2560]))
            sk = f"lf.stg{pr}{half}"
            p.v(lambda e, ps=ps, stg=stg: e.tensor_copy(stg, ps),
                reads=[pkb], writes=[sk], tag=u("lf.soev"))
            p.dma("scalar", lambda e, stg=stg, t=t, half=half: e.dma_start(
                out=acc_dram[t * 128:(t + 1) * 128, half * 512:(half + 1) * 512],
                in_=stg),
                reads=[sk], writes=[f"lf.acc_{t}_{half}"], tag=u("lf.sost"))
    p.barrier()
    hT = FMView(mk.M, 0, 8, 512)
    for e_ in range(8):
        off = (e_ % 2) * 16384
        ew1, k_e1 = mk.load_weight_fm("l1f_ew1b", queue="sync", dst=mk.B, dstoff=off,
                                      key=f"lf.ew1_{e_ % 2}", expert=e_,
                                      src_key="cast.l1f_ew1b")
        ew2, k_e2 = mk.load_weight_fm("l1f_ew2b", queue="sync", dst=mk.B,
                                      dstoff=off + 8192,
                                      key=f"lf.ew2_{e_ % 2}", expert=e_,
                                      src_key="cast.l1f_ew2b")
        for tb in range(0, T, 512):
            for m in range(8):
                ps, pkb = mk.bank(m % 2)
                for kp in range(8):
                    p.t(lambda e, ps=ps, m=m, kp=kp, tb=tb, ew1=ew1: e.matmul(
                        ps, ew1.sl(kp, m * 128, 128), lnT.sl(kp, tb, 512),
                        start=(kp == 0), stop=(kp == 7)),
                        reads=[k_e1] + [f"lf.lnT.t{tb // 128 + i}" for i in range(4)], writes=[pkb], tag=u("lf.emm"))
                p.s(lambda e, ps=ps, m=m: e.activation(
                    hT.sl(m, 0, 512), ps, AF.Gelu),
                    reads=[pkb], writes=[f"lf.hT{m}"], tag=u("lf.egelu"))
            for tl in range(4):
                t = tb // 128 + tl
                for hp in range(8):
                    for half in range(2):
                        ps, pkb = mk.bank(4 + half)
                        p.t(lambda e, ps=ps, hp=hp, tl=tl, half=half, ew2=ew2: e.matmul(
                            ps, hT.sl(hp, tl * 128, 128), ew2.sl(hp, half * 512, 512),
                            start=(hp == 0), stop=(hp == 7)),
                            reads=[f"lf.hT{hp}", k_e2], writes=[pkb], tag=u("lf.eymm"))
                for half in range(2):
                    ps, pkb = mk.bank(4 + half)
                    pr = tl % 2
                    stg = (F[:, half * 512:(half + 1) * 512] if pr == 0 else
                           (F[:, 1024:1536] if half == 0 else F[:, 2048:2560]))
                    sk = f"lf.stg{pr}{half}"
                    p.v(lambda e, ps=ps, stg=stg, t=t, e_=e_: e.tensor_scalar(
                        stg, ps, comb[:, t * 8 + e_:t * 8 + e_ + 1], None, op0=OP.mult),
                        reads=[pkb, "lf.comb"], writes=[sk], tag=u("lf.escl"))
                    p.dma("gpsimd", lambda e, stg=stg, t=t, half=half: e.dma_start(
                        out=acc_dram[t * 128:(t + 1) * 128, half * 512:(half + 1) * 512],
                        in_=stg, accum_op=OP.add),
                        reads=[sk], writes=[f"lf.acc_{t}_{half}"], tag=u("lf.eac"))
    p.barrier()
    for t in range(PT):
        z = F[:, 0:1024]
        n = F[:, 1024:2048]
        p.dma("sync", lambda e, z=z, t=t: e.dma_start(
            out=z, in_=acc_dram[t * 128:(t + 1) * 128, :]),
            reads=[f"lf.acc_{t}_0", f"lf.acc_{t}_1"], writes=["lf.z"], tag=u("lf.fld1"))
        p.dma("sync", lambda e, n=n, t=t: e.dma_start(
            out=n, in_=xin[t * 128:(t + 1) * 128, :]),
            reads=[xin_key], writes=["lf.n"], tag=u("lf.fld2"))
        p.v(lambda e, z=z, n=n: e.tensor_tensor(z, z, n, op=OP.add),
            reads=["lf.z", "lf.n"], writes=["lf.z"], tag=u("lf.fadd"))
        p.dma("scalar", lambda e, z=z, t=t: e.dma_start(
            out=xout[t * 128:(t + 1) * 128, :], in_=z),
            reads=["lf.z"], writes=[xout_key], tag=u("lf.fst"))
    p.barrier()



class _MKHelpers:
    def bTT(self, out, in0, in1, op, reads, writes, tag):
        def fn(e, out=out, in0=in0, in1=in1, op=op):
            a, b = bass.broadcast_tensor_aps(in0, in1)
            return e.tensor_tensor(out, a, b, op=op)
        return self.p.v(fn, reads=reads, writes=writes, tag=tag)

    def router_tile_hook(self, prefix, rwT, rwT_key, logits, logits_key, nE=8):
        p = self.p

        def hook(t, tf, tfk, xtk):
            psT, (pkA, pkB) = self.bank_pair(2)
            for j in range(8):
                p.t(lambda e, tf=tf, j=j, psT=psT: e.transpose(
                    psT[:, j * 128:(j + 1) * 128], tf[:, j * 128:(j + 1) * 128],
                    self.IDF[:, :]),
                    reads=[tfk, "IDF"], writes=[pkA if j < 4 else pkB],
                    tag=self.u(f"{prefix}_rtr"))
            ev = self.F32P[:, 0:1024]
            p.v(lambda e, ev=ev, psT=psT: e.tensor_copy(ev, psT[:, :]),
                reads=[pkA, pkB], writes=[xtk], tag=self.u(f"{prefix}_rtev"))
            psL, pkL = self.bank(2)
            for kp in range(8):
                p.t(lambda e, ev=ev, kp=kp, psL=psL: e.matmul(
                    psL[0:nE, 0:128], rwT[:, kp * nE:(kp + 1) * nE],
                    ev[:, kp * 128:(kp + 1) * 128],
                    start=(kp == 0), stop=(kp == 7)),
                    reads=[xtk, rwT_key], writes=[pkL], tag=self.u(f"{prefix}_rmm"))
            lt = self.F32P[0:nE, 2368:2496]
            p.v(lambda e, lt=lt, psL=psL: e.tensor_copy(lt, psL[0:nE, 0:128]),
                reads=[pkL], writes=[f"{prefix}_lt"], tag=self.u(f"{prefix}_rle"))
            psK, pkK = self.bank(3)
            p.t(lambda e, lt=lt, psK=psK: e.matmul(
                psK[:, 0:nE], lt, self.IDF[0:nE, 0:nE], start=True, stop=True),
                reads=[f"{prefix}_lt", "IDF"], writes=[pkK], tag=self.u(f"{prefix}_rtb"))
            p.v(lambda e, psK=psK, t=t: e.tensor_copy(
                logits[:, t * nE:(t + 1) * nE], psK[:, 0:nE]),
                reads=[pkK], writes=[logits_key], tag=self.u(f"{prefix}_rlc"))
        return hook

    def load_rwT(self, wname, dst, dst_key, prefix, nE=8, transposed=False):
        p = self.p
        w = self.dram[wname]
        if transposed:
            src = w.rearrange("(kp p) e -> p kp e", p=128)
            d3 = dst.rearrange("p (kp e) -> p kp e", kp=8)
            p.dma("sync", lambda e: e.dma_start(out=d3, in_=src),
                  writes=[dst_key], tag=self.u(f"{prefix}_rw"))
            return
        rw = self.F32P[0:nE, 0:1024]
        p.dma("sync", lambda e: e.dma_start(out=rw, in_=w[:, :]),
              writes=[f"{prefix}_rwrows"], tag=self.u(f"{prefix}_rwld"))
        psK, pkK = self.bank(3)
        for kp in range(8):
            p.t(lambda e, rw=rw, kp=kp, psK=psK: e.matmul(
                psK[:, kp * nE:(kp + 1) * nE], rw[:, kp * 128:(kp + 1) * 128],
                self.IDF[0:nE, 0:nE], start=True, stop=(kp == 7),
                skip_group_check=True),
                reads=[f"{prefix}_rwrows", "IDF"], writes=[pkK],
                tag=self.u(f"{prefix}_rwt"))
        p.v(lambda e, psK=psK: e.tensor_copy(dst, psK[:, 0:8 * nE]),
            reads=[pkK], writes=[dst_key], tag=self.u(f"{prefix}_rwe"))

    def top2(self, prefix, scores, scores_key, NT, outs):
        p = self.p
        BIG = 1.0e6
        v3 = lambda ap: ap.rearrange("p (t e) -> p t e", t=NT)
        v2 = lambda ap: ap.rearrange("p (t one) -> p t one", one=1)
        oh1, oh1k = outs["oh1"]
        oh2, oh2k = outs["oh2"]
        m1, m1k = outs["m1"]
        m2, m2k = outs["m2"]
        tmp, tmpk = outs["tmp"]
        idx, idxk = outs["idx"]
        iotaB, iotak = outs["iotaB"]
        for rnd, (oh, ohk, m, mkk, src, srck) in enumerate(
                ((oh1, oh1k, m1, m1k, scores, scores_key),
                 (oh2, oh2k, m2, m2k, tmp, tmpk))):
            p.v(lambda e, src=src, m=m: e.tensor_reduce(
                v2(m), v3(src), axis=AX.X, op=OP.max),
                reads=[srck], writes=[mkk], tag=self.u(f"{prefix}_max{rnd}"))
            self.bTT(v3(oh), v3(src), v2(m), OP.is_equal,
                     reads=[srck, mkk], writes=[ohk], tag=self.u(f"{prefix}_eq{rnd}"))
            p.v(lambda e, oh=oh, iotaB=iotaB, tmp=tmp: e.scalar_tensor_tensor(
                tmp, oh, -BIG, iotaB, op0=OP.mult, op1=OP.add),
                reads=[ohk, iotak], writes=[tmpk], tag=self.u(f"{prefix}_msk{rnd}"))
            p.v(lambda e, tmp=tmp, idx=idx: e.tensor_reduce(
                v2(idx), v3(tmp), axis=AX.X, op=OP.min),
                reads=[tmpk], writes=[idxk], tag=self.u(f"{prefix}_min{rnd}"))
            p.v(lambda e, idx=idx: e.tensor_scalar(
                idx, idx, BIG, None, op0=OP.add),
                reads=[idxk], writes=[idxk], tag=self.u(f"{prefix}_unb{rnd}"))
            self.bTT(v3(oh), v3(iotaB), v2(idx), OP.is_equal,
                     reads=[iotak, idxk], writes=[ohk], tag=self.u(f"{prefix}_oh{rnd}"))
            if rnd == 0:
                p.v(lambda e, oh=oh, scores=scores, tmp=tmp: e.scalar_tensor_tensor(
                    tmp, oh, -BIG, scores, op0=OP.mult, op1=OP.add),
                    reads=[ohk, scores_key], writes=[tmpk], tag=self.u(f"{prefix}_m2s"))


for _n in ('bTT', 'router_tile_hook', 'load_rwT', 'top2'):
    setattr(MK, _n, getattr(_MKHelpers, _n))


def ph_l2_attn(mk, xin, xin_key, xout, xout_key):
    p, T, PT = mk.p, mk.T, mk.PT
    u = mk.u
    SINV = 1.0 / 8.0
    lnT = FMView(mk.A, 0, 8, T)
    attnT_dram = mk.dram["l0_attnT"]
    mk._bcol_cursor = 0
    bqc, k_bq = mk.bias_col("L2_attn_bq")
    bkc, k_bk = mk.bias_col("L2_attn_bk")
    mk.ln_transpose(xin_key, xin, lnT, "l2.lnT", prefix="l2ln")
    p.barrier()
    M = mk.M
    qT = M[:, 0:T]
    kT = M[:, T:2 * T]
    VP = M[:, 2 * T:2 * T + PT * 130]
    p.v(lambda e: e.memset(VP[:, :], 1.0), writes=["l2.VP"], tag=u("l2vones"))
    for h2 in range(8):
        p.barrier()
        wq, k_wq = mk.load_weight_fm("L2_attn_Wq", dst=mk.WT, dstoff=0,
                                     colslice=(h2 * 128, 128), key="l2.wq")
        wk, k_wk = mk.load_weight_fm("L2_attn_Wk", dst=mk.WT, dstoff=1024,
                                     colslice=(h2 * 128, 128), key="l2.wk")
        wv, k_wv = mk.load_weight_fm("L2_attn_Wv", dst=mk.WT, dstoff=2048,
                                     colslice=(h2 * 128, 128), key="l2.wv")
        for which, wv_, wk_, bc, bk_, dst, dk in (
                ("q", wq, k_wq, bqc, k_bq, qT, "l2.qT"),
                ("k", wk, k_wk, bkc, k_bk, kT, "l2.kT")):
            for tb in range(0, T, 512):
                ps, pkb = mk.bank(0 if which == "q" else 1)
                for kp in range(8):
                    p.t(lambda e, ps=ps, kp=kp, tb=tb, wv_=wv_: e.matmul(
                        ps, wv_.sl(kp, 0, 128), lnT.sl(kp, tb, 512),
                        start=(kp == 0), stop=(kp == 7)),
                        reads=[wk_] + [f"l2.lnT.t{tb // 128 + i}" for i in range(4)], writes=[pkb], tag=u(f"l2{which}mm"))
                p.s(lambda e, ps=ps, tb=tb, bc=bc, dst=dst, h2=h2: e.activation(
                    dst[:, tb:tb + 512], ps, AF.Identity, bias=bc[:, h2:h2 + 1]),
                    reads=[pkb, bk_], writes=[dk], tag=u(f"l2{which}ev"))
        p.dma("sync", lambda e, h2=h2: e.dma_start(
            out=mk.ROWS[0:1, 128:256],
            in_=mk.dram["L2_attn_bv"][h2 * 128:(h2 + 1) * 128].rearrange(
                "(one n) -> one n", one=1)),
            writes=["ROWS2"], tag=u("l2bv"))
        for t in range(PT):
            ps, pkb = mk.bank(4)
            p.t(lambda e, ps=ps: e.matmul(
                ps[:, 0:128], mk.ONESF[:, 0:128], mk.ROWS[0:1, 128:256],
                start=True, stop=False),
                reads=["ROWS2", "ONESF"], writes=[pkb], tag=u("l2vb"))
            for kp in range(8):
                p.t(lambda e, ps=ps, kp=kp, t=t, wv=wv: e.matmul(
                    ps[:, 0:128], lnT.sl(kp, t * 128, 128), wv.sl(kp, 0, 128),
                    start=False, stop=(kp == 7)),
                    reads=[k_wv, f"l2.lnT.t{t}"], writes=[pkb], tag=u("l2vmm"))
            dap = VP[:, t * 130:(t + 1) * 130].rearrange(
                "p (two c) -> p two c", two=2)[:, :, 0:64]
            sap = ps[:, 0:128].rearrange("p (two c) -> p two c", two=2)
            p.v(lambda e, dap=dap, sap=sap: e.tensor_copy(dap, sap),
                reads=[pkb], writes=["l2.VP"], tag=u("l2vev"))
        pend = []
        pend = []
        for qb in range(0, T, 512):
            def emit_S(kc, h, qb=qb):
                par = kc % 2
                ps_s, pk_s = mk.bank(h * 2 + par)
                p.t(lambda e, ps_s=ps_s, h=h, kc=kc, qb=qb: e.matmul(
                    ps_s, kT[h * 64:(h + 1) * 64, kc * 128:(kc + 1) * 128],
                    qT[h * 64:(h + 1) * 64, qb:qb + 512], start=True, stop=True),
                    reads=["l2.kT", "l2.qT"], writes=[pk_s], tag=u("l2S"))
            for h in range(2):
                emit_S(0, h)
            for kc in range(PT):
                if kc in (2, 4) and pend:
                    pend.pop(0)()
                if kc + 1 < PT:
                    for h in range(2):
                        emit_S(kc + 1, h)
                for h in range(2):
                    par = kc % 2
                    ps_s, pk_s = mk.bank(h * 2 + par)
                    if par == 0:
                        eb = mk.E[:, h * 512:(h + 1) * 512]
                    else:
                        eb = mk.TOKB[:, h * 512:(h + 1) * 512]
                    ekey = f"l2.E{h}_{par}"
                    p.s(lambda e, ps_s=ps_s, eb=eb: e.activation(
                        eb, ps_s, AF.Exp, scale=SINV),
                        reads=[pk_s], writes=[ekey], tag=u("l2exp"))
                    ps_av, pk_av = mk.bank(4 + h)
                    p.t(lambda e, ps_av=ps_av, h=h, kc=kc, eb=eb: e.matmul(
                        ps_av[0:65, :], VP[:, kc * 130 + h * 65:kc * 130 + (h + 1) * 65],
                        eb, start=(kc == 0), stop=(kc == PT - 1)),
                        reads=["l2.VP", ekey], writes=[pk_av], tag=u("l2AV"))
            for h in range(2):
                ps_av, pk_av = mk.bank(4 + h)
                avs = mk.F32P[0:65, h * 512:(h + 1) * 512]
                p.v(lambda e, ps_av=ps_av, avs=avs: e.tensor_copy(
                    avs, ps_av[0:65, 0:512]),
                    reads=[pk_av], writes=[f"l2.avs{h}"], tag=u("l2avse"))

            def mk_out(h, qb=qb, h2=h2):
                avs = mk.F32P[0:65, h * 512:(h + 1) * 512]
                rrow = mk.F32P[0:1, 1024 + h * 512:1024 + (h + 1) * 512]
                rb = mk.F32P[0:64, 2048:2560]
                stg = mk.TOKB[0:64, 1024 + h * 512:1024 + (h + 1) * 512]
                ps_bc, pk_bc = mk.bank(2 * h + 1)

                def emit():
                    p.v(lambda e, avs=avs, rrow=rrow: e.reciprocal(
                        rrow, avs[64:65, :]),
                        reads=[f"l2.avs{h}"], writes=[f"l2.rrow{h}"],
                        tag=u("l2rec"))
                    p.t(lambda e, ps_bc=ps_bc, rrow=rrow: e.matmul(
                        ps_bc[0:64, :], mk.ONESF[0:1, 0:64], rrow,
                        start=True, stop=True),
                        reads=[f"l2.rrow{h}", "ONESF"], writes=[pk_bc],
                        tag=u("l2bc"))
                    p.v(lambda e, ps_bc=ps_bc, rb=rb: e.tensor_copy(rb, ps_bc[0:64, :]),
                        reads=[pk_bc], writes=["l2.rb"], tag=u("l2rbev"))
                    p.v(lambda e, avs=avs, rb=rb, stg=stg: e.tensor_tensor(
                        stg, avs[0:64, :], rb, op=OP.mult),
                        reads=[f"l2.avs{h}", "l2.rb"], writes=[f"l2.ostg{h}"],
                        tag=u("l2onorm"))
                    drow = (2 * h2 + h) * 64
                    p.dma("scalar", lambda e, stg=stg, drow=drow, qb=qb: e.dma_start(
                        out=attnT_dram[drow:drow + 64, qb:qb + 512], in_=stg),
                        reads=[f"l2.ostg{h}"], writes=["l2.attnT"], tag=u("l2ost"))
                return emit
            pend.append(mk_out(0))
            pend.append(mk_out(1))
        while pend:
            pend.pop(0)()
    p.barrier()
    wo, k_wo = mk.load_weight_fm("L2_attn_Wo", dst=mk.WT, dstoff=0, key="l2.wo")
    for half in range(2):
        mk.bias_row("L2_attn_bo", half * 512, 512, key="ROWS", tag=u("l2bo"))
        for t in range(PT):
            sl = t % 2
            atile = mk.M[:, sl * 1024:(sl + 1) * 1024]
            ak = f"l2.at{sl}"
            sap = attnT_dram[:, t * 128:(t + 1) * 128].rearrange("(j r) c -> r j c", r=128)
            at3 = atile.rearrange("p (j c) -> p j c", j=8)
            p.dma("sync", lambda e, at3=at3, sap=sap: e.dma_start(out=at3, in_=sap),
                  reads=["l2.attnT"], writes=[ak], tag=u("l2wold"))
            xt = mk.F32P[:, sl * 1024:sl * 1024 + 512]
            xk = f"l2.wx{sl}"
            p.dma("sync", lambda e, xt=xt, t=t, half=half: e.dma_start(
                out=xt, in_=xin[t * 128:(t + 1) * 128, half * 512:(half + 1) * 512]),
                reads=[xin_key], writes=[xk], tag=u("l2wox"))
            ps, pkb = mk.bank(2 * sl)
            p.t(lambda e, ps=ps: e.matmul(
                ps, mk.ONESF[:, 0:128], mk.ROWS[0:1, 0:512], start=True, stop=False),
                reads=["ROWS", "ONESF"], writes=[pkb], tag=u("l2wob"))
            for kp in range(8):
                p.t(lambda e, ps=ps, kp=kp, atile=atile, half=half, wo=wo: e.matmul(
                    ps, atile[:, kp * 128:(kp + 1) * 128], wo.sl(kp, half * 512, 512),
                    start=False, stop=(kp == 7)),
                    reads=[ak, "l2.wo"], writes=[pkb], tag=u("l2womm"))
            ot = mk.F32P[:, 2048:2560]
            ok = "l2.wo_ot"
            p.v(lambda e, ps=ps, xt=xt, ot=ot: e.tensor_tensor(ot, ps, xt, op=OP.add),
                reads=[pkb, xk], writes=[ok], tag=u("l2woadd"))
            p.dma("scalar", lambda e, ot=ot, t=t, half=half: e.dma_start(
                out=xout[t * 128:(t + 1) * 128, half * 512:(half + 1) * 512], in_=ot),
                reads=[ok], writes=[xout_key], tag=u("l2wost"))
    p.barrier()


def ph_l2_ffn(mk, xin, xin_key, xout, xout_key):
    p, T, PT = mk.p, mk.T, mk.PT
    u = mk.u
    F = mk.F32P
    NE8 = PT * 8
    xnT = FMView(mk.A, 0, 8, T)
    xblk_dram = mk.dram["l2_xblk"]
    acc_dram = mk.dram["l1f_acc"]
    c_dram = mk.dram["l2_c"]
    cv = mk.constval
    eps_c = cv(1e-5)
    mk._bcol_cursor = 0
    gmc, k_gm = mk.bias_col("L2_ffn_gamma")
    btc, k_bt = mk.bias_col("L2_ffn_beta")
    rwT = F[:, 2048:2112]
    mk.load_rwT("L2_ffn_Wr", rwT, "l2f.rwT", "l2f", transposed=True)
    p.barrier()
    psB, pkB = mk.bank(3)
    for kp in range(8):
        p.t(lambda e, kp=kp, psB=psB: e.matmul(
            psB[0:1, 0:8], btc[:, kp:kp + 1], rwT[:, kp * 8:(kp + 1) * 8],
            start=(kp == 0), stop=(kp == 7)),
            reads=[k_bt, "l2f.rwT"], writes=[pkB], tag=u("l2f.bwr"))
    brow = mk.ROWS[0:1, 0:8]
    p.v(lambda e, psB=psB, brow=brow: e.tensor_copy(brow, psB[0:1, 0:8]),
        reads=[pkB], writes=["ROWS"], tag=u("l2f.bsum"))
    p.dma("sync", lambda e: e.dma_start(
        out=mk.ROWS[0:1, 8:16],
        in_=mk.dram["L2_ffn_br"][0:8].rearrange("(one n) -> one n", one=1)),
        writes=["ROWS2"], tag=u("l2f.brld"))
    p.v(lambda e, brow=brow: e.tensor_tensor(
        brow, brow, mk.ROWS[0:1, 8:16], op=OP.add),
        reads=["ROWS", "ROWS2"], writes=["ROWS"], tag=u("l2f.badd"))
    psK, pkK = mk.bank(2)
    p.t(lambda e, psK=psK: e.matmul(
        psK[:, 0:8], mk.ONESF[:, 0:128], mk.ROWS[0:1, 0:8], start=True, stop=True),
        reads=["ROWS", "ONESF"], writes=[pkK], tag=u("l2f.bb"))
    rb = mk.STAT[:, 8:16]
    p.v(lambda e, psK=psK, rb=rb: e.tensor_copy(rb, psK[:, 0:8]),
        reads=[pkK], writes=["l2f.rbb"], tag=u("l2f.rbe"))
    for kp in range(8):
        p.v(lambda e, kp=kp: e.tensor_scalar(
            rwT[:, kp * 8:(kp + 1) * 8], rwT[:, kp * 8:(kp + 1) * 8],
            gmc[:, kp:kp + 1], None, op0=OP.mult),
            reads=["l2f.rwT", k_gm], writes=["l2f.rwT"], tag=u("l2f.fold"))
    p.barrier()
    logits = F[:, 2112:2112 + NE8]
    hook = mk.router_tile_hook("l2f", rwT, "l2f.rwT", logits, "l2f.logits")
    for t in range(PT):
        xt = F[:, 0:1024]
        xk = "l2f_xt"
        xb2 = F[:, 1024:2048]
        xb2k = "l2f_xb2"
        st = mk.STAT[:, 0:8]
        sk = "l2f_st"
        junk = mk.TOKB[:, 0:1024]
        p.dma("sync", lambda e, xt=xt, t=t: e.dma_start(
            out=xt, in_=xin[t * 128:(t + 1) * 128, :]),
            reads=[xin_key], writes=[xk], tag=u("l2f.ldx"))
        p.v(lambda e, xt=xt, st=st: e.tensor_reduce(st[:, 0:1], xt, axis=AX.X, op=OP.add),
            reads=[xk], writes=[sk], tag=u("l2f.s0"))
        p.v(lambda e, st=st: e.tensor_scalar(st[:, 1:2], st[:, 0:1], -1.0 / D, None, op0=OP.mult),
            reads=[sk], writes=[sk], tag=u("l2f.s1"))
        p.s(lambda e, xt=xt, st=st, junk=junk: e.activation(
            junk, xt, AF.Square, bias=st[:, 1:2], accum_out=st[:, 2:3]),
            reads=[xk, sk], writes=["l2f.junk", sk], tag=u("l2f.s2"))
        p.s(lambda e, st=st: e.activation(st[:, 3:4], st[:, 2:3], AF.Ln, scale=1.0 / D, bias=eps_c),
            reads=[sk, "cpool"], writes=[sk], tag=u("l2f.s3"))
        p.s(lambda e, st=st: e.activation(st[:, 4:5], st[:, 3:4], AF.Exp, scale=-0.5),
            reads=[sk], writes=[sk], tag=u("l2f.s4"))
        p.v(lambda e, st=st: e.tensor_tensor(st[:, 5:6], st[:, 1:2], st[:, 4:5], op=OP.mult),
            reads=[sk], writes=[sk], tag=u("l2f.s5"))
        p.s(lambda e, xt=xt, st=st, xb2=xb2: e.activation(
            xb2, xt, AF.Identity, scale=st[:, 4:5], bias=st[:, 5:6]),
            reads=[xk, sk], writes=[xb2k], tag=u("l2f.s6"))
        p.dma("scalar", lambda e, xb2=xb2, t=t: e.dma_start(
            out=xblk_dram[t * 128:(t + 1) * 128, :], in_=xb2),
            reads=[xb2k], writes=["l2f.xblk"], tag=u("l2f.stx"))
        p.v(lambda e, xb2=xb2, st=st: e.tensor_reduce(st[:, 0:1], xb2, axis=AX.X, op=OP.add),
            reads=[xb2k], writes=[sk], tag=u("l2f.t0"))
        p.v(lambda e, st=st: e.tensor_scalar(st[:, 1:2], st[:, 0:1], -1.0 / D, None, op0=OP.mult),
            reads=[sk], writes=[sk], tag=u("l2f.t1"))
        p.s(lambda e, xb2=xb2, st=st, junk=junk: e.activation(
            junk, xb2, AF.Square, bias=st[:, 1:2], accum_out=st[:, 2:3]),
            reads=[xb2k, sk], writes=["l2f.junk", sk], tag=u("l2f.t2"))
        p.s(lambda e, st=st: e.activation(st[:, 3:4], st[:, 2:3], AF.Ln, scale=1.0 / D, bias=eps_c),
            reads=[sk, "cpool"], writes=[sk], tag=u("l2f.t3"))
        p.s(lambda e, st=st: e.activation(st[:, 4:5], st[:, 3:4], AF.Exp, scale=-0.5),
            reads=[sk], writes=[sk], tag=u("l2f.t4"))
        p.v(lambda e, st=st: e.tensor_tensor(st[:, 5:6], st[:, 1:2], st[:, 4:5], op=OP.mult),
            reads=[sk], writes=[sk], tag=u("l2f.t5"))
        n32 = xt
        p.s(lambda e, xb2=xb2, st=st, n32=n32: e.activation(
            n32, xb2, AF.Identity, scale=st[:, 4:5], bias=st[:, 5:6]),
            reads=[xb2k, sk], writes=[xk], tag=u("l2f.t6"))
        nb = mk.TOKB[:, 1024:2048]
        p.v(lambda e, n32=n32, nb=nb: e.tensor_copy(nb, n32),
            reads=[xk], writes=["l2f.nb"], tag=u("l2f.cast"))
        pbf = mk.PBF[t % 2]
        pbk = f"psBF{t % 2}"
        for j in range(8):
            p.t(lambda e, nb=nb, pbf=pbf, j=j: e.transpose(
                pbf[:, j * 128:(j + 1) * 128], nb[:, j * 128:(j + 1) * 128],
                mk.IDB[:, :]),
                reads=["l2f.nb", "IDB"], writes=[pbk], tag=u("l2f.tr"))
        for j in range(8):
            p.v(lambda e, pbf=pbf, j=j, t=t: e.tensor_scalar(
                xnT.sl(j, t * 128, 128), pbf[:, j * 128:(j + 1) * 128],
                gmc[:, j:j + 1], btc[:, j:j + 1], op0=OP.mult, op1=OP.add),
                reads=[pbk, k_gm, k_bt], writes=["l2f.xnT", f"l2f.xnT.t{t}"], tag=u("l2f.ev"))
        hook(t, n32, xk, xk)
    p.barrier()
    s_ = F[:, 0:NE8]
    oh1 = F[:, 512:512 + NE8]
    oh2 = F[:, 768:768 + NE8]
    tmp = F[:, 1024:1024 + NE8]
    iotaB = F[:, 1280:1280 + NE8]
    w_ = F[:, 2112:2112 + NE8]
    m1 = F[:, 1792:1792 + PT]
    m2 = F[:, 1824:1824 + PT]
    idx = F[:, 1856:1856 + PT]
    mx = F[:, 1888:1888 + PT]
    sm = F[:, 1920:1920 + PT]
    v3 = lambda ap: ap.rearrange("p (t e) -> p t e", t=PT)
    v2 = lambda ap: ap.rearrange("p (t one) -> p t one", one=1)
    if getattr(mk, "DEBUG_L2F", False):
        d = mk.dram["dbg_lg"]
        p.dma("scalar", lambda e, d=d: e.dma_start(out=d[:, 0:NE8], in_=logits),
              reads=["l2f.logits"], writes=["dbg_lg"], tag=u("dbglg"))
        d2 = mk.dram["dbg_rb"]
        p.dma("scalar", lambda e, d2=d2: e.dma_start(out=d2[:, 0:8], in_=rb),
              reads=["l2f.rbb"], writes=["dbg_rb"], tag=u("dbgrb"))
        d3 = mk.dram["dbg_bc"]
        p.dma("scalar", lambda e, d3=d3: e.dma_start(out=d3[:, :], in_=mk.BCOL[:, 0:16]),
              reads=[k_gm, k_bt], writes=["dbg_bc"], tag=u("dbgbc"))
    p.g(lambda e: e.iota(iotaB, pattern=[[0, PT], [1, 8]], base=0,
                         channel_multiplier=0, allow_small_or_imprecise_dtypes=True),
        writes=["l2f.iota"], tag=u("l2f.iota"))
    mk.bTT(v3(s_), v3(logits), rb.rearrange("p (one e) -> p one e", one=1), OP.add,
           reads=["l2f.logits", "l2f.rbb"], writes=["l2f.s"], tag=u("l2f.addb"))
    p.v(lambda e: e.tensor_reduce(v2(mx), v3(s_), axis=AX.X, op=OP.max),
        reads=["l2f.s"], writes=["l2f.mx"], tag=u("l2f.smax"))
    mk.bTT(v3(s_), v3(s_), v2(mx), OP.subtract,
           reads=["l2f.s", "l2f.mx"], writes=["l2f.s"], tag=u("l2f.ssub"))
    p.s(lambda e: e.activation(s_, s_, AF.Exp),
        reads=["l2f.s"], writes=["l2f.s"], tag=u("l2f.sexp"))
    p.v(lambda e: e.tensor_reduce(v2(sm), v3(s_), axis=AX.X, op=OP.add),
        reads=["l2f.s"], writes=["l2f.sm"], tag=u("l2f.ssum"))
    p.v(lambda e: e.reciprocal(sm, sm),
        reads=["l2f.sm"], writes=["l2f.sm"], tag=u("l2f.srec"))
    mk.bTT(v3(s_), v3(s_), v2(sm), OP.mult,
           reads=["l2f.s", "l2f.sm"], writes=["l2f.s"], tag=u("l2f.snrm"))
    mk.top2("l2f", s_, "l2f.s", PT, dict(
        oh1=(oh1, "l2f.oh1"), oh2=(oh2, "l2f.oh2"), m1=(m1, "l2f.m1"),
        m2=(m2, "l2f.m2"), tmp=(tmp, "l2f.tmp"), idx=(idx, "l2f.idx"),
        iotaB=(iotaB, "l2f.iota")))
    mk.bTT(v3(tmp), v3(oh1), v2(m1), OP.mult,
           reads=["l2f.oh1", "l2f.m1"], writes=["l2f.tmp"], tag=u("l2f.w1"))
    mk.bTT(v3(w_), v3(oh2), v2(m2), OP.mult,
           reads=["l2f.oh2", "l2f.m2"], writes=["l2f.w"], tag=u("l2f.w2"))
    p.v(lambda e: e.tensor_tensor(w_, w_, tmp, op=OP.add),
        reads=["l2f.w", "l2f.tmp"], writes=["l2f.w"], tag=u("l2f.w3"))
    if getattr(mk, "DEBUG_L2F", False):
        for nm, buf, rk in (("dbg_w", w_, "l2f.w"), ("dbg_s", s_, "l2f.s")):
            d = mk.dram[nm]
            p.dma("scalar", lambda e, d=d, buf=buf: e.dma_start(out=d[:, 0:buf.shape[1]], in_=buf),
                  reads=[rk], writes=[nm], tag=u("dbgw"))
    z0 = F[:, 0:512]
    p.v(lambda e, z0=z0: e.memset(z0, 0.0), writes=["l2f.z0", "l2f.s"], tag=u("l2f.z0"))
    for t in range(PT):
        for half in range(2):
            p.dma("scalar", lambda e, z0=z0, t=t, half=half: e.dma_start(
                out=acc_dram[t * 128:(t + 1) * 128, half * 512:(half + 1) * 512],
                in_=z0),
                reads=["l2f.z0"], writes=[f"l2f.acc_{t}_{half}"], tag=u("l2f.zst"))
    p.barrier()
    hT = FMView(mk.M, 0, 8, 512)
    for e_ in range(8):
        for hq in range(4):
            qb_ = (e_ * 4 + hq) % 2
            mk._bcol_cursor = qb_ * 8
            w1h, k_w1 = mk.load_weight_fm("l2f_w1b", queue="sync", dst=mk.B,
                                          dstoff=qb_ * 16384,
                                          colslice=(hq * 1024, 1024),
                                          key=f"l2f.w1h{qb_}", expert=e_,
                                          src_key="cast.l2f_w1b")
            w2h, k_w2 = mk.load_weight_fm("l2f_w2b", queue="sync", dst=mk.B,
                                          dstoff=qb_ * 16384 + 8192,
                                          rowslice=(hq * 1024, 1024),
                                          key=f"l2f.w2h{qb_}", expert=e_,
                                          src_key="cast.l2f_w2b")
            b1c, k_b1 = mk.bias_col_expert("L2_ffn_b1", e_, hq, width=1024,
                                           key=f"l2f.b1q{qb_}")
            rb1 = mk.TOKB[:, 1152:1160]
            p.v(lambda e, b1c=b1c, rb1=rb1: e.tensor_scalar(
                rb1, b1c, 0.0, None, op0=OP.max),
                reads=[k_b1], writes=["l2f.rb1"], tag=u("l2f.rb1"))
            psc, (pc0, pc1) = mk.bank_pair(2)
            for half in range(2):
                for hp in range(8):
                    p.t(lambda e, psc=psc, hp=hp, half=half, rb1=rb1, w2h=w2h: e.matmul(
                        psc[0:1, half * 512:(half + 1) * 512], rb1[:, hp:hp + 1],
                        w2h.sl(hp, half * 512, 512), start=(hp == 0), stop=(hp == 7),
                        skip_group_check=True),
                        reads=["l2f.rb1", k_w2], writes=[pc0 if half == 0 else pc1],
                        tag=u("l2f.cmm"))
            cstg = mk.TOKB[0:1, 0:1024]
            if hq == 0:
                p.v(lambda e, psc=psc, cstg=cstg: e.tensor_copy(cstg, psc[0:1, 0:1024]),
                    reads=[pc0, pc1], writes=["l2f.cstg"], tag=u("l2f.cev0"))
            else:
                p.v(lambda e, psc=psc, cstg=cstg: e.tensor_tensor(
                    cstg, cstg, psc[0:1, 0:1024], op=OP.add),
                    reads=[pc0, pc1, "l2f.cstg"], writes=["l2f.cstg"], tag=u("l2f.cev1"))
            for tb in range(0, T, 512):
                for m in range(8):
                    ps, pkb = mk.bank(m % 2)
                    for kp in range(8):
                        p.t(lambda e, ps=ps, m=m, kp=kp, tb=tb, w1h=w1h: e.matmul(
                            ps, w1h.sl(kp, m * 128, 128), xnT.sl(kp, tb, 512),
                            start=(kp == 0), stop=(kp == 7)),
                            reads=[k_w1] + [f"l2f.xnT.t{tb // 128 + i}" for i in range(4)], writes=[pkb], tag=u("l2f.hmm"))
                    p.s(lambda e, ps=ps, m=m, b1c=b1c: e.activation(
                        hT.sl(m, 0, 512), ps, AF.Relu, bias=b1c[:, m:m + 1]),
                        reads=[pkb, k_b1], writes=[f"l2f.hT{m}"], tag=u("l2f.relu"))
                for tl in range(4):
                    t = tb // 128 + tl
                    for hp in range(8):
                        for hd in range(2):
                            ps, pkb = mk.bank(4 + hd)
                            p.t(lambda e, ps=ps, hp=hp, tl=tl, hd=hd, w2h=w2h: e.matmul(
                                ps, hT.sl(hp, tl * 128, 128), w2h.sl(hp, hd * 512, 512),
                                start=(hp == 0), stop=(hp == 7)),
                                reads=[f"l2f.hT{hp}", k_w2], writes=[pkb], tag=u("l2f.ymm"))
                    pr = tl % 2
                    for hd in range(2):
                        ps, pkb = mk.bank(4 + hd)
                        stg = F[:, pr * 1024 + hd * 512:pr * 1024 + (hd + 1) * 512]
                        sk = f"l2f.stg{pr}{hd}"
                        p.v(lambda e, ps=ps, stg=stg, t=t, e_=e_: e.tensor_scalar(
                            stg, ps, w_[:, t * 8 + e_:t * 8 + e_ + 1], None, op0=OP.mult),
                            reads=[pkb, "l2f.w"], writes=[sk], tag=u("l2f.scl"))
                        p.dma("gpsimd", lambda e, stg=stg, t=t, hd=hd: e.dma_start(
                            out=acc_dram[t * 128:(t + 1) * 128,
                                         hd * 512:(hd + 1) * 512],
                            in_=stg, accum_op=OP.add),
                            reads=[sk], writes=[f"l2f.acc_{t}_{hd}"],
                            tag=u("l2f.ac"))
        cstg = mk.TOKB[0:1, 0:1024]
        for half in range(2):
            p.dma("sync", lambda e, half=half, e_=e_: e.dma_start(
                out=mk.ROWS[0:1, 0:512],
                in_=mk.dram["L2_ffn_b2"][e_, half * 512:(half + 1) * 512].rearrange(
                    "(one n) -> one n", one=1)),
                writes=["ROWS"], tag=u("l2f.cb2"))
            p.v(lambda e, cstg=cstg, half=half: e.tensor_tensor(
                cstg[0:1, half * 512:(half + 1) * 512],
                cstg[0:1, half * 512:(half + 1) * 512],
                mk.ROWS[0:1, 0:512], op=OP.add),
                reads=["l2f.cstg", "ROWS"], writes=["l2f.cstg"], tag=u("l2f.cba"))
        p.dma("scalar", lambda e, cstg=cstg, e_=e_: e.dma_start(
            out=c_dram[e_:e_ + 1, :], in_=cstg),
            reads=["l2f.cstg"], writes=["l2f.c"], tag=u("l2f.cst"))
    p.barrier()
    csb = mk.E[0:8, 0:1024]
    p.dma("gpsimd", lambda e, csb=csb: e.dma_start(out=csb, in_=c_dram[:, :]),
          reads=["l2f.c"], writes=["l2f.csb"], tag=u("l2f.cld"))
    one8 = mk.ROWS[0:1, 16:24]
    p.v(lambda e, one8=one8: e.memset(one8, 1.0), writes=["l2f.one8"], tag=u("l2f.o8"))
    onec = mk.TOKB[0:8, 1168:1169]
    p.v(lambda e, onec=onec: e.memset(onec, 1.0), writes=["l2f.onec"], tag=u("l2f.oc"))
    psc, (pc0, pc1) = mk.bank_pair(2)
    for half in range(2):
        p.t(lambda e, psc=psc, half=half, onec=onec, csb=csb: e.matmul(
            psc[0:1, half * 512:(half + 1) * 512], onec, csb[:, half * 512:(half + 1) * 512],
            start=True, stop=True, skip_group_check=True),
            reads=["l2f.onec", "l2f.csb"], writes=[pc0 if half == 0 else pc1],
            tag=u("l2f.ct"))
    ctrow = mk.ROWS[0:1, 0:512]
    psb2, pkb2 = mk.bank(0)
    b2sb = mk.M[0:8, 0:1024]
    p.dma("gpsimd", lambda e, b2sb=b2sb: e.dma_start(
        out=b2sb, in_=mk.dram["L2_ffn_b2"][:, :]),
        writes=["l2f.b2sb"], tag=u("l2f.b2ld"))
    cp = mk.TOKB[0:8, 0:1024]
    for half in range(2):
        p.v(lambda e, psc=psc, ctrow=ctrow, half=half: e.tensor_copy(
            ctrow, psc[0:1, half * 512:(half + 1) * 512]),
            reads=[pc0 if half == 0 else pc1], writes=["ROWS"], tag=u("l2f.cte"))
        p.t(lambda e, psb2=psb2, ctrow=ctrow: e.matmul(
            psb2[0:8, 0:512], mk.ONESF[0:1, 0:8], ctrow, start=True, stop=True),
            reads=["ROWS", "ONESF"], writes=[pkb2], tag=u("l2f.ctb"))
        p.v(lambda e, psb2=psb2, cp=cp, csb=csb, half=half: e.tensor_tensor(
            cp[0:8, half * 512:(half + 1) * 512], psb2[0:8, 0:512],
            csb[:, half * 512:(half + 1) * 512], op=OP.subtract),
            reads=[pkb2, "l2f.csb"], writes=["l2f.cp"], tag=u("l2f.cps"))
        p.v(lambda e, cp=cp, b2sb=b2sb, half=half: e.tensor_tensor(
            cp[0:8, half * 512:(half + 1) * 512],
            cp[0:8, half * 512:(half + 1) * 512],
            b2sb[:, half * 512:(half + 1) * 512], op=OP.add),
            reads=["l2f.cp", "l2f.b2sb"], writes=["l2f.cp"], tag=u("l2f.cpb"))
    for t in range(PT):
        pswt, pkwt = mk.bank(1)
        p.t(lambda e, pswt=pswt, t=t: e.matmul(
            pswt[0:8, 0:128], w_[:, t * 8:(t + 1) * 8], mk.IDF[:, :],
            start=True, stop=True),
            reads=["l2f.w", "IDF"], writes=[pkwt], tag=u("l2f.wt"))
        wtb = mk.TOKB[0:8, 1024:1152]
        p.v(lambda e, pswt=pswt, wtb=wtb: e.tensor_copy(wtb, pswt[0:8, 0:128]),
            reads=[pkwt], writes=["l2f.wtb"], tag=u("l2f.wte"))
        for half in range(2):
            z = F[:, half * 512:(half + 1) * 512]
            n = F[:, 1024 + half * 512:1024 + (half + 1) * 512]
            psx, pkx = mk.bank(4 + half)
            p.t(lambda e, psx=psx, wtb=wtb, cp=cp, half=half: e.matmul(
                psx, wtb, cp[0:8, half * 512:(half + 1) * 512],
                start=True, stop=True),
                reads=["l2f.wtb", "l2f.cp"], writes=[pkx], tag=u("l2f.corr"))
            p.dma("sync", lambda e, z=z, t=t, half=half: e.dma_start(
                out=z, in_=acc_dram[t * 128:(t + 1) * 128, half * 512:(half + 1) * 512]),
                reads=[f"l2f.acc_{t}_{half}"], writes=["l2f.z"], tag=u("l2f.fl1"))
            p.dma("sync", lambda e, n=n, t=t, half=half: e.dma_start(
                out=n, in_=xblk_dram[t * 128:(t + 1) * 128, half * 512:(half + 1) * 512]),
                reads=["l2f.xblk"], writes=["l2f.n"], tag=u("l2f.fl2"))
            p.v(lambda e, z=z, psx=psx: e.tensor_tensor(z, z, psx, op=OP.add),
                reads=["l2f.z", pkx], writes=["l2f.z"], tag=u("l2f.fc"))
            p.v(lambda e, z=z, n=n: e.tensor_tensor(z, z, n, op=OP.add),
                reads=["l2f.z", "l2f.n"], writes=["l2f.z"], tag=u("l2f.fa1"))
            p.dma("sync", lambda e, n=n, t=t, half=half: e.dma_start(
                out=n, in_=xin[t * 128:(t + 1) * 128, half * 512:(half + 1) * 512]),
                reads=[xin_key], writes=["l2f.n"], tag=u("l2f.fl3"))
            p.v(lambda e, z=z, n=n: e.tensor_tensor(z, z, n, op=OP.add),
                reads=["l2f.z", "l2f.n"], writes=["l2f.z"], tag=u("l2f.fa2"))
            p.dma("scalar", lambda e, z=z, t=t, half=half: e.dma_start(
                out=xout[t * 128:(t + 1) * 128, half * 512:(half + 1) * 512], in_=z),
                reads=["l2f.z"], writes=[xout_key], tag=u("l2f.fst"))
    p.barrier()


def _bias_col_expert(self, wname, e_, hh, width=2048, key=None):
    p = self.p
    vec = self.dram[wname]
    ncols = width // 128
    c0 = self._bcol_cursor
    assert c0 + ncols <= 48
    self._bcol_cursor += ncols
    dst = self.BCOL[:, c0:c0 + ncols]
    key = key or f"bcole_{wname}_{e_}_{hh}_{c0}"
    psum, pkey = self.bank(4)
    for h0 in range(0, width, 512):
        hn = min(512, width - h0)
        p.dma("sync", lambda e, h0=h0, hn=hn: e.dma_start(
            out=self.ROWS[0:1, 0:hn],
            in_=vec[e_, hh * width + h0:hh * width + h0 + hn].rearrange(
                "(one n) -> one n", one=1)),
            writes=["ROWS"], tag=self.u("ldbe"))
        nch = hn // 128
        for j in range(nch):
            p.t(lambda e, j=j: e.matmul(
                psum[:, j:j + 1], self.ROWS[0:1, j * 128:(j + 1) * 128],
                self.ONE11F[:, :], start=True, stop=True),
                reads=["ROWS", "ONE11F"], writes=[pkey], tag=self.u("r2ce"))
        base = h0 // 128
        p.v(lambda e, nch=nch, base=base: e.tensor_copy(
            dst[:, base:base + nch], psum[:, 0:nch]),
            reads=[pkey], writes=[key], tag=self.u("evbe"))
    return dst, key


def _bias_row_expert(self, wname, e_, hd):
    vec = self.dram[wname]
    self.p.dma("sync", lambda e: e.dma_start(
        out=self.ROWS[0:1, 0:512],
        in_=vec[e_, hd * 512:(hd + 1) * 512].rearrange("(one n) -> one n", one=1)),
        writes=["ROWS"], tag=self.u("ldbre"))


MK.bias_col_expert = _bias_col_expert
MK.bias_row_expert = _bias_row_expert


def ph_l3_attn(mk, xin, xin_key, xout, xout_key):
    p, T, PT = mk.p, mk.T, mk.PT
    u = mk.u
    F = mk.F32P
    M = mk.M
    cv = mk.constval
    eps_c = cv(1e-5)
    wt_dram = mk.dram["l3_wt"]
    bo_dram = mk.dram["l3_bo"]
    WNAMES = ["L3_attn_Wk", "L3_attn_Wv", "L3_attn_Wq", "L3_attn_Wkp",
              "L3_attn_Wqp", "L3_attn_Wo"]
    for i, nm in enumerate(WNAMES):
        p.dma("gpsimd", lambda e, i=i, nm=nm: e.dma_start(
            out=wt_dram[i], in_=mk.dram[nm][:, :]),
            writes=["l3.wbf"], tag=u("l3cast"))
    views = {}
    for i, nm in enumerate(WNAMES):
        pool = mk.A if i < 4 else mk.B
        off = (i % 4) * 8192 if i < 4 else (i - 4) * 8192
        views[nm] = FMView(pool, off, 8, 1024)
        for j in range(8):
            p.dma("sync", lambda e, i=i, j=j, pool=pool, off=off: e.dma_start(
                out=pool[:, off + j * 1024:off + (j + 1) * 1024],
                in_=wt_dram[i][:, j * 128:(j + 1) * 128], transpose=True),
                reads=["l3.wbf"], writes=[f"l3.w_{nm}"], tag=u("l3wtr"))
    wk, wv, wq, wkp, wqp, wo = (views[n] for n in WNAMES)
    p.barrier()
    mk._bcol_cursor = 40
    lnbc, k_lnb = mk.bias_col("L3_attn_ln_b")
    lnbb = mk.TOKB[:, 0:8]
    p.v(lambda e, lnbb=lnbb: e.tensor_copy(lnbb, lnbc[:, 0:8]),
        reads=[k_lnb], writes=["l3.lnbb"], tag=u("l3lnbb"))
    for half in range(2):
        psX, pkX = mk.bank(half)
        for kp in range(8):
            p.t(lambda e, kp=kp, psX=psX, half=half: e.matmul(
                psX[0:1, 0:512], lnbb[:, kp:kp + 1], wo.sl(kp, half * 512, 512),
                start=(kp == 0), stop=(kp == 7)),
                reads=["l3.lnbb", "l3.w_L3_attn_Wo"], writes=[pkX], tag=u("l3bo2"))
        mk.bias_row("L3_attn_bo", half * 512, 512, key="ROWS", tag=u("l3bor"))
        br = mk.ROWS[0:1, 0:512]
        p.v(lambda e, psX=psX, br=br: e.tensor_tensor(
            br, br, psX[0:1, 0:512], op=OP.add),
            reads=[pkX, "ROWS"], writes=["ROWS"], tag=u("l3boa"))
        p.dma("scalar", lambda e, br=br, half=half: e.dma_start(
            out=bo_dram[0:1, half * 512:(half + 1) * 512], in_=br),
            reads=["ROWS"], writes=["l3.bo"], tag=u("l3bos"))
    p.barrier()
    mk._bcol_cursor = 40
    lngc, k_lng = mk.bias_col("L3_attn_ln_g")
    for kp in range(8):
        p.v(lambda e, kp=kp: e.tensor_scalar(
            wo.sl(kp, 0, 1024), wo.sl(kp, 0, 1024), lngc[:, kp:kp + 1], None,
            op0=OP.mult),
            reads=["l3.w_L3_attn_Wo", k_lng], writes=["l3.w_L3_attn_Wo"],
            tag=u("l3fold"))
    mk._bcol_cursor = 0
    bkc, k_bk = mk.bias_col("L3_attn_bk")
    bvc, k_bv = mk.bias_col("L3_attn_bv")
    bqc, k_bq = mk.bias_col("L3_attn_bq")
    bkpc, k_bkp = mk.bias_col("L3_attn_bkp")
    bqpc, k_bqp = mk.bias_col("L3_attn_bqp")
    psc, k_ps = mk.bias_col("L3_attn_phase_scale")
    p.barrier()
    mS = dict(lnT=0, keyT=1024, valT=2048, qTT=3072, kphF=4096, qphF=5120,
              val_t=6144, kph_t=7168, qph_t=8192, sin_k=9216, cos_k=10240,
              sin_q=11264, cos_q=12288, br_=13312, bi_=14336, nT=15360)
    sl_ = lambda name: M[:, mS[name]:mS[name] + 1024]
    TRIG_TMP = M[:, 3072:4096]
    for t in range(PT):
        xt = F[:, 0:1024]
        ln = F[:, 1024:2048]
        st = mk.STAT[:, 0:8]
        junk = mk.TOKB[:, 0:1024]
        kx, lk, sk = "l3.xt", "l3.ln", "l3.st"
        p.dma("sync", lambda e, xt=xt, t=t: e.dma_start(
            out=xt, in_=xin[t * 128:(t + 1) * 128, :]),
            reads=[xin_key], writes=[kx, "l3.ot", "l3.t0", "l3.t1"], tag=u("l3ldx"))
        p.v(lambda e, xt=xt, st=st: e.tensor_reduce(st[:, 0:1], xt, axis=AX.X, op=OP.add),
            reads=[kx], writes=[sk], tag=u("l3s0"))
        p.v(lambda e, st=st: e.tensor_scalar(st[:, 1:2], st[:, 0:1], -1.0 / D, None, op0=OP.mult),
            reads=[sk], writes=[sk], tag=u("l3s1"))
        p.s(lambda e, xt=xt, st=st, junk=junk: e.activation(
            junk, xt, AF.Square, bias=st[:, 1:2], accum_out=st[:, 2:3]),
            reads=[kx, sk], writes=["l3.junk", sk], tag=u("l3s2"))
        p.s(lambda e, st=st: e.activation(st[:, 3:4], st[:, 2:3], AF.Ln, scale=1.0 / D, bias=eps_c),
            reads=[sk, "cpool"], writes=[sk], tag=u("l3s3"))
        p.s(lambda e, st=st: e.activation(st[:, 4:5], st[:, 3:4], AF.Exp, scale=-0.5),
            reads=[sk], writes=[sk], tag=u("l3s4"))
        p.v(lambda e, st=st: e.tensor_tensor(st[:, 5:6], st[:, 1:2], st[:, 4:5], op=OP.mult),
            reads=[sk], writes=[sk], tag=u("l3s5"))
        p.s(lambda e, xt=xt, st=st, ln=ln: e.activation(
            ln, xt, AF.Identity, scale=st[:, 4:5], bias=st[:, 5:6]),
            reads=[kx, sk], writes=[lk], tag=u("l3s6"))
        lnb = mk.TOKB[:, 1024:2048]
        p.v(lambda e, ln=ln, lnb=lnb: e.tensor_copy(lnb, ln),
            reads=[lk], writes=["l3.lnb"], tag=u("l3cast"))
        pbf = mk.PBF[0]
        pbk = "psBF0"
        for j in range(8):
            p.t(lambda e, lnb=lnb, pbf=pbf, j=j: e.transpose(
                pbf[:, j * 128:(j + 1) * 128], lnb[:, j * 128:(j + 1) * 128],
                mk.IDB[:, :]),
                reads=["l3.lnb", "IDB"], writes=[pbk], tag=u("l3lntr"))
        p.v(lambda e, pbf=pbf: e.tensor_copy(sl_("lnT"), pbf[:, :]),
            reads=[pbk], writes=["l3.lnT"], tag=u("l3lnev"))
        def proj(src_slot, src_key, wview, wkey, bcol, bk_, dst_slot, dst_key,
                 tanh_ps=False):
            ps0, (pA, pB) = mk.bank_pair(0)
            for m in range(8):
                pk_ = pA if m < 4 else pB
                for kp in range(8):
                    p.t(lambda e, m=m, kp=kp, ps0=ps0, wview=wview: e.matmul(
                        ps0[:, m * 128:(m + 1) * 128],
                        wview.sl(kp, m * 128, 128),
                        sl_(src_slot)[:, kp * 128:(kp + 1) * 128],
                        start=(kp == 0 and m % 4 == 0), stop=False,
                        skip_group_check=True),
                        reads=[wkey, src_key], writes=[pk_], tag=u("l3pj"))
            if not tanh_ps:
                for m in range(8):
                    pk_ = pA if m < 4 else pB
                    p.s(lambda e, m=m, ps0=ps0: e.activation(
                        sl_(dst_slot)[:, m * 128:(m + 1) * 128],
                        ps0[:, m * 128:(m + 1) * 128], AF.Identity,
                        bias=bcol[:, m:m + 1]),
                        reads=[pk_, bk_], writes=[dst_key], tag=u("l3pe"))
            else:
                for m in range(8):
                    pk_ = pA if m < 4 else pB
                    p.s(lambda e, m=m, ps0=ps0: e.activation(
                        sl_(dst_slot)[:, m * 128:(m + 1) * 128],
                        ps0[:, m * 128:(m + 1) * 128], AF.Tanh,
                        bias=bcol[:, m:m + 1]),
                        reads=[pk_, bk_], writes=[dst_key], tag=u("l3pt"))
                for m in range(8):
                    p.v(lambda e, m=m: e.tensor_scalar(
                        sl_(dst_slot)[:, m * 128:(m + 1) * 128],
                        sl_(dst_slot)[:, m * 128:(m + 1) * 128],
                        psc[:, m:m + 1], None, op0=OP.mult),
                        reads=[dst_key, k_ps], writes=[dst_key], tag=u("l3pp"))
        proj("lnT", "l3.lnT", wk, "l3.w_L3_attn_Wk", bkc, k_bk, "keyT", "l3.keyT")
        proj("lnT", "l3.lnT", wv, "l3.w_L3_attn_Wv", bvc, k_bv, "valT", "l3.valT")
        proj("lnT", "l3.lnT", wq, "l3.w_L3_attn_Wq", bqc, k_bq, "qTT", "l3.qTT")
        proj("keyT", "l3.keyT", wkp, "l3.w_L3_attn_Wkp", bkpc, k_bkp, "kphF",
             "l3.kphF", tanh_ps=True)
        proj("qTT", "l3.qTT", wqp, "l3.w_L3_attn_Wqp", bqpc, k_bqp, "qphF",
             "l3.qphF", tanh_ps=True)
        for src_slot, src_key, dst_slot, dst_key in (
                ("valT", "l3.valT", "val_t", "l3.val_t"),
                ("kphF", "l3.kphF", "kph_t", "l3.kph_t"),
                ("qphF", "l3.qphF", "qph_t", "l3.qph_t")):
            pbf2 = mk.PBF[1]
            pbk2 = "psBF1"
            for j in range(8):
                p.t(lambda e, j=j, pbf2=pbf2, src_slot=src_slot: e.transpose(
                    pbf2[:, j * 128:(j + 1) * 128],
                    sl_(src_slot)[:, j * 128:(j + 1) * 128], mk.IDB[:, :]),
                    reads=[src_key, "IDB"], writes=[pbk2], tag=u("l3tt"))
            p.v(lambda e, pbf2=pbf2, dst_slot=dst_slot: e.tensor_copy(
                sl_(dst_slot), pbf2[:, :]),
                reads=[pbk2], writes=[dst_key], tag=u("l3tte"))
        for ph_slot, ph_key in (("kph_t", "l3.kph_t"), ("qph_t", "l3.qph_t")):
            t1w = sl_("sin_k")
            t2w = sl_("cos_k")
            p.v(lambda e, ph_slot=ph_slot, t1w=t1w: e.tensor_scalar(
                t1w, sl_(ph_slot), math.pi, None, op0=OP.is_gt),
                reads=[ph_key], writes=["l3.sin_k"], tag=u("l3rr1"))
            p.v(lambda e, ph_slot=ph_slot, t2w=t2w: e.tensor_scalar(
                t2w, sl_(ph_slot), -math.pi, None, op0=OP.is_lt),
                reads=[ph_key], writes=["l3.cos_k"], tag=u("l3rr2"))
            p.v(lambda e, t1w=t1w, t2w=t2w: e.tensor_tensor(
                t1w, t1w, t2w, op=OP.subtract),
                reads=["l3.sin_k", "l3.cos_k"], writes=["l3.sin_k"], tag=u("l3rr3"))
            p.v(lambda e, t1w=t1w: e.tensor_scalar(
                t1w, t1w, -2.0 * math.pi, None, op0=OP.mult),
                reads=["l3.sin_k"], writes=["l3.sin_k"], tag=u("l3rr4"))
            p.v(lambda e, ph_slot=ph_slot, t1w=t1w: e.tensor_tensor(
                sl_(ph_slot), sl_(ph_slot), t1w, op=OP.add),
                reads=[ph_key, "l3.sin_k"], writes=[ph_key], tag=u("l3rr5"))
        for ph_slot, ph_key, snm, cnm in (("kph_t", "l3.kph_t", "sin_k", "cos_k"),
                                          ("qph_t", "l3.qph_t", "sin_q", "cos_q")):
            p.s(lambda e, ph_slot=ph_slot, snm=snm: e.activation(
                sl_(snm), sl_(ph_slot), AF.Sin),
                reads=[ph_key], writes=[f"l3.{snm}"], tag=u("l3sin"))
            p.s(lambda e, ph_slot=ph_slot, cnm=cnm: e.activation(
                sl_(cnm), sl_(ph_slot), AF.Sin, scale=0.5),
                reads=[ph_key], writes=[f"l3.{cnm}"], tag=u("l3sinh"))
            p.s(lambda e, cnm=cnm: e.activation(
                sl_(cnm), sl_(cnm), AF.Square),
                reads=[f"l3.{cnm}"], writes=[f"l3.{cnm}"], tag=u("l3sq"))
            p.v(lambda e, cnm=cnm: e.tensor_scalar(
                sl_(cnm), sl_(cnm), -2.0, 1.0, op0=OP.mult, op1=OP.add),
                reads=[f"l3.{cnm}"], writes=[f"l3.{cnm}"], tag=u("l3cos"))
        p.v(lambda e: e.tensor_tensor(sl_("br_"), sl_("val_t"), sl_("cos_k"), op=OP.mult),
            reads=["l3.val_t", "l3.cos_k"], writes=["l3.br"], tag=u("l3br"))
        p.v(lambda e: e.tensor_tensor(sl_("bi_"), sl_("val_t"), sl_("sin_k"), op=OP.mult),
            reads=["l3.val_t", "l3.sin_k"], writes=["l3.bi"], tag=u("l3bi"))
        psR, (pR0, pR1) = mk.bank_pair(1)
        psI, (pI0, pI1) = mk.bank_pair(2)
        for half in range(2):
            p.t(lambda e, half=half, psR=psR: e.matmul(
                psR[:, half * 512:(half + 1) * 512], mk.TRI[:, :],
                sl_("br_")[:, half * 512:(half + 1) * 512], start=True, stop=True),
                reads=["TRI", "l3.br"], writes=[pR0 if half == 0 else pR1],
                tag=u("l3memr"))
            p.t(lambda e, half=half, psI=psI: e.matmul(
                psI[:, half * 512:(half + 1) * 512], mk.TRI[:, :],
                sl_("bi_")[:, half * 512:(half + 1) * 512], start=True, stop=True),
                reads=["TRI", "l3.bi"], writes=[pI0 if half == 0 else pI1],
                tag=u("l3memi"))
        for pass_ in range(2):
            for half in range(2):
                t0 = F[:, 0:512]
                t1 = F[:, 512:1024]
                rh = F[:, 2048:2560]
                pRk = pR0 if half == 0 else pR1
                pIk = pI0 if half == 0 else pI1
                p.v(lambda e, half=half, t0=t0, psR=psR: e.tensor_tensor(
                    t0, psR[:, half * 512:(half + 1) * 512],
                    sl_("cos_q")[:, half * 512:(half + 1) * 512], op=OP.mult),
                    reads=[pRk, "l3.cos_q"], writes=["l3.t0", kx], tag=u("l3r0"))
                p.v(lambda e, half=half, t1=t1, psI=psI: e.tensor_tensor(
                    t1, psI[:, half * 512:(half + 1) * 512],
                    sl_("sin_q")[:, half * 512:(half + 1) * 512], op=OP.mult),
                    reads=[pIk, "l3.sin_q"], writes=["l3.t1", kx], tag=u("l3r1"))
                p.v(lambda e, t0=t0, t1=t1, rh=rh: e.tensor_tensor(
                    rh, t0, t1, op=OP.add),
                    reads=["l3.t0", "l3.t1"], writes=["l3.rh"], tag=u("l3r2"))
                if pass_ == 0:
                    p.v(lambda e, rh=rh, st=st, half=half: e.tensor_reduce(
                        st[:, 6 + half:7 + half], rh, axis=AX.X, op=OP.add),
                        reads=["l3.rh"], writes=[sk], tag=u("l3rs"))
                else:
                    nb2 = mk.TOKB[:, 1024:2048]
                    p.s(lambda e, rh=rh, st=st, half=half, nb2=nb2: e.activation(
                        nb2[:, half * 512:(half + 1) * 512], rh, AF.Identity,
                        scale=st[:, 4:5], bias=st[:, 5:6]),
                        reads=["l3.rh", sk], writes=["l3.nb2"], tag=u("l3nn"))
            if pass_ == 0:
                p.v(lambda e, st=st: e.tensor_tensor(
                    st[:, 0:1], st[:, 6:7], st[:, 7:8], op=OP.add),
                    reads=[sk], writes=[sk], tag=u("l3ra"))
                p.v(lambda e, st=st: e.tensor_scalar(
                    st[:, 1:2], st[:, 0:1], -1.0 / D, None, op0=OP.mult),
                    reads=[sk], writes=[sk], tag=u("l3rb"))
                for half in range(2):
                    t0 = F[:, 0:512]
                    t1 = F[:, 512:1024]
                    rh = F[:, 2048:2560]
                    pRk = pR0 if half == 0 else pR1
                    pIk = pI0 if half == 0 else pI1
                    p.v(lambda e, half=half, t0=t0, psR=psR: e.tensor_tensor(
                        t0, psR[:, half * 512:(half + 1) * 512],
                        sl_("cos_q")[:, half * 512:(half + 1) * 512], op=OP.mult),
                        reads=[pRk, "l3.cos_q"], writes=["l3.t0", kx], tag=u("l3q0"))
                    p.v(lambda e, half=half, t1=t1, psI=psI: e.tensor_tensor(
                        t1, psI[:, half * 512:(half + 1) * 512],
                        sl_("sin_q")[:, half * 512:(half + 1) * 512], op=OP.mult),
                        reads=[pIk, "l3.sin_q"], writes=["l3.t1", kx], tag=u("l3q1"))
                    p.v(lambda e, t0=t0, t1=t1, rh=rh: e.tensor_tensor(
                        rh, t0, t1, op=OP.add),
                        reads=["l3.t0", "l3.t1"], writes=["l3.rh"], tag=u("l3q2"))
                    p.s(lambda e, rh=rh, st=st, junk=junk, half=half: e.activation(
                        junk[:, 0:512], rh, AF.Square, bias=st[:, 1:2],
                        accum_out=st[:, 6 + half:7 + half]),
                        reads=["l3.rh", sk], writes=["l3.junk", sk], tag=u("l3qs"))
                p.v(lambda e, st=st: e.tensor_tensor(
                    st[:, 2:3], st[:, 6:7], st[:, 7:8], op=OP.add),
                    reads=[sk], writes=[sk], tag=u("l3qa"))
                p.s(lambda e, st=st: e.activation(
                    st[:, 3:4], st[:, 2:3], AF.Ln, scale=1.0 / (D * 1024.0), bias=eps_c),
                    reads=[sk, "cpool"], writes=[sk], tag=u("l3ql"))
                p.s(lambda e, st=st: e.activation(
                    st[:, 4:5], st[:, 3:4], AF.Exp, scale=-0.5),
                    reads=[sk], writes=[sk], tag=u("l3qe"))
                p.v(lambda e, st=st: e.tensor_scalar(
                    st[:, 4:5], st[:, 4:5], 1.0 / 32.0, None, op0=OP.mult),
                    reads=[sk], writes=[sk], tag=u("l3qf"))
                p.v(lambda e, st=st: e.tensor_tensor(
                    st[:, 5:6], st[:, 1:2], st[:, 4:5], op=OP.mult),
                    reads=[sk], writes=[sk], tag=u("l3qg"))
        nb2 = mk.TOKB[:, 1024:2048]
        pbf = mk.PBF[0]
        for j in range(8):
            p.t(lambda e, j=j, pbf=pbf, nb2=nb2: e.transpose(
                pbf[:, j * 128:(j + 1) * 128], nb2[:, j * 128:(j + 1) * 128],
                mk.IDB[:, :]),
                reads=["l3.nb2", "IDB"], writes=["psBF0"], tag=u("l3ntr"))
        p.v(lambda e, pbf=pbf: e.tensor_copy(sl_("nT"), pbf[:, :]),
            reads=["psBF0"], writes=["l3.nT"], tag=u("l3nev"))
        for half in range(2):
            psW, pkW = mk.bank(0 if half == 0 else 1)
            p.dma("sync", lambda e, half=half: e.dma_start(
                out=mk.ROWS[0:1, 0:512], in_=bo_dram[0:1, half * 512:(half + 1) * 512]),
                reads=["l3.bo"], writes=["ROWS"], tag=u("l3bold"))
            p.t(lambda e, psW=psW: e.matmul(
                psW, mk.ONESF[:, 0:128], mk.ROWS[0:1, 0:512], start=True, stop=False),
                reads=["ROWS", "ONESF"], writes=[pkW], tag=u("l3wb"))
            for kp in range(8):
                p.t(lambda e, kp=kp, half=half, psW=psW: e.matmul(
                    psW, sl_("nT")[:, kp * 128:(kp + 1) * 128],
                    wo.sl(kp, half * 512, 512), start=False, stop=(kp == 7)),
                    reads=["l3.nT", "l3.w_L3_attn_Wo"], writes=[pkW], tag=u("l3wmm"))
            ot = F[:, 0 + half * 512:512 + half * 512]
            p.v(lambda e, psW=psW, ot=ot, ln=ln, half=half: e.tensor_tensor(
                ot, psW, ln[:, half * 512:(half + 1) * 512], op=OP.add),
                reads=[pkW, lk], writes=["l3.ot", kx], tag=u("l3oa"))
            xrl = F[:, 2048:2560]
            p.dma("sync", lambda e, xrl=xrl, t=t, half=half: e.dma_start(
                out=xrl, in_=xin[t * 128:(t + 1) * 128, half * 512:(half + 1) * 512]),
                reads=[xin_key], writes=["l3.xrl", "l3.rh"], tag=u("l3xrl"))
            p.v(lambda e, ot=ot, xrl=xrl: e.tensor_tensor(ot, ot, xrl, op=OP.add),
                reads=["l3.ot", "l3.xrl"], writes=["l3.ot"], tag=u("l3ob"))
            p.dma("scalar", lambda e, ot=ot, t=t, half=half: e.dma_start(
                out=xout[t * 128:(t + 1) * 128, half * 512:(half + 1) * 512], in_=ot),
                reads=["l3.ot"], writes=[xout_key], tag=u("l3ost"))
    p.barrier()


def ph_l3_ffn(mk, xin, xin_key, xout, xout_key):
    p, T, PT = mk.p, mk.T, mk.PT
    u = mk.u
    F = mk.F32P
    NE8 = PT * 8
    lnT = FMView(mk.A, 0, 8, T)
    acc_dram = mk.dram["l1f_acc"]
    w1d, w2d, w3d = mk.dram["l3f_w1"], mk.dram["l3f_w2"], mk.dram["l3f_w3"]
    for e_ in range(8):
        p.dma("gpsimd", lambda e, e_=e_: e.dma_start(
            out=w1d[e_], in_=mk.dram["L3_ffn_W1"][e_]),
            writes=["l3f.wbf"], tag=u("l3fc1"))
        p.dma("gpsimd", lambda e, e_=e_: e.dma_start(
            out=w2d[e_], in_=mk.dram["L3_ffn_W2"][e_]),
            writes=["l3f.wbf"], tag=u("l3fc2"))
        p.dma("gpsimd", lambda e, e_=e_: e.dma_start(
            out=w3d[e_], in_=mk.dram["L3_ffn_W3"][e_]),
            writes=["l3f.wbf"], tag=u("l3fc3"))
    rwT = F[:, 2048:2112]
    mk.load_rwT("L3_ffn_Wg", rwT, "l3f.rwT", "l3f")
    p.barrier()
    logits = F[:, 2112:2112 + NE8]
    hook = mk.router_tile_hook("l3f", rwT, "l3f.rwT", logits, "l3f.logits")
    mk.ln_transpose(xin_key, xin, lnT, "l3f.lnT", prefix="l3fln", hook=hook)
    p.barrier()
    oh1 = F[:, 512:512 + NE8]
    oh2 = F[:, 768:768 + NE8]
    tmp = F[:, 1024:1024 + NE8]
    iotaB = F[:, 1280:1280 + NE8]
    w_ = F[:, 2112:2112 + NE8]
    m1 = F[:, 1792:1792 + PT]
    m2 = F[:, 1824:1824 + PT]
    idx = F[:, 1856:1856 + PT]
    tt_ = F[:, 1888:1888 + PT]
    rr_ = F[:, 1920:1920 + PT]
    p2_ = F[:, 1952:1952 + PT]
    v3 = lambda ap: ap.rearrange("p (t e) -> p t e", t=PT)
    v2 = lambda ap: ap.rearrange("p (t one) -> p t one", one=1)
    p.g(lambda e: e.iota(iotaB, pattern=[[0, PT], [1, 8]], base=0,
                         channel_multiplier=0, allow_small_or_imprecise_dtypes=True),
        writes=["l3f.iota"], tag=u("l3f.iota"))
    mk.top2("l3f", logits, "l3f.logits", PT, dict(
        oh1=(oh1, "l3f.oh1"), oh2=(oh2, "l3f.oh2"), m1=(m1, "l3f.m1"),
        m2=(m2, "l3f.m2"), tmp=(tmp, "l3f.tmp"), idx=(idx, "l3f.idx"),
        iotaB=(iotaB, "l3f.iota")))
    p.v(lambda e: e.tensor_tensor(tt_, m2, m1, op=OP.subtract),
        reads=["l3f.m2", "l3f.m1"], writes=["l3f.tt"], tag=u("l3f.p0"))
    p.s(lambda e: e.activation(tt_, tt_, AF.Exp),
        reads=["l3f.tt"], writes=["l3f.tt"], tag=u("l3f.p1"))
    p.v(lambda e: e.tensor_scalar(rr_, tt_, 1.0, None, op0=OP.add),
        reads=["l3f.tt"], writes=["l3f.rr"], tag=u("l3f.p2"))
    p.v(lambda e: e.reciprocal(rr_, rr_),
        reads=["l3f.rr"], writes=["l3f.rr"], tag=u("l3f.p3"))
    p.v(lambda e: e.tensor_tensor(p2_, tt_, rr_, op=OP.mult),
        reads=["l3f.tt", "l3f.rr"], writes=["l3f.p2c"], tag=u("l3f.p4"))
    mk.bTT(v3(tmp), v3(oh1), v2(rr_), OP.mult,
           reads=["l3f.oh1", "l3f.rr"], writes=["l3f.tmp"], tag=u("l3f.w1"))
    mk.bTT(v3(w_), v3(oh2), v2(p2_), OP.mult,
           reads=["l3f.oh2", "l3f.p2c"], writes=["l3f.w", "l3f.logits"], tag=u("l3f.w2"))
    p.v(lambda e: e.tensor_tensor(w_, w_, tmp, op=OP.add),
        reads=["l3f.w", "l3f.tmp"], writes=["l3f.w"], tag=u("l3f.w3"))
    z0 = F[:, 0:512]
    p.v(lambda e, z0=z0: e.memset(z0, 0.0), writes=["l3f.z0"], tag=u("l3f.z0"))
    for t in range(PT):
        for half in range(2):
            p.dma("scalar", lambda e, z0=z0, t=t, half=half: e.dma_start(
                out=acc_dram[t * 128:(t + 1) * 128, half * 512:(half + 1) * 512],
                in_=z0),
                reads=["l3f.z0"], writes=[f"l3f.acc_{t}_{half}"], tag=u("l3f.zst"))
    p.barrier()
    s1T = FMView(mk.M, 0, 4, 512)
    h2T = FMView(mk.M, 2048, 4, 512)
    for e_ in range(8):
        for hq in range(4):
            qb_ = (e_ * 4 + hq) % 2
            off = qb_ * 12288
            w1v = FMView(mk.B, off, 8, 512)
            w2v = FMView(mk.B, off + 4096, 8, 512)
            w3v = FMView(mk.B, off + 8192, 4, 1024)
            kw1, kw2, kw3 = (f"l3f.w1v{qb_}", f"l3f.w2v{qb_}", f"l3f.w3v{qb_}")
            for j in range(8):
                p.dma("sync", lambda e, e_=e_, hq=hq, j=j, off=off: e.dma_start(
                    out=mk.B[:, off + j * 512:off + (j + 1) * 512],
                    in_=w1d[e_][hq * 512:(hq + 1) * 512, j * 128:(j + 1) * 128],
                    transpose=True),
                    reads=["l3f.wbf"], writes=[kw1], tag=u("l3f.t1"))
                p.dma("sync", lambda e, e_=e_, hq=hq, j=j, off=off: e.dma_start(
                    out=mk.B[:, off + 4096 + j * 512:off + 4096 + (j + 1) * 512],
                    in_=w2d[e_][hq * 512:(hq + 1) * 512, j * 128:(j + 1) * 128],
                    transpose=True),
                    reads=["l3f.wbf"], writes=[kw2], tag=u("l3f.t2"))
            for j in range(4):
                p.dma("sync", lambda e, e_=e_, hq=hq, j=j, off=off: e.dma_start(
                    out=mk.B[:, off + 8192 + j * 1024:off + 8192 + (j + 1) * 1024],
                    in_=w3d[e_][:, hq * 512 + j * 128:hq * 512 + (j + 1) * 128],
                    transpose=True),
                    reads=["l3f.wbf"], writes=[kw3], tag=u("l3f.t3"))
            for tb in range(0, T, 512):
                for m in range(4):
                    ps, pkb = mk.bank(m % 2)
                    for kp in range(8):
                        p.t(lambda e, ps=ps, m=m, kp=kp, tb=tb, w1v=w1v: e.matmul(
                            ps, w1v.sl(kp, m * 128, 128), lnT.sl(kp, tb, 512),
                            start=(kp == 0), stop=(kp == 7)),
                            reads=[kw1] + [f"l3f.lnT.t{tb // 128 + i}" for i in range(4)], writes=[pkb], tag=u("l3f.h1"))
                    p.s(lambda e, ps=ps, m=m: e.activation(
                        s1T.sl(m, 0, 512), ps, AF.Silu),
                        reads=[pkb], writes=[f"l3f.s1_{m}"], tag=u("l3f.silu"))
                for m in range(4):
                    ps, pkb = mk.bank(2 + m % 2)
                    for kp in range(8):
                        p.t(lambda e, ps=ps, m=m, kp=kp, tb=tb, w2v=w2v: e.matmul(
                            ps, w2v.sl(kp, m * 128, 128), lnT.sl(kp, tb, 512),
                            start=(kp == 0), stop=(kp == 7)),
                            reads=[kw2] + [f"l3f.lnT.t{tb // 128 + i}" for i in range(4)], writes=[pkb], tag=u("l3f.h2"))
                    p.v(lambda e, ps=ps, m=m: e.tensor_copy(h2T.sl(m, 0, 512), ps),
                        reads=[pkb], writes=[f"l3f.h2s{m}"], tag=u("l3f.h2e"))
                for m in range(4):
                    p.v(lambda e, m=m: e.tensor_tensor(
                        mk.M[:, m * 512:(m + 1) * 512], mk.M[:, m * 512:(m + 1) * 512],
                        mk.M[:, 2048 + m * 512:2048 + (m + 1) * 512], op=OP.mult),
                        reads=[f"l3f.s1_{m}", f"l3f.h2s{m}"], writes=[f"l3f.s1_{m}"],
                        tag=u("l3f.g"))
                for tl in range(4):
                    t = tb // 128 + tl
                    for hp in range(4):
                        for hd in range(2):
                            ps, pkb = mk.bank(4 + hd)
                            p.t(lambda e, ps=ps, hp=hp, tl=tl, hd=hd, w3v=w3v: e.matmul(
                                ps, s1T.sl(hp, tl * 128, 128), w3v.sl(hp, hd * 512, 512),
                                start=(hp == 0), stop=(hp == 3)),
                                reads=[f"l3f.s1_{hp}", kw3], writes=[pkb], tag=u("l3f.y"))
                    pr = tl % 2
                    for hd in range(2):
                        ps, pkb = mk.bank(4 + hd)
                        stg = F[:, pr * 1024 + hd * 512:pr * 1024 + (hd + 1) * 512]
                        sk = f"l3f.stg{pr}{hd}"
                        p.v(lambda e, ps=ps, stg=stg, t=t, e_=e_: e.tensor_scalar(
                            stg, ps, w_[:, t * 8 + e_:t * 8 + e_ + 1], None, op0=OP.mult),
                            reads=[pkb, "l3f.w"], writes=[sk], tag=u("l3f.scl"))
                        p.dma("gpsimd", lambda e, stg=stg, t=t, hd=hd: e.dma_start(
                            out=acc_dram[t * 128:(t + 1) * 128,
                                         hd * 512:(hd + 1) * 512],
                            in_=stg, accum_op=OP.add),
                            reads=[sk], writes=[f"l3f.acc_{t}_{hd}"],
                            tag=u("l3f.ac"))
    p.barrier()
    for t in range(PT):
        z = F[:, 0:1024]
        n = F[:, 1024:2048]
        p.dma("sync", lambda e, z=z, t=t: e.dma_start(
            out=z, in_=acc_dram[t * 128:(t + 1) * 128, :]),
            reads=[f"l3f.acc_{t}_0", f"l3f.acc_{t}_1"], writes=["l3f.z", "l3f.stg"],
            tag=u("l3f.fl1"))
        p.dma("sync", lambda e, n=n, t=t: e.dma_start(
            out=n, in_=xin[t * 128:(t + 1) * 128, :]),
            reads=[xin_key], writes=["l3f.n"], tag=u("l3f.fl2"))
        p.v(lambda e, z=z, n=n: e.tensor_tensor(z, z, n, op=OP.add),
            reads=["l3f.z", "l3f.n"], writes=["l3f.z"], tag=u("l3f.fadd"))
        p.dma("scalar", lambda e, z=z, t=t: e.dma_start(
            out=xout[t * 128:(t + 1) * 128, :], in_=z),
            reads=["l3f.z"], writes=[xout_key], tag=u("l3f.fst"))
    p.barrier()


def _fm_pass(self, wview, wkey, srcview, srckey, nm, nkp, T_, evac, tag,
             tbw=512):
    p = self.p
    u = self.u
    nTB = T_ // tbw
    for m in range(nm):
        for g0 in range(0, nTB, 3):
            gn = min(3, nTB - g0)
            par = (g0 // 3) % 2
            banks = [self.bank(par * 3 + j) for j in range(gn)]
            for kp in range(nkp):
                for j in range(gn):
                    ps, pk = banks[j]
                    tb = (g0 + j) * tbw
                    p.t(lambda e, ps=ps, m=m, kp=kp, tb=tb: e.matmul(
                        ps[:, 0:tbw], wview.sl(kp, m * 128, 128),
                        srcview.sl(kp, tb, tbw),
                        start=(kp == 0), stop=(kp == nkp - 1)),
                        reads=[wkey, srckey], writes=[pk], tag=u(tag))
            for j in range(gn):
                ps, pk = banks[j]
                evac(m, (g0 + j) * tbw, ps[:, 0:tbw], pk)


MK.fm_pass = _fm_pass


# version 15 (final)
# speedup vs baseline: 1.1757x; 1.0108x over previous
import math
import numpy as np

import concourse.bass as bass
import concourse.mybir as mybir
from concourse.bass_utils import run_bass_kernel_spmd
from concourse.alu_op_type import AluOpType as OP

F32 = mybir.dt.float32
BF16 = mybir.dt.bfloat16
AF = mybir.ActivationFunctionType

ENGINES = ("tensor", "vector", "scalar", "gpsimd", "sync")
DMA_POOL = 8


def _dma_base(engine):
    if engine == "gpsimd":
        return "gpsimd_dma"
    if engine == "scalar":
        return "scalar_dma"
    return engine


def _chan_engine(chan):
    base = chan.split("#")[0]
    return {"gpsimd_dma": "gpsimd", "scalar_dma": "scalar"}.get(base, base)



class _Op:
    __slots__ = ("engine", "fn", "reads", "writes", "idx", "inc", "count_after",
                 "deps", "tag", "_dma", "chan")

    def __init__(self, engine, fn, reads, writes, tag=""):
        self.engine = engine
        self.fn = fn
        self.reads = reads
        self.writes = writes
        self.idx = -1
        self.inc = 0
        self.count_after = 0
        self.deps = []
        self.tag = tag


class Prog:
    TENSOR_SAME_ENGINE_SYNC = False
    OTHER_SAME_ENGINE_SYNC = True
    PRUNE = True

    def __init__(self):
        self.ops = []
        self._last_write = {}
        self._readers = {}
        self._barrier_deps = []
        self._force_inc = set()
        self._dma_counts = {}

    def add(self, engine, fn, reads=(), writes=(), dma=False, tag=""):
        op = _Op(engine, fn, tuple(reads), tuple(writes), tag)
        op.idx = len(self.ops)
        op._dma = dma
        if dma:
            base = _dma_base(engine)
            i = self._dma_counts.get(base, 0)
            self._dma_counts[base] = i + 1
            op.chan = f"{base}#{i % DMA_POOL}"
        else:
            op.chan = engine
        deps = set(self._barrier_deps)
        for k in op.reads:
            w = self._last_write.get(k)
            if w is not None:
                deps.add(w)
        for k in op.writes:
            w = self._last_write.get(k)
            if w is not None:
                deps.add(w)
            for r in self._readers.get(k, ()):
                deps.add(r)
        deps.discard(op.idx)
        op.deps = sorted(deps)
        for k in op.reads:
            self._readers.setdefault(k, []).append(op.idx)
        for k in op.writes:
            self._last_write[k] = op.idx
            self._readers[k] = []
        self.ops.append(op)
        return op

    def rekey(self, new_key, *old_keys):
        allv = []
        for o in old_keys:
            w = self._last_write.pop(o, None)
            if w is not None:
                allv.append(w)
            allv += self._readers.pop(o, [])
        allv = sorted(set(allv))
        if allv:
            self._readers[new_key] = allv
            self._last_write[new_key] = allv[-1]

    def barrier(self):
        last = {}
        for op in self.ops:
            last[op.chan] = op.idx
        self._barrier_deps = sorted(last.values())
        self._force_inc.update(self._barrier_deps)
        self._last_write = {}
        self._readers = {}

    def t(self, fn, reads=(), writes=(), tag=""):
        return self.add("tensor", fn, reads, writes, tag=tag)

    def v(self, fn, reads=(), writes=(), tag=""):
        return self.add("vector", fn, reads, writes, tag=tag)

    def s(self, fn, reads=(), writes=(), tag=""):
        return self.add("scalar", fn, reads, writes, tag=tag)

    def g(self, fn, reads=(), writes=(), tag=""):
        return self.add("gpsimd", fn, reads, writes, tag=tag)

    def dma(self, queue, fn, reads=(), writes=(), tag=""):
        return self.add(queue, fn, reads, writes, dma=True, tag=tag)

    def seal(self):
        is_target = [False] * len(self.ops)
        for i in self._force_inc:
            is_target[i] = True
        for op in self.ops:
            for d in op.deps:
                dep = self.ops[d]
                if dep.engine == op.engine:
                    if dep.engine == "tensor" and not self.TENSOR_SAME_ENGINE_SYNC:
                        continue
                    if dep.engine != "tensor" and not self.OTHER_SAME_ENGINE_SYNC:
                        continue
                is_target[d] = True
        self.channels = sorted({op.chan for op in self.ops})
        counts = {c: 0 for c in self.channels}
        for op in self.ops:
            if is_target[op.idx] or op._dma:
                op.inc = 16 if op._dma else 1
                counts[op.chan] += op.inc
            op.count_after = counts[op.chan]
        self.final_counts = counts
        vc = {e: {c: 0 for c in self.channels} for e in ENGINES}
        op_vc = [None] * len(self.ops)
        self.op_waits = [None] * len(self.ops)
        for op in self.ops:
            waits = []
            myvc = vc[op.engine]
            for d in sorted(op.deps, key=lambda i: -self.ops[i].count_after):
                dep = self.ops[d]
                if dep.engine == op.engine:
                    if dep.engine == "tensor" and not self.TENSOR_SAME_ENGINE_SYNC:
                        continue
                    if dep.engine != "tensor" and not self.OTHER_SAME_ENGINE_SYNC:
                        continue
                assert dep.inc > 0, (dep.tag, op.tag)
                if self.PRUNE and myvc[dep.chan] >= dep.count_after:
                    continue
                waits.append((dep.chan, dep.count_after))
                dvc = op_vc[d]
                for e2, c2 in dvc.items():
                    if c2 > myvc[e2]:
                        myvc[e2] = c2
                if dep.count_after > myvc[dep.chan]:
                    myvc[dep.chan] = dep.count_after
            dedup = {}
            for (c, v) in waits:
                dedup[c] = max(dedup.get(c, 0), v)
            waits = sorted(dedup.items())
            self.op_waits[op.idx] = waits
            done_vc = dict(myvc)
            done_vc[op.chan] = max(done_vc[op.chan], op.count_after)
            op_vc[op.idx] = done_vc
        return self

    def emit(self, nc):
        import contextlib
        with contextlib.ExitStack() as stack:
            sems = {c: stack.enter_context(nc.semaphore(f"S_{c}".replace("#", "_")))
                    for c in self.channels}
            block = stack.enter_context(nc.Block())
            per_engine = {e: [] for e in ENGINES}
            for op in self.ops:
                per_engine[op.engine].append(op)

            def make_section(engine):
                ops = per_engine[engine]

                def section(eng):
                    for op in ops:
                        if op._dma and op.count_after - op.inc > 0:
                            eng.wait_ge(sems[op.chan], op.count_after - op.inc)
                        for (we, wv) in self.op_waits[op.idx]:
                            eng.wait_ge(sems[we], wv)
                        instr = op.fn(eng)
                        if op.inc:
                            assert instr is not None, f"op {op.tag} needs inc but returned None"
                            instr.then_inc(sems[op.chan], op.inc)
                    for c in self.channels:
                        if self.final_counts[c] and _chan_engine(c) == engine:
                            eng.wait_ge(sems[c], self.final_counts[c])
                return section

            for e in ENGINES:
                if per_engine[e]:
                    getattr(block, e)(make_section(e))
        return self

    def make_const_pool(self, nc, sbuf_tensor, dtype=F32):
        pool = {}

        def constval(x):
            x = float(x)
            if x == 0.0:
                return 0.0
            if x not in pool:
                i = len(pool)
                assert i < sbuf_tensor.shape[1]
                ap = sbuf_tensor[:, i:i + 1]
                self.v(lambda e, ap=ap, x=x: e.memset(ap, x), writes=["cpool"],
                       tag=f"const_{x}")
                nc.const_aps.register(dtype, x, ap)
                pool[x] = x
            return x
        return constval

    def stats(self):
        n_by_eng = {e: 0 for e in ENGINES}
        n_waits = 0
        n_incs = 0
        for op in self.ops:
            n_by_eng[op.engine] += 1
            n_waits += len(self.op_waits[op.idx])
            n_incs += 1 if op.inc else 0
        return dict(n_ops=len(self.ops), per_engine=n_by_eng, waits=n_waits,
                    incs=n_incs, final=self.final_counts)



D = 1024
NCORES = 8
AX = mybir.AxisListType

F32SZ = 4


def ceil_div(a, b):
    return (a + b - 1) // b


class FMView:

    def __init__(self, t, off, nkp, W):
        self.t, self.off, self.nkp, self.W = t, off, nkp, W

    def sl(self, kp, c0, n, r0=0, nr=128):
        assert kp < self.nkp and c0 + n <= self.W, (kp, c0, n, self.nkp, self.W)
        return self.t[r0:r0 + nr, self.off + kp * self.W + c0:
                      self.off + kp * self.W + c0 + n]


class MK:

    def __init__(self, nc, p, T, weights_np):
        self.nc, self.p, self.T = nc, p, T
        self.PT = T // 128
        self.weights_np = weights_np
        self.dram = {}
        self._stack = None
        self.uid = 0

    def u(self, tag):
        self.uid += 1
        return f"{tag}_{self.uid}"

    def din(self, name, shape, dtype=F32):
        ap = self.nc.dram_tensor(name, list(shape), dtype, kind="ExternalInput").ap()
        self.dram[name] = ap
        return ap

    def dout(self, name, shape, dtype=F32):
        ap = self.nc.dram_tensor(name, list(shape), dtype, kind="ExternalOutput").ap()
        self.dram[name] = ap
        return ap

    def dscratch(self, name, shape, dtype=F32):
        ap = self.nc.dram_tensor(name, list(shape), dtype, kind="Internal").ap()
        self.dram[name] = ap
        return ap

    def alloc_pools(self, stack):
        nc, T = self.nc, self.T
        sb = lambda name, shape, dt=BF16: stack.enter_context(nc.sbuf_tensor(name, shape, dt))
        ps = lambda name, shape, dt=F32: stack.enter_context(nc.psum_tensor(name, shape, dt))
        self.A = sb("pool_A", [128, max(8 * T, 32768)])
        self.B = sb("pool_B", [128, max(8 * T, 32768)])
        self.M = sb("pool_M", [128, 16448])
        self.WT = sb("pool_WT", [128, 8192])
        self.F32P = sb("pool_F32", [128, 2560], F32)
        self.TRIG = sb("pool_TRIG", [128, 4096])
        self.E = sb("pool_E", [128, 1024])
        self.TOKB = sb("pool_TOKB", [128, 2048])
        self.STAT = sb("pool_STAT", [128, 16], F32)
        self.BCOL = sb("pool_BCOL", [128, 48], F32)
        self.ROWS = sb("pool_ROWS", [1, 1024], F32)
        self.CP = sb("pool_CP", [128, 16], F32)
        self.IDB = sb("ident_bf", [128, 128])
        self.IDF = sb("ident_f32", [128, 128], F32)
        self.ONESF = sb("ones_f32", [1, 128], F32)
        self.ONE11F = sb("one11_f32", [1, 1], F32)
        self.TRI = sb("tri_bf", [128, 128])
        self.PF = [ps(f"psF{i}", [128, 1024], F32) for i in range(3)]
        self.PBF = [ps(f"psB{i}", [128, 1024], BF16) for i in range(2)]
        self.constval = self.p.make_const_pool(nc, self.CP)
        self._bcol_cursor = 0

    def bank(self, i):
        t = self.PF[i // 2]
        half = i % 2
        return t[:, half * 512:(half + 1) * 512], f"psF{i // 2}_{half}"

    def bank_pair(self, i):
        return self.PF[i], (f"psF{i}_0", f"psF{i}_1")

    def load_consts(self, consts_dram):
        p = self.p
        p.dma("sync", lambda e: e.dma_start(out=self.IDF[:, :], in_=consts_dram[0:128, :]),
              writes=["IDF"], tag="ld_idf")
        p.dma("gpsimd", lambda e: e.dma_start(out=self.IDB[:, :], in_=consts_dram[0:128, :]),
              writes=["IDB"], tag="ld_idb")
        p.dma("gpsimd", lambda e: e.dma_start(out=self.TRI[:, :], in_=consts_dram[128:256, :]),
              writes=["TRI"], tag="ld_tri")
        p.v(lambda e: e.memset(self.ONESF[:, :], 1.0), writes=["ONESF"], tag="ms_onesf")
        p.v(lambda e: e.memset(self.ONE11F[:, :], 1.0), writes=["ONE11F"], tag="ms_one11")

    def bias_col(self, wname, n=None, scale=None):
        p, nc = self.p, self.nc
        vec = self.dram[wname]
        Dv = vec.shape[0] if n is None else n
        ncols = ceil_div(Dv, 128)
        c0 = self._bcol_cursor
        assert c0 + ncols <= 48, "BCOL overflow"
        self._bcol_cursor += ncols
        dst = self.BCOL[:, c0:c0 + ncols]
        key = f"bcol_{wname}_{c0}"
        psum, pkey = self.bank(4)
        for h0 in range(0, Dv, 512):
            hn = min(512, Dv - h0)
            rk = f"rows_bias_{wname}_{h0}"
            p.dma("sync", lambda e, h0=h0, hn=hn: e.dma_start(
                out=self.ROWS[0:1, 0:hn], in_=vec[h0:h0 + hn].rearrange("(one n) -> one n", one=1)),
                writes=["ROWS"], tag=self.u("ld_bias"))
            nch = ceil_div(hn, 128)
            for j in range(nch):
                jn = min(128, hn - j * 128)
                p.t(lambda e, j=j, jn=jn: e.matmul(
                    psum[0:jn, j:j + 1], self.ROWS[0:1, j * 128:j * 128 + jn],
                    self.ONE11F[:, :], start=True, stop=True),
                    reads=["ROWS", "ONE11F"], writes=[pkey], tag=self.u("r2c"))
            base = h0 // 128
            if scale is None:
                p.v(lambda e, nch=nch, base=base: e.tensor_copy(
                    dst[:, base:base + nch], psum[:, 0:nch]),
                    reads=[pkey], writes=[key, "BCOL"], tag=self.u("ev_bias"))
            else:
                p.v(lambda e, nch=nch, base=base: e.tensor_scalar(
                    dst[:, base:base + nch], psum[:, 0:nch], float(scale), None, op0=OP.mult),
                    reads=[pkey], writes=[key, "BCOL"], tag=self.u("ev_bias"))
        return dst, key

    def load_weight_fm(self, wname, queue="gpsimd", dst=None, dstoff=0, key=None,
                       rowslice=None, colslice=None, tag=None, expert=None,
                       src_key=None):
        p = self.p
        w = self.dram[wname]
        if expert is not None:
            w = w[expert]
        R, C = w.shape
        if colslice is not None:
            c0, cn = colslice
        else:
            c0, cn = 0, C
        if rowslice is not None:
            r0, rn = rowslice
            w = w[r0:r0 + rn, :]
            R = rn
        nkp = R // 128
        key = key or f"wfm_{wname}_{dstoff}_{c0}"
        view = FMView(dst, dstoff, nkp, cn)
        src = w[:, c0:c0 + cn].rearrange("(kp p) c -> p kp c", p=128)
        ap_out = dst[:, dstoff:dstoff + nkp * cn].rearrange(
            "p (kp c) -> p kp c", kp=nkp)
        rd = [src_key] if src_key else []
        p.dma(queue, lambda e: e.dma_start(out=ap_out, in_=src),
              reads=rd, writes=[key], tag=tag or self.u(f"ldw_{wname}"))
        return view, key

    @staticmethod
    def fm_tile_ap(view, t0, n):
        full = view.t[:, view.off:view.off + view.nkp * view.W]
        r = full.rearrange("p (j w) -> p j w", j=view.nkp)
        return r[:, :, t0:t0 + n]

    def ln_transpose(self, src_key, src_dram, dst, dst_key, eps=1e-5,
                     affine=None, tok_f32=None, prefix="ln", tiles=None, dst_c0=0,
                     hook=None, xt_db_ok=True):
        p, T = self.p, self.T
        cv = self.constval
        eps_c = cv(eps)
        t0r, t1r = (0, self.PT) if tiles is None else tiles
        xt_db = xt_db_ok
        for t in range(t0r, t1r):
            tl = t - t0r
            sl = t % 2
            xt = self.F32P[:, sl * 1024:(sl + 1) * 1024] if xt_db                 else self.F32P[:, 0:1024]
            xk = f"{prefix}_xt{sl}" if xt_db else f"{prefix}_xt"
            st = self.STAT[:, sl * 8:(sl + 1) * 8]
            sk = f"{prefix}_st{sl}"
            xn = self.TOKB[:, sl * 1024:(sl + 1) * 1024]
            xnk = f"{prefix}_xn{sl}"
            junk = self.E[:, 0:1024]
            pbf = self.PBF[sl]
            pbk = f"psBF{sl}"
            tfk = f"{prefix}_tf{sl}"
            p.dma("sync", lambda e, t=t, xt=xt: e.dma_start(
                out=xt, in_=src_dram[t * 128:(t + 1) * 128, :]),
                reads=[src_key], writes=[xk, tfk], tag=self.u(f"{prefix}_ldx"))
            p.v(lambda e, xt=xt, st=st: e.tensor_reduce(
                st[:, 0:1], xt, axis=AX.X, op=OP.add),
                reads=[xk], writes=[sk], tag=self.u(f"{prefix}_sum"))
            p.v(lambda e, st=st: e.tensor_scalar(
                st[:, 1:2], st[:, 0:1], -1.0 / D, None, op0=OP.mult),
                reads=[sk], writes=[sk], tag=self.u(f"{prefix}_nm"))
            p.s(lambda e, xt=xt, st=st, junk=junk: e.activation(
                junk, xt, AF.Square, bias=st[:, 1:2], accum_out=st[:, 2:3]),
                reads=[xk, sk], writes=["ln_junk", sk], tag=self.u(f"{prefix}_sq"))
            p.s(lambda e, st=st: e.activation(
                st[:, 3:4], st[:, 2:3], AF.Ln, scale=1.0 / D, bias=eps_c),
                reads=[sk, "cpool"], writes=[sk], tag=self.u(f"{prefix}_lnv"))
            p.s(lambda e, st=st: e.activation(
                st[:, 4:5], st[:, 3:4], AF.Exp, scale=-0.5),
                reads=[sk], writes=[sk], tag=self.u(f"{prefix}_rstd"))
            p.v(lambda e, st=st: e.tensor_tensor(
                st[:, 5:6], st[:, 1:2], st[:, 4:5], op=OP.mult),
                reads=[sk], writes=[sk], tag=self.u(f"{prefix}_nmr"))
            p.s(lambda e, xt=xt, st=st, xn=xn: e.activation(
                xn, xt, AF.Identity, scale=st[:, 4:5], bias=st[:, 5:6]),
                reads=[xk, sk], writes=[xnk], tag=self.u(f"{prefix}_norm"))
            for j in range(8):
                p.t(lambda e, xn=xn, pbf=pbf, j=j: e.transpose(
                    pbf[:, j * 128:(j + 1) * 128], xn[:, j * 128:(j + 1) * 128],
                    self.IDB[:, :]),
                    reads=[xnk, "IDB"], writes=[pbk], tag=self.u(f"{prefix}_tr"))
            tkey = f"{dst_key}.t{dst_c0 // 128 + tl}"
            if affine is None:
                dap = self.fm_tile_ap(dst, dst_c0 + tl * 128, 128)
                sap = pbf[:, 0:1024].rearrange("p (j r) -> p j r", j=8)
                p.v(lambda e, sap=sap, dap=dap: e.tensor_copy(dap, sap),
                    reads=[pbk], writes=[dst_key, tkey], tag=self.u(f"{prefix}_ev"))
            else:
                gcol, bcol, akeys = affine
                for j in range(8):
                    p.v(lambda e, pbf=pbf, j=j, tl=tl: e.tensor_scalar(
                        dst.sl(j, dst_c0 + tl * 128, 128), pbf[:, j * 128:(j + 1) * 128],
                        gcol[:, j:j + 1], bcol[:, j:j + 1], op0=OP.mult, op1=OP.add),
                        reads=[pbk] + akeys, writes=[dst_key, tkey],
                        tag=self.u(f"{prefix}_eva"))
            if hook is not None:
                hook(t)
            if tok_f32 is not None:
                tf = xt
                if affine is None:
                    p.s(lambda e, xt=xt, st=st, tf=tf: e.activation(
                        tf, xt, AF.Identity, scale=st[:, 4:5], bias=st[:, 5:6]),
                        reads=[xk, sk], writes=[tfk], tag=self.u(f"{prefix}_tok"))
                else:
                    raise NotImplementedError
                tdram, tkey = tok_f32
                p.dma("scalar", lambda e, t=t, tf=tf: e.dma_start(
                    out=tdram[t * 128:(t + 1) * 128, :], in_=tf),
                    reads=[tfk], writes=[tkey], tag=self.u(f"{prefix}_sttok"))

    def bias_row(self, wname, lo, n, key="ROWS", tag=None):
        vec = self.dram[wname]
        self.p.dma("sync", lambda e: e.dma_start(
            out=self.ROWS[0:1, 0:n], in_=vec[lo:lo + n].rearrange("(one n) -> one n", one=1)),
            writes=[key], tag=tag or self.u(f"brow_{wname}"))
        return self.ROWS[0:1, 0:n]

    @staticmethod
    def evens(ap2d):
        r = ap2d.rearrange("p (c two) -> p c two", two=2)
        return r[:, :, 0:1], r[:, :, 1:2]

    @staticmethod
    def as3(ap2d):
        return ap2d.rearrange("p (c one) -> p c one", one=1)


def ph_l0_attn(mk, xin, xin_key, xout, xout_key):
    p, T, PT = mk.p, mk.T, mk.PT
    u = mk.u
    SINV = 1.0 / math.sqrt(16.0 + 512.0 + 512.0)
    lnT = FMView(mk.A, 0, 8, T)
    ctqT = FMView(mk.B, 0, 4, T)
    ctkvT = FMView(mk.B, 4 * T, 4, T)
    ktr_dram = mk.dram["l0_ktr"]
    attnT_dram = mk.dram["l0_attnT"]
    rc_dram, rs_dram = mk.dram["rope_cos"], mk.dram["rope_sin"]

    mk._bcol_cursor = 0
    b_dq, k_dq = mk.bias_col("L0_attn_Wd_q_b")
    b_dkv, k_dkv = mk.bias_col("L0_attn_Wd_kv_b")
    b_uq, k_uq = mk.bias_col("L0_attn_Wu_q_b")
    b_uk, k_uk = mk.bias_col("L0_attn_Wu_k_b")

    mk.ln_transpose(xin_key, xin, lnT, "l0.lnT", prefix="l0ln")
    p.barrier()

    wq, k_wq = mk.load_weight_fm("L0_attn_Wd_q_w", dst=mk.WT, dstoff=0)
    wkv, k_wkv = mk.load_weight_fm("L0_attn_Wd_kv_w", dst=mk.WT, dstoff=4096)
    for which, wv, wk, bc, bk, dstv, dstk in (
        ("q", wq, k_wq, b_dq, k_dq, ctqT, "l0.ctqT"),
        ("kv", wkv, k_wkv, b_dkv, k_dkv, ctkvT, "l0.ctkvT"),
    ):
        for m in range(4):
            for tb in range(0, T, 512):
                ps, pk = mk.bank(m % 2)
                for kp in range(8):
                    p.t(lambda e, ps=ps, wv=wv, m=m, kp=kp, tb=tb: e.matmul(
                        ps, wv.sl(kp, m * 128, 128), lnT.sl(kp, tb, 512),
                        start=(kp == 0), stop=(kp == 7)),
                        reads=[wk] + [f"l0.lnT.t{tb // 128 + i}" for i in range(4)], writes=[pk], tag=u(f"l0ct{which}mm"))
                p.s(lambda e, ps=ps, dstv=dstv, m=m, tb=tb, bc=bc: e.activation(
                    dstv.sl(m, tb, 512), ps, AF.Identity, bias=bc[:, m:m + 1]),
                    reads=[pk, bk], writes=[dstk], tag=u(f"l0ct{which}ev"))
    p.barrier()

    wkr, k_wkr = mk.load_weight_fm("L0_attn_Wk_r_w", dst=mk.WT, dstoff=0, key="l0.wkr")
    for t in range(PT):
        stg = mk.TOKB[:, 0:1024]
        for half in range(2):
            ps, pk = mk.bank(half)
            mk.bias_row("L0_attn_Wk_r_b", half * 512, 512, key="ROWS", tag=u("l0bkr"))
            p.t(lambda e, ps=ps: e.matmul(
                ps, mk.ONESF[:, 0:128], mk.ROWS[0:1, 0:512],
                start=True, stop=False),
                reads=["ROWS", "ONESF"], writes=[pk], tag=u("l0krb"))
            for kp in range(8):
                p.t(lambda e, ps=ps, kp=kp, t=t, half=half: e.matmul(
                    ps, lnT.sl(kp, t * 128, 128), wkr.sl(kp, half * 512, 512),
                    start=False, stop=(kp == 7)),
                    reads=["l0.wkr", f"l0.lnT.t{t}"], writes=[pk], tag=u("l0krmm"))
            cs = mk.TRIG[:, half * 512:half * 512 + 256]
            sn = mk.TRIG[:, 1024 + half * 512:1024 + half * 512 + 256]
            ck, sk2 = f"l0.cs{half}", f"l0.sn{half}"
            p.dma("gpsimd", lambda e, t=t, half=half, cs=cs: e.dma_start(
                out=cs, in_=rc_dram[t * 128:(t + 1) * 128, half * 256:half * 256 + 256]),
                writes=[ck], tag=u("l0cs"))
            p.dma("gpsimd", lambda e, t=t, half=half, sn=sn: e.dma_start(
                out=sn, in_=rs_dram[t * 128:(t + 1) * 128, half * 256:half * 256 + 256]),
                writes=[sk2], tag=u("l0sn"))
            x0, x1 = MK.evens(ps)
            t0 = MK.as3(mk.F32P[:, half * 512:half * 512 + 256])
            t1 = MK.as3(mk.F32P[:, half * 512 + 256:half * 512 + 512])
            sh = stg[:, half * 512:(half + 1) * 512]
            s_even, s_odd = MK.evens(sh)
            csap = MK.as3(cs)
            snap = MK.as3(sn)
            tk0, tk1, sgk = f"l0.t0{half}", f"l0.t1{half}", f"l0.stg{half}"
            p.v(lambda e, x0=x0, csap=csap, t0=t0: e.tensor_tensor(t0, x0, csap, op=OP.mult),
                reads=[pk, ck], writes=[tk0], tag=u("l0r0"))
            p.v(lambda e, x1=x1, snap=snap, t1=t1: e.tensor_tensor(t1, x1, snap, op=OP.mult),
                reads=[pk, sk2], writes=[tk1], tag=u("l0r1"))
            p.v(lambda e, t0=t0, t1=t1, s_even=s_even: e.tensor_tensor(
                s_even, t0, t1, op=OP.subtract),
                reads=[tk0, tk1], writes=[sgk], tag=u("l0r2"))
            p.v(lambda e, x1=x1, csap=csap, t0=t0: e.tensor_tensor(t0, x1, csap, op=OP.mult),
                reads=[pk, ck], writes=[tk0], tag=u("l0r3"))
            p.v(lambda e, x0=x0, snap=snap, t1=t1: e.tensor_tensor(t1, x0, snap, op=OP.mult),
                reads=[pk, sk2], writes=[tk1], tag=u("l0r4"))
            p.v(lambda e, t0=t0, t1=t1, s_odd=s_odd: e.tensor_tensor(s_odd, t0, t1, op=OP.add),
                reads=[tk0, tk1], writes=[sgk], tag=u("l0r5"))
        pbf = mk.PBF[t % 2]
        pbk = f"psBF{t % 2}"
        for j in range(8):
            p.t(lambda e, stg=stg, pbf=pbf, j=j: e.transpose(
                pbf[:, j * 128:(j + 1) * 128], stg[:, j * 128:(j + 1) * 128],
                mk.IDB[:, :]),
                reads=["l0.stg0", "l0.stg1", "IDB"], writes=[pbk], tag=u("l0ktr_tr"))
        st2 = mk.TOKB[:, 1024:2048]
        p.v(lambda e, pbf=pbf, st2=st2: e.tensor_copy(st2, pbf[:, :]),
            reads=[pbk], writes=["l0.st2"], tag=u("l0ktr_ev"))
        dap = ktr_dram[:, t * 128:(t + 1) * 128].rearrange("(j r) c -> r j c", r=128)
        sap2 = st2.rearrange("p (j c) -> p j c", j=8)
        p.dma("scalar", lambda e, sap2=sap2, dap=dap: e.dma_start(out=dap, in_=sap2),
              reads=["l0.st2"], writes=["l0.ktr_dram"], tag=u("l0ktr_st"))
    p.barrier()

    M = mk.M
    qT = M[:, 0:T]
    kT = M[:, T:2 * T]
    VP = M[:, 2 * T:2 * T + PT * 130]
    QR = M[:, 2 * T + PT * 130:3 * T + PT * 130]
    p.v(lambda e: e.memset(VP[:, :], 1.0), writes=["l0.VP"], tag=u("l0vones"))
    import os as _os
    _NP2 = int(_os.environ.get('L0PAIRS', '8'))
    _order = list(range(_NP2)) if not _os.environ.get('L0REV') else list(range(_NP2))[::-1]
    for h2 in _order:
        p.barrier()
        wuq, k_wuq = mk.load_weight_fm("L0_attn_Wu_q_w", dst=mk.WT, dstoff=0,
                                       colslice=(h2 * 128, 128), key="l0.wuq")
        wuk, k_wuk = mk.load_weight_fm("L0_attn_Wu_k_w", dst=mk.WT, dstoff=512,
                                       colslice=(h2 * 128, 128), key="l0.wuk")
        wuv, k_wuv = mk.load_weight_fm("L0_attn_Wu_v_w", dst=mk.WT, dstoff=1024,
                                       colslice=(h2 * 128, 128), key="l0.wuv")
        wqr, k_wqr = mk.load_weight_fm("L0_attn_Wq_r_w", dst=mk.WT, dstoff=1536,
                                       colslice=(h2 * 128, 128), key="l0.wqr")
        cosP = mk.TRIG[:, 0:2048]
        sinP = mk.TRIG[:, 2048:4096]
        csrc = rc_dram[:, h2 * 64:(h2 + 1) * 64].rearrange("(t p) c -> p t c", p=128)
        ssrc = rs_dram[:, h2 * 64:(h2 + 1) * 64].rearrange("(t p) c -> p t c", p=128)
        ntr = T // 128
        cosP3 = cosP[:, 0:ntr * 64].rearrange("p (t c) -> p t c", t=ntr)
        sinP3 = sinP[:, 0:ntr * 64].rearrange("p (t c) -> p t c", t=ntr)
        p.dma("gpsimd", lambda e, csrc=csrc, cosP3=cosP3: e.dma_start(out=cosP3, in_=csrc),
              writes=["l0.cosP"], tag=u("l0cosP"))
        p.dma("gpsimd", lambda e, ssrc=ssrc, sinP3=sinP3: e.dma_start(out=sinP3, in_=ssrc),
              writes=["l0.sinP"], tag=u("l0sinP"))
        for tb in range(0, T, 512):
            ps, pkb = mk.bank(0)
            for kp in range(4):
                p.t(lambda e, ps=ps, kp=kp, tb=tb, wuq=wuq: e.matmul(
                    ps, wuq.sl(kp, 0, 128), ctqT.sl(kp, tb, 512),
                    start=(kp == 0), stop=(kp == 3)),
                    reads=[k_wuq, "l0.ctqT"], writes=[pkb], tag=u("l0qc"))
            p.s(lambda e, ps=ps, tb=tb, h2=h2: e.activation(
                qT[:, tb:tb + 512], ps, AF.Identity, bias=b_uq[:, h2:h2 + 1]),
                reads=[pkb, k_uq], writes=["l0.qT"], tag=u("l0qcev"))
        mk.bias_row("L0_attn_Wq_r_b", h2 * 128, 128, key="ROWS", tag=u("l0bqr"))
        for t in range(PT):
            ps, pkb = mk.bank(2 + (t % 2))
            p.t(lambda e, ps=ps: e.matmul(
                ps[:, 0:128], mk.ONESF[:, 0:128], mk.ROWS[0:1, 0:128],
                start=True, stop=False),
                reads=["ROWS", "ONESF"], writes=[pkb], tag=u("l0qrb"))
            for kp in range(4):
                p.t(lambda e, ps=ps, kp=kp, t=t, wqr=wqr: e.matmul(
                    ps[:, 0:128], ctqT.sl(kp, t * 128, 128), wqr.sl(kp, 0, 128),
                    start=False, stop=(kp == 3)),
                    reads=[k_wqr, "l0.ctqT"], writes=[pkb], tag=u("l0qrmm"))
            p.v(lambda e, ps=ps, t=t: e.tensor_copy(QR[:, t * 128:(t + 1) * 128], ps[:, 0:128]),
                reads=[pkb], writes=["l0.QR"], tag=u("l0qrev"))
        GSEG = min(2048, T)
        for g in range(T // GSEG):
            seg = QR[:, g * GSEG:(g + 1) * GSEG]
            x0, x1 = MK.evens(seg)
            cseg = MK.as3(cosP[:, g * GSEG // 2:(g + 1) * GSEG // 2])
            sseg = MK.as3(sinP[:, g * GSEG // 2:(g + 1) * GSEG // 2])
            t0 = MK.as3(mk.F32P[:, 0:GSEG // 2])
            t1 = MK.as3(mk.F32P[:, 1024:1024 + GSEG // 2])
            stg = mk.TOKB[:, 0:GSEG]
            s_even, s_odd = MK.evens(stg)
            p.v(lambda e, x0=x0, cseg=cseg, t0=t0: e.tensor_tensor(t0, x0, cseg, op=OP.mult),
                reads=["l0.QR", "l0.cosP"], writes=["l0.t0"], tag=u("l0R0"))
            p.v(lambda e, x1=x1, sseg=sseg, t1=t1: e.tensor_tensor(t1, x1, sseg, op=OP.mult),
                reads=["l0.QR", "l0.sinP"], writes=["l0.t1"], tag=u("l0R1"))
            p.v(lambda e, t0=t0, t1=t1, s_even=s_even: e.tensor_tensor(
                s_even, t0, t1, op=OP.subtract),
                reads=["l0.t0", "l0.t1"], writes=["l0.rstg"], tag=u("l0R2"))
            p.v(lambda e, x1=x1, cseg=cseg, t0=t0: e.tensor_tensor(t0, x1, cseg, op=OP.mult),
                reads=["l0.QR", "l0.cosP"], writes=["l0.t0"], tag=u("l0R3"))
            p.v(lambda e, x0=x0, sseg=sseg, t1=t1: e.tensor_tensor(t1, x0, sseg, op=OP.mult),
                reads=["l0.QR", "l0.sinP"], writes=["l0.t1"], tag=u("l0R4"))
            p.v(lambda e, t0=t0, t1=t1, s_odd=s_odd: e.tensor_tensor(s_odd, t0, t1, op=OP.add),
                reads=["l0.t0", "l0.t1"], writes=["l0.rstg"], tag=u("l0R5"))
            for tt in range(GSEG // 128):
                t = g * (GSEG // 128) + tt
                pbf = mk.PBF[tt % 2]
                pbk = f"psBF{tt % 2}"
                p.t(lambda e, stg=stg, tt=tt, pbf=pbf: e.transpose(
                    pbf[:, 0:128], stg[:, tt * 128:(tt + 1) * 128], mk.IDB[:, :]),
                    reads=["l0.rstg", "IDB"], writes=[pbk], tag=u("l0Rtr"))
                p.v(lambda e, pbf=pbf, t=t: e.tensor_tensor(
                    qT[:, t * 128:(t + 1) * 128], qT[:, t * 128:(t + 1) * 128],
                    pbf[:, 0:128], op=OP.add),
                    reads=[pbk, "l0.qT"], writes=["l0.qT"], tag=u("l0Radd"))
        ktr_sb = QR
        p.rekey("l0.ktrsb", "l0.QR")
        p.dma("sync", lambda e, h2=h2: e.dma_start(
            out=ktr_sb[:, :], in_=ktr_dram[h2 * 128:(h2 + 1) * 128, :]),
            reads=["l0.ktr_dram"], writes=["l0.ktrsb"], tag=u("l0ktrld"))
        for tb in range(0, T, 512):
            ps, pkb = mk.bank(1)
            for kp in range(4):
                p.t(lambda e, ps=ps, kp=kp, tb=tb, wuk=wuk: e.matmul(
                    ps, wuk.sl(kp, 0, 128), ctkvT.sl(kp, tb, 512),
                    start=(kp == 0), stop=(kp == 3)),
                    reads=[k_wuk, "l0.ctkvT"], writes=[pkb], tag=u("l0kc"))
            p.s(lambda e, ps=ps, tb=tb, h2=h2: e.activation(
                kT[:, tb:tb + 512], ps, AF.Identity, bias=b_uk[:, h2:h2 + 1]),
                reads=[pkb, k_uk], writes=["l0.kT"], tag=u("l0kcev"))
            p.v(lambda e, tb=tb: e.tensor_tensor(
                kT[:, tb:tb + 512], kT[:, tb:tb + 512], ktr_sb[:, tb:tb + 512], op=OP.add),
                reads=["l0.kT", "l0.ktrsb"], writes=["l0.kT"], tag=u("l0kadd"))
        p.dma("sync", lambda e, h2=h2: e.dma_start(
            out=mk.ROWS[0:1, 512:640],
            in_=mk.dram["L0_attn_Wu_v_b"][h2 * 128:(h2 + 1) * 128].rearrange("(one n) -> one n", one=1)),
            writes=["ROWS2"], tag=u("l0buv"))
        for t in range(PT):
            ps, pkb = mk.bank(4)
            p.t(lambda e, ps=ps: e.matmul(
                ps[:, 0:128], mk.ONESF[:, 0:128], mk.ROWS[0:1, 512:640],
                start=True, stop=False),
                reads=["ROWS2", "ONESF"], writes=[pkb], tag=u("l0vb"))
            for kp in range(4):
                p.t(lambda e, ps=ps, kp=kp, t=t, wuv=wuv: e.matmul(
                    ps[:, 0:128], ctkvT.sl(kp, t * 128, 128), wuv.sl(kp, 0, 128),
                    start=False, stop=(kp == 3)),
                    reads=[k_wuv, "l0.ctkvT"], writes=[pkb], tag=u("l0vmm"))
            dap = VP[:, t * 130:(t + 1) * 130].rearrange("p (two c) -> p two c", two=2)[:, :, 0:64]
            sap = ps[:, 0:128].rearrange("p (two c) -> p two c", two=2)
            p.v(lambda e, dap=dap, sap=sap: e.tensor_copy(dap, sap),
                reads=[pkb], writes=["l0.VP"], tag=u("l0vev"))
        if getattr(mk, "DEBUG_L0", False):
            for nm, buf, rk in (("dbg_qT", qT, "l0.qT"), ("dbg_kT", kT, "l0.kT"),
                                ("dbg_VP", VP, "l0.VP")):
                d = mk.dram[nm]
                p.dma("scalar", lambda e, d=d, buf=buf, h2=h2: e.dma_start(
                    out=d[h2 * 128:(h2 + 1) * 128, 0:buf.shape[1]], in_=buf),
                    reads=[rk], writes=[nm], tag=u("dbg"))
        pend = []
        pend = []
        for qb in range(0, T, 512):
            def emit_S(kc, h, qb=qb):
                par = kc % 2
                ps_s, pk_s = mk.bank(h * 2 + par)
                p.t(lambda e, ps_s=ps_s, h=h, kc=kc, qb=qb: e.matmul(
                    ps_s, kT[h * 64:(h + 1) * 64, kc * 128:(kc + 1) * 128],
                    qT[h * 64:(h + 1) * 64, qb:qb + 512], start=True, stop=True),
                    reads=["l0.kT", "l0.qT"], writes=[pk_s], tag=u("l0S"))
            for h in range(2):
                emit_S(0, h)
            for kc in range(PT):
                if kc in (2, 4) and pend:
                    pend.pop(0)()
                if kc + 1 < PT:
                    for h in range(2):
                        emit_S(kc + 1, h)
                for h in range(2):
                    par = kc % 2
                    ps_s, pk_s = mk.bank(h * 2 + par)
                    if par == 0:
                        eb = mk.E[:, h * 512:(h + 1) * 512]
                    else:
                        eb = mk.TOKB[:, h * 512:(h + 1) * 512]
                    ekey = f"l0.E{h}_{par}"
                    p.s(lambda e, ps_s=ps_s, eb=eb: e.activation(
                        eb, ps_s, AF.Exp, scale=SINV),
                        reads=[pk_s], writes=[ekey], tag=u("l0exp"))
                    ps_av, pk_av = mk.bank(4 + h)
                    p.t(lambda e, ps_av=ps_av, h=h, kc=kc, eb=eb: e.matmul(
                        ps_av[0:65, :], VP[:, kc * 130 + h * 65:kc * 130 + (h + 1) * 65],
                        eb, start=(kc == 0), stop=(kc == PT - 1)),
                        reads=["l0.VP", ekey], writes=[pk_av], tag=u("l0AV"))
            for h in range(2):
                ps_av, pk_av = mk.bank(4 + h)
                avs = mk.F32P[0:65, h * 512:(h + 1) * 512]
                p.v(lambda e, ps_av=ps_av, avs=avs: e.tensor_copy(
                    avs, ps_av[0:65, 0:512]),
                    reads=[pk_av], writes=[f"l0.avs{h}"], tag=u("l0avse"))

            def mk_out(h, qb=qb, h2=h2):
                avs = mk.F32P[0:65, h * 512:(h + 1) * 512]
                rrow = mk.F32P[0:1, 1024 + h * 512:1024 + (h + 1) * 512]
                rb = mk.F32P[0:64, 2048:2560]
                stg = mk.TOKB[0:64, 1024 + h * 512:1024 + (h + 1) * 512]
                ps_bc, pk_bc = mk.bank(2 * h + 1)

                def emit():
                    p.v(lambda e, avs=avs, rrow=rrow: e.reciprocal(
                        rrow, avs[64:65, :]),
                        reads=[f"l0.avs{h}"], writes=[f"l0.rrow{h}"],
                        tag=u("l0rec"))
                    p.t(lambda e, ps_bc=ps_bc, rrow=rrow: e.matmul(
                        ps_bc[0:64, :], mk.ONESF[0:1, 0:64], rrow,
                        start=True, stop=True),
                        reads=[f"l0.rrow{h}", "ONESF"], writes=[pk_bc],
                        tag=u("l0bc"))
                    p.v(lambda e, ps_bc=ps_bc, rb=rb: e.tensor_copy(rb, ps_bc[0:64, :]),
                        reads=[pk_bc], writes=["l0.rb"], tag=u("l0rbev"))
                    p.v(lambda e, avs=avs, rb=rb, stg=stg: e.tensor_tensor(
                        stg, avs[0:64, :], rb, op=OP.mult),
                        reads=[f"l0.avs{h}", "l0.rb"], writes=[f"l0.ostg{h}"],
                        tag=u("l0onorm"))
                    drow = (2 * h2 + h) * 64
                    p.dma("scalar", lambda e, stg=stg, drow=drow, qb=qb: e.dma_start(
                        out=attnT_dram[drow:drow + 64, qb:qb + 512], in_=stg),
                        reads=[f"l0.ostg{h}"], writes=["l0.attnT"], tag=u("l0ost"))
                return emit
            pend.append(mk_out(0))
            pend.append(mk_out(1))
        while pend:
            pend.pop(0)()
    p.barrier()

    wo, k_wo = mk.load_weight_fm("L0_attn_Wo_w", dst=mk.WT, dstoff=0, key="l0.wo")
    for half in range(2):
        mk.bias_row("L0_attn_Wo_b", half * 512, 512, key="ROWS", tag=u("l0bo"))
        for t in range(PT):
            sl = t % 2
            atile = mk.M[:, sl * 1024:(sl + 1) * 1024]
            ak = f"l0.at{sl}"
            sap = attnT_dram[:, t * 128:(t + 1) * 128].rearrange("(j r) c -> r j c", r=128)
            at3 = atile.rearrange("p (j c) -> p j c", j=8)
            p.dma("sync", lambda e, at3=at3, sap=sap: e.dma_start(out=at3, in_=sap),
                  reads=["l0.attnT"], writes=[ak], tag=u("l0wold"))
            xt = mk.F32P[:, sl * 1024:sl * 1024 + 512]
            xk = f"l0.wx{sl}"
            p.dma("sync", lambda e, xt=xt, t=t, half=half: e.dma_start(
                out=xt, in_=xin[t * 128:(t + 1) * 128, half * 512:(half + 1) * 512]),
                reads=[xin_key], writes=[xk], tag=u("l0wox"))
            ps, pkb = mk.bank(2 * sl)
            p.t(lambda e, ps=ps: e.matmul(
                ps, mk.ONESF[:, 0:128], mk.ROWS[0:1, 0:512],
                start=True, stop=False),
                reads=["ROWS", "ONESF"], writes=[pkb], tag=u("l0wob"))
            for kp in range(8):
                p.t(lambda e, ps=ps, kp=kp, atile=atile, half=half, wo=wo: e.matmul(
                    ps, atile[:, kp * 128:(kp + 1) * 128], wo.sl(kp, half * 512, 512),
                    start=False, stop=(kp == 7)),
                    reads=[ak, "l0.wo"], writes=[pkb], tag=u("l0womm"))
            ot = mk.F32P[:, 2048:2560]
            ok = "l0.wo_ot"
            p.v(lambda e, ps=ps, xt=xt, ot=ot: e.tensor_tensor(
                ot, ps, xt, op=OP.add),
                reads=[pkb, xk], writes=[ok], tag=u("l0woadd"))
            p.dma("scalar", lambda e, ot=ot, t=t, half=half: e.dma_start(
                out=xout[t * 128:(t + 1) * 128, half * 512:(half + 1) * 512], in_=ot),
                reads=[ok], writes=[xout_key], tag=u("l0wost"))
        p.barrier()

WEIGHT_SPECS = [
    ("L0_attn_Wd_q_w", (1024, 512)), ("L0_attn_Wd_q_b", (512,)),
    ("L0_attn_Wu_q_w", (512, 1024)), ("L0_attn_Wu_q_b", (1024,)),
    ("L0_attn_Wq_r_w", (512, 1024)), ("L0_attn_Wq_r_b", (1024,)),
    ("L0_attn_Wk_r_w", (1024, 1024)), ("L0_attn_Wk_r_b", (1024,)),
    ("L0_attn_Wd_kv_w", (1024, 512)), ("L0_attn_Wd_kv_b", (512,)),
    ("L0_attn_Wu_k_w", (512, 1024)), ("L0_attn_Wu_k_b", (1024,)),
    ("L0_attn_Wu_v_w", (512, 1024)), ("L0_attn_Wu_v_b", (1024,)),
    ("L0_attn_Wo_w", (1024, 1024)), ("L0_attn_Wo_b", (1024,)),
    ("L0_ffn_w1", (4096, 1024)), ("L0_ffn_b1", (4096,)),
    ("L0_ffn_w2", (1024, 4096)), ("L0_ffn_b2", (1024,)),
    ("L1_attn_Wq", (1024, 1024)), ("L1_attn_bq", (1024,)),
    ("L1_attn_Wk", (1024, 1024)), ("L1_attn_bk", (1024,)),
    ("L1_attn_Wv", (1024, 1024)), ("L1_attn_bv", (1024,)),
    ("L1_attn_Wo", (1024, 1024)), ("L1_attn_bo", (1024,)),
    ("L1_attn_W1", (1024, 4096)), ("L1_attn_b1", (4096,)),
    ("L1_attn_W2", (4096, 1024)), ("L1_attn_b2", (1024,)),
    ("L1_attn_g1", (1024,)), ("L1_attn_be1", (1024,)),
    ("L1_attn_g2", (1024,)), ("L1_attn_be2", (1024,)),
    ("L1_ffn_router_w", (8, 1024)), ("L1_ffn_router_bias", (8,)),
    ("L1_ffn_sw1", (1024, 2048)), ("L1_ffn_sw3", (1024, 2048)),
    ("L1_ffn_sw2", (2048, 1024)),
    ("L1_ffn_ew1", (8, 1024, 1024)), ("L1_ffn_ew2", (8, 1024, 1024)),
    ("L2_attn_Wq", (1024, 1024)), ("L2_attn_bq", (1024,)),
    ("L2_attn_Wk", (1024, 1024)), ("L2_attn_bk", (1024,)),
    ("L2_attn_Wv", (1024, 1024)), ("L2_attn_bv", (1024,)),
    ("L2_attn_Wo", (1024, 1024)), ("L2_attn_bo", (1024,)),
    ("L2_ffn_gamma", (1024,)), ("L2_ffn_beta", (1024,)),
    ("L2_ffn_Wr", (1024, 8)), ("L2_ffn_br", (8,)),
    ("L2_ffn_W1", (8, 1024, 4096)), ("L2_ffn_b1", (8, 4096)),
    ("L2_ffn_W2", (8, 4096, 1024)), ("L2_ffn_b2", (8, 1024)),
    ("L3_attn_Wk", (1024, 1024)), ("L3_attn_bk", (1024,)),
    ("L3_attn_Wv", (1024, 1024)), ("L3_attn_bv", (1024,)),
    ("L3_attn_Wq", (1024, 1024)), ("L3_attn_bq", (1024,)),
    ("L3_attn_Wkp", (1024, 1024)), ("L3_attn_bkp", (1024,)),
    ("L3_attn_Wqp", (1024, 1024)), ("L3_attn_bqp", (1024,)),
    ("L3_attn_phase_scale", (1024,)), ("L3_attn_ln_g", (1024,)),
    ("L3_attn_ln_b", (1024,)),
    ("L3_attn_Wo", (1024, 1024)), ("L3_attn_bo", (1024,)),
    ("L3_ffn_Wg", (8, 1024)),
    ("L3_ffn_W1", (8, 2048, 1024)), ("L3_ffn_W2", (8, 2048, 1024)),
    ("L3_ffn_W3", (8, 1024, 2048)),
]

ALL_PHASES = ("l0a", "l0f", "l1a", "l1f", "l2a", "l2f", "l3a", "l3f")


def build_program(T, phases=ALL_PHASES, debug=False):
    import contextlib
    nc = bass.Bass("TRN2", target_bir_lowering=False)
    p = Prog()
    mk = MK(nc, p, T, None)
    mk.din("x", [T, D])
    for name, shape in WEIGHT_SPECS:
        mk.din(name, shape)
    consts = mk.din("consts", [256, 128])
    mk.din("rope_cos", [T, 512])
    mk.din("rope_sin", [T, 512])
    out = mk.dout("out", [T, D])
    mk.dscratch("l0_ktr", [D, T], BF16)
    mk.dscratch("l0_bw1", [4096, 1024], BF16)
    mk.dscratch("l0_bw2", [1024, 4096], BF16)
    mk.dscratch("l1_src", [T, D], F32)
    mk.dscratch("l1_x1", [T, D], F32)
    mk.dscratch("l1_acc", [T, D], F32)
    mk.dscratch("l1f_acc", [T, D], F32)
    mk.dscratch("l1f_gsh", [2048, T], BF16)
    mk.dscratch("l2_xblk", [T, D], F32)
    mk.dscratch("l2_c", [8, 1024], BF16)
    mk.dscratch("l3_wt", [6, 1024, 1024], BF16)
    mk.dscratch("l3_bo", [1, 1024], F32)
    mk.dscratch("l3f_w1", [8, 2048, 1024], BF16)
    mk.dscratch("l3f_w2", [8, 2048, 1024], BF16)
    mk.dscratch("l3f_w3", [8, 1024, 2048], BF16)
    mk.dscratch("l1f_ew1b", [8, 1024, 1024], BF16)
    mk.dscratch("l1f_ew2b", [8, 1024, 1024], BF16)
    mk.dscratch("l2f_w1b", [8, 1024, 4096], BF16)
    mk.dscratch("l2f_w2b", [8, 4096, 1024], BF16)
    if debug:
        mk.DEBUG_L0 = True
        mk.DEBUG_L2F = True
        mk.dscratch("dbg_w", [128, 256], F32)
        mk.dscratch("dbg_s", [128, 256], F32)
        mk.dscratch("dbg_lg", [128, 256], F32)
        mk.dscratch("dbg_rb", [128, 8], F32)
        mk.dscratch("dbg_bc", [128, 16], F32)
        mk.dscratch("dbg_brow", [1, 8], F32)
        mk.dscratch("dbg_qT", [1024, T], BF16)
        mk.dscratch("dbg_kT", [1024, T], BF16)
        mk.dscratch("dbg_VP", [1024, 33 * 130], BF16)
    mk.dscratch("l0_attnT", [D, T], BF16)
    xr = [mk.dscratch(f"xr{i}", [T, D]) for i in range(7)]
    with contextlib.ExitStack() as stack:
        mk.alloc_pools(stack)
        mk.load_consts(consts)
        cur, cur_key = mk.dram["x"], "x"
        chain = []
        fns = {"l0a": ph_l0_attn, "l0f": ph_l0_ffn, "l1a": ph_l1_attn, "l1f": ph_l1_ffn, "l2a": ph_l2_attn, "l2f": ph_l2_ffn, "l3a": ph_l3_attn, "l3f": ph_l3_ffn}
        for i, ph in enumerate(phases):
            last = (i == len(phases) - 1)
            dst = out if last else xr[i]
            dst_key = "out" if last else f"xr{i}"
            fns[ph](mk, cur, cur_key, dst, dst_key)
            cur, cur_key = dst, dst_key
        p.seal()
        p.emit(nc)
    return nc, mk, p


def make_consts(T):
    c = np.zeros((256, 128), np.float32)
    c[0:128, :] = np.eye(128, dtype=np.float32)
    lt = np.triu(np.ones((64, 64), np.float32))
    c[128:192, 0:64] = lt
    c[192:256, 64:128] = lt
    return c


def make_rope(T):
    ids = np.arange(256, dtype=np.float32)
    theta = np.float32(10000.0) ** (np.float32(-2.0) * ids[None, :512 // 2][0]) \
        if False else None
    i = np.arange(512, dtype=np.float32)
    theta = (np.float32(10000.0) ** (np.float32(-2.0) * i)) / np.float32(512.0)
    pos = np.arange(T, dtype=np.float32)[:, None]
    r = pos * theta[None, :]
    return np.cos(r).astype(np.float32), np.sin(r).astype(np.float32)


def kernel(**inputs):
    T = 4096
    x = np.ascontiguousarray(inputs["L0_attn_x"], dtype=np.float32)
    B = x.shape[0]
    nc, mk, p = build_program(T)
    cos, sin = make_rope(T)
    consts = make_consts(T)
    base = {}
    for name, shape in WEIGHT_SPECS:
        base[name] = np.ascontiguousarray(inputs[name], dtype=np.float32)
    base["consts"] = consts
    base["rope_cos"] = cos
    base["rope_sin"] = sin
    in_maps = [dict(base, x=np.ascontiguousarray(x[i])) for i in range(B)]
    res = run_bass_kernel_spmd(nc, in_maps, core_ids=list(range(B)))
    return np.stack([r["out"] for r in res.results], axis=0)


def ph_l0_ffn(mk, xin, xin_key, xout, xout_key):
    p, T, PT = mk.p, mk.T, mk.PT
    u = mk.u
    bw1_dram = mk.dram["l0_bw1"]
    bw2_dram = mk.dram["l0_bw2"]
    w1, w2 = mk.dram["L0_ffn_w1"], mk.dram["L0_ffn_w2"]
    mk._bcol_cursor = 0
    b1c, k_b1 = mk.bias_col("L0_ffn_b1")
    for rt in range(32):
        sl = rt % 2
        wt = mk.F32P[:, sl * 1024:(sl + 1) * 1024]
        sg = mk.TOKB[:, sl * 1024:(sl + 1) * 1024]
        st = mk.STAT[:, sl:sl + 1]
        kw, ks, kst, kb = f"f.wt{sl}", f"f.sg{sl}", f"f.st{sl}", f"f.bwt{sl}"
        p.dma("sync", lambda e, rt=rt, wt=wt: e.dma_start(
            out=wt, in_=w1[rt * 128:(rt + 1) * 128, :]),
            writes=[kw], tag=u("f.w1ld"))
        p.v(lambda e, wt=wt, st=st: e.tensor_reduce(
            st, wt, axis=AX.X, op=OP.add, apply_absolute_value=True),
            reads=[kw], writes=[kst], tag=u("f.abs"))
        p.s(lambda e, wt=wt, sg=sg: e.activation(sg, wt, AF.Sign),
            reads=[kw], writes=[ks], tag=u("f.sign"))
        bwt = mk.M[:, sl * 1024:(sl + 1) * 1024]
        p.v(lambda e, sg=sg, st=st, bwt=bwt: e.tensor_scalar(
            bwt, sg, st, 1.0 / 1024.0, op0=OP.mult, op1=OP.mult),
            reads=[ks, kst], writes=[kb], tag=u("f.bw"))
        p.dma("scalar", lambda e, rt=rt, bwt=bwt: e.dma_start(
            out=bw1_dram[rt * 128:(rt + 1) * 128, :], in_=bwt),
            reads=[kb], writes=["l0.bw1d"], tag=u("f.w1st"))
    for rt in range(8):
        st4 = mk.STAT[:, 8:12]
        for c in range(4):
            sl = c % 2
            wt = mk.F32P[:, sl * 1024:(sl + 1) * 1024]
            kw = f"f.wt{sl}"
            p.dma("sync", lambda e, rt=rt, c=c, wt=wt: e.dma_start(
                out=wt, in_=w2[rt * 128:(rt + 1) * 128, c * 1024:(c + 1) * 1024]),
                writes=[kw], tag=u("f.w2ld"))
            p.v(lambda e, wt=wt, st4=st4, c=c: e.tensor_reduce(
                st4[:, c:c + 1], wt, axis=AX.X, op=OP.add, apply_absolute_value=True),
                reads=[kw], writes=["f.st4"], tag=u("f.abs2"))
        st = mk.STAT[:, 12:13]
        p.v(lambda e, st4=st4, st=st: e.tensor_reduce(
            st, st4, axis=AX.X, op=OP.add),
            reads=["f.st4"], writes=["f.st"], tag=u("f.asum"))
        for c in range(4):
            sl = c % 2
            wt = mk.F32P[:, sl * 1024:(sl + 1) * 1024]
            sg = mk.TOKB[:, sl * 1024:(sl + 1) * 1024]
            bwt = mk.M[:, sl * 1024:(sl + 1) * 1024]
            kw, ks, kb = f"f.wt{sl}", f"f.sg{sl}", f"f.bwt{sl}"
            p.dma("sync", lambda e, rt=rt, c=c, wt=wt: e.dma_start(
                out=wt, in_=w2[rt * 128:(rt + 1) * 128, c * 1024:(c + 1) * 1024]),
                writes=[kw], tag=u("f.w2ld2"))
            p.s(lambda e, wt=wt, sg=sg: e.activation(sg, wt, AF.Sign),
                reads=[kw], writes=[ks], tag=u("f.sign2"))
            p.v(lambda e, sg=sg, st=st, bwt=bwt: e.tensor_scalar(
                bwt, sg, st, 1.0 / 4096.0, op0=OP.mult, op1=OP.mult),
                reads=[ks, "f.st"], writes=[kb], tag=u("f.bw2"))
            p.dma("scalar", lambda e, rt=rt, c=c, bwt=bwt: e.dma_start(
                out=bw2_dram[rt * 128:(rt + 1) * 128, c * 1024:(c + 1) * 1024], in_=bwt),
                reads=[kb], writes=["l0.bw2d"], tag=u("f.w2st"))
    p.barrier()
    bw1T = FMView(mk.A, 0, 8, 4096)
    for ks in range(8):
        p.dma("sync", lambda e, ks=ks: e.dma_start(
            out=mk.A[:, ks * 4096:(ks + 1) * 4096],
            in_=bw1_dram[:, ks * 128:(ks + 1) * 128], transpose=True),
            reads=["l0.bw1d"], writes=["f.bw1T"], tag=u("f.tr1"))
    bw2T = FMView(mk.B, 0, 32, 1024)
    for hs in range(32):
        p.dma("sync", lambda e, hs=hs: e.dma_start(
            out=mk.B[:, hs * 1024:(hs + 1) * 1024],
            in_=bw2_dram[:, hs * 128:(hs + 1) * 128], transpose=True),
            reads=["l0.bw2d"], writes=["f.bw2T"], tag=u("f.tr2"))
    p.barrier()
    NB = T // 256
    for blk in range(NB):
        sl = blk % 2
        lnb = FMView(mk.M, sl * 2048, 8, 256)
        lnk = f"f.lnb{sl}"
        mk.ln_transpose(xin_key, xin, lnb, lnk, prefix="fln",
                        tiles=(blk * 2, blk * 2 + 2), xt_db_ok=False)
        gT = FMView(mk.M, 4096, 32, 256)
        for m in range(32):
            ps, pkb = mk.bank(m % 2)
            for kp in range(8):
                p.t(lambda e, ps=ps, m=m, kp=kp, lnb=lnb: e.matmul(
                    ps[:, 0:256], bw1T.sl(kp, m * 128, 128), lnb.sl(kp, 0, 256),
                    start=(kp == 0), stop=(kp == 7)),
                    reads=["f.bw1T", lnk], writes=[pkb], tag=u("f.hmm"))
            p.s(lambda e, ps=ps, m=m: e.activation(
                gT.sl(m, 0, 256), ps[:, 0:256], AF.Gelu, bias=b1c[:, m:m + 1]),
                reads=[pkb, k_b1], writes=["f.gT"], tag=u("f.gelu"))
        for tl in range(2):
            t = blk * 2 + tl
            xt = mk.F32P[:, 1024:2048]
            xk = "f.xt"
            p.dma("sync", lambda e, xt=xt, t=t: e.dma_start(
                out=xt, in_=xin[t * 128:(t + 1) * 128, :]),
                reads=[xin_key], writes=[xk], tag=u("f.xld"))
            for half in range(2):
                mk.bias_row("L0_ffn_b2", half * 512, 512, key="ROWS", tag=u("f.b2"))
                ps, pkb = mk.bank(2 + half)
                p.t(lambda e, ps=ps: e.matmul(
                    ps, mk.ONESF[:, 0:128], mk.ROWS[0:1, 0:512], start=True, stop=False),
                    reads=["ROWS", "ONESF"], writes=[pkb], tag=u("f.ob"))
                for hp in range(32):
                    p.t(lambda e, ps=ps, hp=hp, tl=tl, half=half: e.matmul(
                        ps, gT.sl(hp, tl * 128, 128), bw2T.sl(hp, half * 512, 512),
                        start=False, stop=(hp == 31)),
                        reads=["f.gT", "f.bw2T"], writes=[pkb], tag=u("f.omm"))
                ot = mk.F32P[:, 2048 + 0:2048 + 512]
                ok = "f.ot"
                p.v(lambda e, ps=ps, xt=xt, ot=ot, half=half: e.tensor_tensor(
                    ot, ps, xt[:, half * 512:(half + 1) * 512], op=OP.add),
                    reads=[pkb, xk], writes=[ok], tag=u("f.oadd"))
                p.dma("scalar", lambda e, ot=ot, t=t, half=half: e.dma_start(
                    out=xout[t * 128:(t + 1) * 128, half * 512:(half + 1) * 512], in_=ot),
                    reads=[ok], writes=[xout_key], tag=u("f.ost"))
    p.barrier()


def ph_l1_attn(mk, xin, xin_key, xout, xout_key):
    p, T, PT = mk.p, mk.T, mk.PT
    u = mk.u
    for e_ in range(8):
        for srcn, dstn in (("L1_ffn_ew1", "l1f_ew1b"), ("L1_ffn_ew2", "l1f_ew2b"),
                           ("L2_ffn_W1", "l2f_w1b"), ("L2_ffn_W2", "l2f_w2b")):
            p.dma("gpsimd", lambda e, srcn=srcn, dstn=dstn, e_=e_: e.dma_start(
                out=mk.dram[dstn][e_], in_=mk.dram[srcn][e_]),
                writes=[f"cast.{dstn}"], tag=u("precast"))
    lnT = FMView(mk.A, 0, 8, T)
    src_dram = mk.dram["l1_src"]
    x1_dram = mk.dram["l1_x1"]
    acc_dram = mk.dram["l1_acc"]
    mk._bcol_cursor = 0
    bqc, k_bq = mk.bias_col("L1_attn_bq")
    mk.ln_transpose(xin_key, xin, lnT, "l1.lnT", prefix="l1ln",
                    tok_f32=(src_dram, "l1.src"))
    p.barrier()
    M = mk.M
    qT = M[:, 0:T]
    KP_ = M[:, T:2 * T]
    VP = M[:, 2 * T:2 * T + PT * 130]
    oT = FMView(mk.B, 0, 8, T)
    p.v(lambda e: e.memset(VP[:, :], 1.0), writes=["l1.VP"], tag=u("l1vones"))
    EPS = 1e-6
    for h2 in range(8):
        wq, k_wq = mk.load_weight_fm("L1_attn_Wq", dst=mk.WT, dstoff=0,
                                     colslice=(h2 * 128, 128), key="l1.wq")
        wk, k_wk = mk.load_weight_fm("L1_attn_Wk", dst=mk.WT, dstoff=1024,
                                     colslice=(h2 * 128, 128), key="l1.wk")
        wv, k_wv = mk.load_weight_fm("L1_attn_Wv", dst=mk.WT, dstoff=2048,
                                     colslice=(h2 * 128, 128), key="l1.wv")
        p.dma("sync", lambda e, h2=h2: e.dma_start(
            out=mk.ROWS[0:1, 128:256],
            in_=mk.dram["L1_attn_bv"][h2 * 128:(h2 + 1) * 128].rearrange(
                "(one n) -> one n", one=1)),
            writes=["ROWS2"], tag=u("l1bv"))
        for tb in range(0, T, 512):
            ps, pkb = mk.bank(tb // 512 % 2)
            for kp in range(8):
                p.t(lambda e, ps=ps, kp=kp, tb=tb, wq=wq: e.matmul(
                    ps, wq.sl(kp, 0, 128), lnT.sl(kp, tb, 512),
                    start=(kp == 0), stop=(kp == 7)),
                    reads=[k_wq] + [f"l1.lnT.t{tb // 128 + i}" for i in range(4)], writes=[pkb], tag=u("l1q"))
            par = tb // 512 % 2
            tmin = mk.F32P[:, par * 1024:par * 1024 + 512]
            texp = mk.F32P[:, par * 1024 + 512:par * 1024 + 1024]
            km, ke = f"l1.tmin{par}", f"l1.texp{par}"
            p.v(lambda e, ps=ps, tmin=tmin, h2=h2: e.tensor_scalar(
                tmin, ps, bqc[:, h2:h2 + 1], 0.0, op0=OP.add, op1=OP.min),
                reads=[pkb, k_bq], writes=[km], tag=u("l1qa"))
            p.s(lambda e, tmin=tmin, texp=texp: e.activation(texp, tmin, AF.Exp),
                reads=[km], writes=[ke], tag=u("l1qe"))
            p.v(lambda e, ps=ps, tb=tb, h2=h2: e.tensor_scalar(
                qT[:, tb:tb + 512], ps, bqc[:, h2:h2 + 1], 0.0,
                op0=OP.add, op1=OP.max),
                reads=[pkb, k_bq], writes=["l1.qT"], tag=u("l1qr"))
            p.v(lambda e, texp=texp, tb=tb: e.tensor_tensor(
                qT[:, tb:tb + 512], qT[:, tb:tb + 512], texp, op=OP.add),
                reads=[ke, "l1.qT"], writes=["l1.qT"], tag=u("l1qs"))
        mk.bias_row("L1_attn_bk", h2 * 128, 128, key="ROWS", tag=u("l1kbr"))
        for t in range(PT):
            ps, pkx = mk.bank(t % 4)
            p.t(lambda e, ps=ps: e.matmul(
                ps[:, 0:128], mk.ONESF[:, 0:128], mk.ROWS[0:1, 0:128],
                start=True, stop=False),
                reads=["ROWS", "ONESF"], writes=[pkx], tag=u("l1kb"))
            for kp in range(8):
                p.t(lambda e, ps=ps, kp=kp, t=t, wk=wk: e.matmul(
                    ps[:, 0:128], lnT.sl(kp, t * 128, 128), wk.sl(kp, 0, 128),
                    start=False, stop=(kp == 7)),
                    reads=[k_wk, f"l1.lnT.t{t}"], writes=[pkx], tag=u("l1kmm"))
            par = t % 2
            tmin = mk.F32P[:, par * 1024:par * 1024 + 128]
            texp = mk.F32P[:, par * 1024 + 512:par * 1024 + 640]
            km, ke = f"l1.tmin{par}", f"l1.texp{par}"
            p.v(lambda e, ps=ps, tmin=tmin: e.tensor_scalar(
                tmin, ps[:, 0:128], 0.0, None, op0=OP.min),
                reads=[pkx], writes=[km], tag=u("l1ka"))
            p.s(lambda e, tmin=tmin, texp=texp: e.activation(texp, tmin, AF.Exp),
                reads=[km], writes=[ke], tag=u("l1ke"))
            p.v(lambda e, ps=ps, t=t: e.tensor_scalar(
                KP_[:, t * 128:(t + 1) * 128], ps[:, 0:128], 0.0, None, op0=OP.max),
                reads=[pkx], writes=["l1.KP"], tag=u("l1kr"))
            p.v(lambda e, texp=texp, t=t: e.tensor_tensor(
                KP_[:, t * 128:(t + 1) * 128], KP_[:, t * 128:(t + 1) * 128],
                texp, op=OP.add),
                reads=[ke, "l1.KP"], writes=["l1.KP"], tag=u("l1ks"))
        for t in range(PT):
            ps, pkx = mk.bank(4 + t % 2)
            p.t(lambda e, ps=ps: e.matmul(
                ps[:, 0:128], mk.ONESF[:, 0:128], mk.ROWS[0:1, 128:256],
                start=True, stop=False),
                reads=["ROWS2", "ONESF"], writes=[pkx], tag=u("l1vb"))
            for kp in range(8):
                p.t(lambda e, ps=ps, kp=kp, t=t, wv=wv: e.matmul(
                    ps[:, 0:128], lnT.sl(kp, t * 128, 128), wv.sl(kp, 0, 128),
                    start=False, stop=(kp == 7)),
                    reads=[k_wv, f"l1.lnT.t{t}"], writes=[pkx], tag=u("l1vmm"))
            dap = VP[:, t * 130:(t + 1) * 130].rearrange(
                "p (two c) -> p two c", two=2)[:, :, 0:64]
            sap = ps[:, 0:128].rearrange("p (two c) -> p two c", two=2)
            p.v(lambda e, dap=dap, sap=sap: e.tensor_copy(dap, sap),
                reads=[pkx], writes=["l1.VP"], tag=u("l1vev"))
        kvb = mk.TOKB[:, 0:128]
        for h in range(2):
            ps_kv, pkv = mk.bank(h)
            for t in range(PT):
                p.t(lambda e, ps_kv=ps_kv, t=t, h=h: e.matmul(
                    ps_kv[0:64, 0:65],
                    KP_[:, t * 128 + h * 64:t * 128 + (h + 1) * 64],
                    VP[:, t * 130 + h * 65:t * 130 + (h + 1) * 65],
                    start=(t == 0), stop=(t == PT - 1)),
                    reads=["l1.KP", "l1.VP"], writes=[pkv], tag=u("l1kv"))
            p.v(lambda e, ps_kv=ps_kv, kvb=kvb, h=h: e.tensor_copy(
                kvb[h * 64:(h + 1) * 64, 0:65], ps_kv[0:64, 0:65]),
                reads=[pkv], writes=["l1.kvb"], tag=u("l1kve"))
        for t in range(PT):
            zc = mk.STAT[:, 0:2]
            stg = mk.TOKB[:, 1024:1152]
            for h in range(2):
                ps_o, pko = mk.bank(5 - h)
                p.t(lambda e, ps_o=ps_o, t=t, h=h, kvb=kvb: e.matmul(
                    ps_o[:, 0:65],
                    qT[h * 64:(h + 1) * 64, t * 128:(t + 1) * 128],
                    kvb[h * 64:(h + 1) * 64, 0:65],
                    start=True, stop=True),
                    reads=["l1.qT", "l1.kvb"], writes=[pko], tag=u("l1o"))
                p.v(lambda e, ps_o=ps_o, zc=zc, h=h: e.tensor_scalar(
                    zc[:, h:h + 1], ps_o[:, 64:65], EPS, None, op0=OP.add),
                    reads=[pko], writes=["l1.zc"], tag=u("l1za"))
                p.v(lambda e, zc=zc, h=h: e.reciprocal(zc[:, h:h + 1], zc[:, h:h + 1]),
                    reads=["l1.zc"], writes=["l1.zc"], tag=u("l1zr"))
                p.s(lambda e, ps_o=ps_o, stg=stg, zc=zc, h=h: e.activation(
                    stg[:, h * 64:(h + 1) * 64], ps_o[:, 0:64],
                    AF.Identity, scale=zc[:, h:h + 1]),
                    reads=[pko, "l1.zc"], writes=["l1.ostg"], tag=u("l1on"))
            pbf = mk.PBF[t % 2]
            pbk = f"psBF{t % 2}"
            p.t(lambda e, stg=stg, pbf=pbf: e.transpose(
                pbf[:, 0:128], stg, mk.IDB[:, :]),
                reads=["l1.ostg", "IDB"], writes=[pbk], tag=u("l1otr"))
            p.v(lambda e, pbf=pbf, t=t, h2=h2: e.tensor_copy(
                oT.sl(h2, t * 128, 128), pbf[:, 0:128]),
                reads=[pbk], writes=["l1.oT"], tag=u("l1oev"))
    p.barrier()

    wo, k_wo = mk.load_weight_fm("L1_attn_Wo", dst=mk.WT, dstoff=0, key="l1.wo")
    x1T = FMView(mk.A, 0, 8, T)
    cv = mk.constval
    eps_c = cv(1e-5)
    for nm, banki, kk in (("L1_attn_g1", 1, "l1.g1b"), ("L1_attn_be1", 2, "l1.be1b")):
        pg, (pga, pgb) = mk.bank_pair(banki)
        for half in range(2):
            mk.bias_row(nm, half * 512, 512, key="ROWS", tag=u("l1g"))
            p.t(lambda e, pg=pg, half=half: e.matmul(
                pg[:, half * 512:(half + 1) * 512], mk.ONESF[:, 0:128],
                mk.ROWS[0:1, 0:512], start=True, stop=True),
                reads=["ROWS", "ONESF"], writes=[kk, pga if half == 0 else pgb],
                tag=u("l1gb"))
    g1b, _ = mk.bank_pair(1)
    be1b, _ = mk.bank_pair(2)
    for t in range(PT):
        ps, (pk0, pk1) = mk.bank_pair(0)
        z = mk.F32P[:, 0:1024]
        for half in range(2):
            mk.bias_row("L1_attn_bo", half * 512, 512, key="ROWS", tag=u("l1bo"))
            psl = ps[:, half * 512:(half + 1) * 512]
            pkx = pk0 if half == 0 else pk1
            p.t(lambda e, psl=psl: e.matmul(
                psl, mk.ONESF[:, 0:128], mk.ROWS[0:1, 0:512], start=True, stop=False),
                reads=["ROWS", "ONESF"], writes=[pkx], tag=u("l1wob"))
            for kp in range(8):
                p.t(lambda e, psl=psl, kp=kp, t=t, half=half, wo=wo: e.matmul(
                    psl, oT.sl(kp, t * 128, 128), wo.sl(kp, half * 512, 512),
                    start=False, stop=(kp == 7)),
                    reads=["l1.oT", "l1.wo"], writes=[pkx], tag=u("l1womm"))
        srct = mk.F32P[:, 1024:2048]
        p.dma("sync", lambda e, srct=srct, t=t: e.dma_start(
            out=srct, in_=src_dram[t * 128:(t + 1) * 128, :]),
            reads=["l1.src"], writes=["l1.srct"], tag=u("l1srcld"))
        p.v(lambda e, z=z, ps=ps, srct=srct: e.tensor_tensor(
            z[:, 0:512], ps[:, 0:512], srct[:, 0:512], op=OP.add),
            reads=[pk0, "l1.srct"], writes=["l1.z"], tag=u("l1z0"))
        p.v(lambda e, z=z, ps=ps, srct=srct: e.tensor_tensor(
            z[:, 512:1024], ps[:, 512:1024], srct[:, 512:1024], op=OP.add),
            reads=[pk1, "l1.srct"], writes=["l1.z"], tag=u("l1z1"))
        st = mk.STAT[:, 0:8]
        junk = mk.TOKB[:, 0:1024]
        p.v(lambda e, z=z, st=st: e.tensor_reduce(st[:, 0:1], z, axis=AX.X, op=OP.add),
            reads=["l1.z"], writes=["l1.st"], tag=u("l1s0"))
        p.v(lambda e, st=st: e.tensor_scalar(st[:, 1:2], st[:, 0:1], -1.0 / D, None, op0=OP.mult),
            reads=["l1.st"], writes=["l1.st"], tag=u("l1s1"))
        p.s(lambda e, z=z, st=st, junk=junk: e.activation(
            junk, z, AF.Square, bias=st[:, 1:2], accum_out=st[:, 2:3]),
            reads=["l1.z", "l1.st"], writes=["l1.junk", "l1.st"], tag=u("l1s2"))
        p.s(lambda e, st=st: e.activation(st[:, 3:4], st[:, 2:3], AF.Ln, scale=1.0 / D, bias=eps_c),
            reads=["l1.st", "cpool"], writes=["l1.st"], tag=u("l1s3"))
        p.s(lambda e, st=st: e.activation(st[:, 4:5], st[:, 3:4], AF.Exp, scale=-0.5),
            reads=["l1.st"], writes=["l1.st"], tag=u("l1s4"))
        p.v(lambda e, st=st: e.tensor_tensor(st[:, 5:6], st[:, 1:2], st[:, 4:5], op=OP.mult),
            reads=["l1.st"], writes=["l1.st"], tag=u("l1s5"))
        n = mk.F32P[:, 1024:2048]
        p.s(lambda e, z=z, st=st, n=n: e.activation(
            n, z, AF.Identity, scale=st[:, 4:5], bias=st[:, 5:6]),
            reads=["l1.z", "l1.st"], writes=["l1.n"], tag=u("l1s6"))
        p.v(lambda e, n=n, z=z, g1b=g1b: e.tensor_tensor(z, n, g1b[:, :], op=OP.mult),
            reads=["l1.n", "l1.g1b"], writes=["l1.z"], tag=u("l1a1"))
        p.v(lambda e, z=z, be1b=be1b: e.tensor_tensor(z, z, be1b[:, :], op=OP.add),
            reads=["l1.z", "l1.be1b"], writes=["l1.z"], tag=u("l1a2"))
        p.dma("scalar", lambda e, z=z, t=t: e.dma_start(
            out=x1_dram[t * 128:(t + 1) * 128, :], in_=z),
            reads=["l1.z"], writes=["l1.x1"], tag=u("l1x1st"))
        xb = mk.TOKB[:, 1024:2048]
        p.v(lambda e, z=z, xb=xb: e.tensor_copy(xb, z),
            reads=["l1.z"], writes=["l1.xb"], tag=u("l1cast"))
        pbf = mk.PBF[t % 2]
        pbk = f"psBF{t % 2}"
        for j in range(8):
            p.t(lambda e, xb=xb, pbf=pbf, j=j: e.transpose(
                pbf[:, j * 128:(j + 1) * 128], xb[:, j * 128:(j + 1) * 128],
                mk.IDB[:, :]),
                reads=["l1.xb", "IDB"], writes=[pbk], tag=u("l1xtr"))
        dap = mk.fm_tile_ap(x1T, t * 128, 128)
        sap = pbf[:, 0:1024].rearrange("p (j r) -> p j r", j=8)
        p.v(lambda e, sap=sap, dap=dap: e.tensor_copy(dap, sap),
            reads=[pbk], writes=["l1.x1T"], tag=u("l1xev"))
    p.barrier()

    mk._bcol_cursor = 0
    b1c, k_b1 = mk.bias_col("L1_attn_b1")
    for hh in range(2):
        w1h, k_w1 = mk.load_weight_fm("L1_attn_W1", dst=mk.B, dstoff=0,
                                      colslice=(hh * 2048, 2048), key="l1.w1h")
        w2h, k_w2 = mk.load_weight_fm("L1_attn_W2", dst=mk.B, dstoff=16384,
                                      rowslice=(hh * 2048, 2048), key="l1.w2h")
        gT = FMView(mk.M, 0, 16, 512)
        for tb in range(0, T, 512):
            for m in range(16):
                ps, pkb = mk.bank(m % 2)
                for kp in range(8):
                    p.t(lambda e, ps=ps, m=m, kp=kp, tb=tb, w1h=w1h: e.matmul(
                        ps, w1h.sl(kp, m * 128, 128), x1T.sl(kp, tb, 512),
                        start=(kp == 0), stop=(kp == 7)),
                        reads=[k_w1, "l1.x1T"], writes=[pkb], tag=u("l1hmm"))
                p.s(lambda e, ps=ps, m=m, hh=hh: e.activation(
                    gT.sl(m, 0, 512), ps, AF.Gelu, bias=b1c[:, hh * 16 + m:hh * 16 + m + 1]),
                    reads=[pkb, k_b1], writes=["l1.gT"], tag=u("l1gelu"))
            for tl in range(4):
                t = tb // 128 + tl
                for hd in range(2):
                    ps, pkb = mk.bank(4 + hd)
                    if hh == 0:
                        mk.bias_row("L1_attn_b2", hd * 512, 512, key="ROWS", tag=u("l1b2"))
                        p.t(lambda e, ps=ps: e.matmul(
                            ps, mk.ONESF[:, 0:128], mk.ROWS[0:1, 0:512],
                            start=True, stop=False),
                            reads=["ROWS", "ONESF"], writes=[pkb], tag=u("l1yb"))
                    for hp in range(16):
                        p.t(lambda e, ps=ps, hp=hp, tl=tl, hd=hd, w2h=w2h, hh=hh: e.matmul(
                            ps, gT.sl(hp, tl * 128, 128), w2h.sl(hp, hd * 512, 512),
                            start=(hh == 1 and hp == 0), stop=(hp == 15)),
                            reads=["l1.gT", k_w2], writes=[pkb], tag=u("l1ymm"))
                    stg = mk.F32P[:, 2048 + 0:2048 + 512]
                    p.v(lambda e, ps=ps, stg=stg: e.tensor_copy(stg, ps),
                        reads=[pkb], writes=["l1.ystg"], tag=u("l1yev"))
                    if hh == 0:
                        p.dma("scalar", lambda e, stg=stg, t=t, hd=hd: e.dma_start(
                            out=acc_dram[t * 128:(t + 1) * 128, hd * 512:(hd + 1) * 512],
                            in_=stg),
                            reads=["l1.ystg"], writes=[f"l1.acc_{t}_{hd}"], tag=u("l1yst"))
                    else:
                        p.dma("gpsimd", lambda e, stg=stg, t=t, hd=hd: e.dma_start(
                            out=acc_dram[t * 128:(t + 1) * 128, hd * 512:(hd + 1) * 512],
                            in_=stg, accum_op=OP.add),
                            reads=["l1.ystg"], writes=[f"l1.acc_{t}_{hd}"], tag=u("l1yac"))
        p.barrier()

    for nm, banki, kk in (("L1_attn_g2", 1, "l1.g2b"), ("L1_attn_be2", 2, "l1.be2b")):
        pg, (pga, pgb) = mk.bank_pair(banki)
        for half in range(2):
            mk.bias_row(nm, half * 512, 512, key="ROWS", tag=u("l1g2"))
            p.t(lambda e, pg=pg, half=half: e.matmul(
                pg[:, half * 512:(half + 1) * 512], mk.ONESF[:, 0:128],
                mk.ROWS[0:1, 0:512], start=True, stop=True),
                reads=["ROWS", "ONESF"], writes=[kk, pga if half == 0 else pgb],
                tag=u("l1g2b"))
    g2b, _ = mk.bank_pair(1)
    be2b, _ = mk.bank_pair(2)
    for t in range(PT):
        z = mk.F32P[:, 0:1024]
        n = mk.F32P[:, 1024:2048]
        st = mk.STAT[:, 0:8]
        junk = mk.TOKB[:, 0:1024]
        p.dma("sync", lambda e, z=z, t=t: e.dma_start(
            out=z, in_=x1_dram[t * 128:(t + 1) * 128, :]),
            reads=["l1.x1"], writes=["l1.z"], tag=u("l1eld1"))
        p.dma("sync", lambda e, n=n, t=t: e.dma_start(
            out=n, in_=acc_dram[t * 128:(t + 1) * 128, :]),
            reads=[f"l1.acc_{t}_0", f"l1.acc_{t}_1"], writes=["l1.n"], tag=u("l1eld2"))
        p.v(lambda e, z=z, n=n: e.tensor_tensor(z, z, n, op=OP.add),
            reads=["l1.z", "l1.n"], writes=["l1.z"], tag=u("l1ez"))
        p.v(lambda e, z=z, st=st: e.tensor_reduce(st[:, 0:1], z, axis=AX.X, op=OP.add),
            reads=["l1.z"], writes=["l1.st"], tag=u("l1e0"))
        p.v(lambda e, st=st: e.tensor_scalar(st[:, 1:2], st[:, 0:1], -1.0 / D, None, op0=OP.mult),
            reads=["l1.st"], writes=["l1.st"], tag=u("l1e1"))
        p.s(lambda e, z=z, st=st, junk=junk: e.activation(
            junk, z, AF.Square, bias=st[:, 1:2], accum_out=st[:, 2:3]),
            reads=["l1.z", "l1.st"], writes=["l1.junk", "l1.st"], tag=u("l1e2"))
        p.s(lambda e, st=st: e.activation(st[:, 3:4], st[:, 2:3], AF.Ln, scale=1.0 / D, bias=eps_c),
            reads=["l1.st", "cpool"], writes=["l1.st"], tag=u("l1e3"))
        p.s(lambda e, st=st: e.activation(st[:, 4:5], st[:, 3:4], AF.Exp, scale=-0.5),
            reads=["l1.st"], writes=["l1.st"], tag=u("l1e4"))
        p.v(lambda e, st=st: e.tensor_tensor(st[:, 5:6], st[:, 1:2], st[:, 4:5], op=OP.mult),
            reads=["l1.st"], writes=["l1.st"], tag=u("l1e5"))
        p.s(lambda e, z=z, st=st, n=n: e.activation(
            n, z, AF.Identity, scale=st[:, 4:5], bias=st[:, 5:6]),
            reads=["l1.z", "l1.st"], writes=["l1.n"], tag=u("l1e6"))
        p.v(lambda e, n=n, z=z, g2b=g2b: e.tensor_tensor(z, n, g2b[:, :], op=OP.mult),
            reads=["l1.n", "l1.g2b"], writes=["l1.z"], tag=u("l1e7"))
        p.v(lambda e, z=z, be2b=be2b: e.tensor_tensor(z, z, be2b[:, :], op=OP.add),
            reads=["l1.z", "l1.be2b"], writes=["l1.z"], tag=u("l1e8"))
        p.dma("sync", lambda e, n=n, t=t: e.dma_start(
            out=n, in_=xin[t * 128:(t + 1) * 128, :]),
            reads=[xin_key], writes=["l1.n"], tag=u("l1eld3"))
        p.v(lambda e, z=z, n=n: e.tensor_tensor(z, z, n, op=OP.add),
            reads=["l1.z", "l1.n"], writes=["l1.z"], tag=u("l1e9"))
        p.dma("scalar", lambda e, z=z, t=t: e.dma_start(
            out=xout[t * 128:(t + 1) * 128, :], in_=z),
            reads=["l1.z"], writes=[xout_key], tag=u("l1est"))
    p.barrier()


def ph_l1_ffn(mk, xin, xin_key, xout, xout_key):
    p, T, PT = mk.p, mk.T, mk.PT
    u = mk.u
    F = mk.F32P
    lnT = FMView(mk.A, 0, 8, T)
    acc_dram = mk.dram["l1f_acc"]
    gsh_dram = mk.dram["l1f_gsh"]
    NE8 = PT * 8
    rwT = F[:, 2048:2112]
    mk.load_rwT("L1_ffn_router_w", rwT, "lf.rwT", "lf")
    p.barrier()
    logits = F[:, 2112:2112 + NE8]
    hook = mk.router_tile_hook2("lf", rwT, "lf.rwT", logits, "lf.logits",
                                lnT, "lf.lnT")
    mk.ln_transpose(xin_key, xin, lnT, "lf.lnT", prefix="lfln", hook=hook)
    p.barrier()
    s_ = F[:, 0:NE8]
    sel = F[:, 256:256 + NE8]
    oh1 = F[:, 512:512 + NE8]
    oh2 = F[:, 768:768 + NE8]
    tmp = F[:, 1024:1024 + NE8]
    iotaB = F[:, 1280:1280 + NE8]
    comb = F[:, 1536:1536 + NE8]
    m1 = F[:, 1792:1792 + PT]
    m2 = F[:, 1824:1824 + PT]
    idx = F[:, 1856:1856 + PT]
    s1 = F[:, 1888:1888 + PT]
    s2 = F[:, 1920:1920 + PT]
    den = F[:, 1952:1952 + PT]
    inv = F[:, 1984:1984 + PT]
    msk = F[:, 2016:2016 + PT]
    p.g(lambda e: e.iota(iotaB, pattern=[[0, PT], [1, 8]], base=0,
                         channel_multiplier=0, allow_small_or_imprecise_dtypes=True),
        writes=["lf.iota"], tag=u("lf.iota"))
    p.s(lambda e: e.activation(s_, logits, AF.Sigmoid),
        reads=["lf.logits"], writes=["lf.s"], tag=u("lf.sig"))
    mk.bias_row("L1_ffn_router_bias", 0, 8, key="ROWS", tag=u("lf.rb"))
    psK, pkK = mk.bank(3)
    p.t(lambda e, psK=psK: e.matmul(
        psK[:, 0:8], mk.ONESF[:, 0:128], mk.ROWS[0:1, 0:8], start=True, stop=True),
        reads=["ROWS", "ONESF"], writes=[pkK], tag=u("lf.rbb"))
    rb = mk.STAT[:, 8:16]
    p.v(lambda e, psK=psK, rb=rb: e.tensor_copy(rb, psK[:, 0:8]),
        reads=[pkK], writes=["lf.rbb"], tag=u("lf.rbe"))
    mk.bTT(sel.rearrange("p (t e) -> p t e", t=PT),
           s_.rearrange("p (t e) -> p t e", t=PT),
           rb.rearrange("p (one e) -> p one e", one=1), OP.add,
           reads=["lf.s", "lf.rbb"], writes=["lf.sel"], tag=u("lf.selc"))
    mk.top2("lf", sel, "lf.sel", PT, dict(
        oh1=(oh1, "lf.oh1"), oh2=(oh2, "lf.oh2"), m1=(m1, "lf.m1"),
        m2=(m2, "lf.m2"), tmp=(tmp, "lf.tmp"), idx=(idx, "lf.idx"),
        iotaB=(iotaB, "lf.iota")))
    v3 = lambda ap: ap.rearrange("p (t e) -> p t e", t=PT)
    v2 = lambda ap: ap.rearrange("p (t one) -> p t one", one=1)
    p.v(lambda e: e.tensor_tensor(tmp, s_, oh1, op=OP.mult),
        reads=["lf.s", "lf.oh1"], writes=["lf.tmp"], tag=u("lf.g1m"))
    p.v(lambda e: e.tensor_reduce(v2(s1), v3(tmp), axis=AX.X, op=OP.add),
        reads=["lf.tmp"], writes=["lf.s1"], tag=u("lf.g1r"))
    p.v(lambda e: e.tensor_tensor(tmp, s_, oh2, op=OP.mult),
        reads=["lf.s", "lf.oh2"], writes=["lf.tmp"], tag=u("lf.g2m"))
    p.v(lambda e: e.tensor_reduce(v2(s2), v3(tmp), axis=AX.X, op=OP.add),
        reads=["lf.tmp"], writes=["lf.s2"], tag=u("lf.g2r"))
    p.v(lambda e: e.tensor_tensor(den, s1, s2, op=OP.add),
        reads=["lf.s1", "lf.s2"], writes=["lf.den"], tag=u("lf.den"))
    p.v(lambda e: e.tensor_scalar(inv, den, 1e-9, None, op0=OP.add),
        reads=["lf.den"], writes=["lf.inv"], tag=u("lf.inva"))
    p.v(lambda e: e.reciprocal(inv, inv),
        reads=["lf.inv"], writes=["lf.inv"], tag=u("lf.invr"))
    p.v(lambda e: e.tensor_scalar(msk, den, 1e-9, None, op0=OP.is_gt),
        reads=["lf.den"], writes=["lf.msk"], tag=u("lf.msk"))
    for sk_, skk in ((s1, "lf.s1"), (s2, "lf.s2")):
        p.v(lambda e, sk_=sk_: e.tensor_tensor(sk_, sk_, inv, op=OP.mult),
            reads=[skk, "lf.inv"], writes=[skk], tag=u("lf.gk1"))
        p.v(lambda e, sk_=sk_: e.tensor_scalar(sk_, sk_, 0.5, None, op0=OP.subtract),
            reads=[skk], writes=[skk], tag=u("lf.gk2"))
        p.v(lambda e, sk_=sk_: e.tensor_tensor(sk_, sk_, msk, op=OP.mult),
            reads=[skk, "lf.msk"], writes=[skk], tag=u("lf.gk3"))
        p.v(lambda e, sk_=sk_: e.tensor_scalar(sk_, sk_, 0.5, None, op0=OP.add),
            reads=[skk], writes=[skk], tag=u("lf.gk4"))
    mk.bTT(v3(tmp), v3(oh1), v2(s1), OP.mult,
           reads=["lf.oh1", "lf.s1"], writes=["lf.tmp"], tag=u("lf.c1"))
    mk.bTT(v3(comb), v3(oh2), v2(s2), OP.mult,
           reads=["lf.oh2", "lf.s2"], writes=["lf.comb"], tag=u("lf.c2"))
    p.v(lambda e: e.tensor_tensor(comb, comb, tmp, op=OP.add),
        reads=["lf.comb", "lf.tmp"], writes=["lf.comb"], tag=u("lf.c3"))
    p.barrier()
    sw1, k_sw1 = mk.load_weight_fm("L1_ffn_sw1", dst=mk.B, dstoff=0, key="lf.sw1")
    sw3, k_sw3 = mk.load_weight_fm("L1_ffn_sw3", dst=mk.B, dstoff=16384, key="lf.sw3")
    aT = FMView(mk.M, 0, 16, 512)
    bT = FMView(mk.M, 8192, 16, 512)
    NBLK = T // 512
    for blk3 in range(0, NBLK, 1):
        pass
    for tb in range(0, T, 512):
        for which, wv, wk, dstv, dstk, act in (
                (0, sw1, k_sw1, aT, "lf.aT", AF.Silu), (1, sw3, k_sw3, bT, "lf.bT", None)):
            for m in range(16):
                ps, pkb = mk.bank((m + which) % 2)
                for kp in range(8):
                    p.t(lambda e, ps=ps, m=m, kp=kp, tb=tb, wv=wv: e.matmul(
                        ps, wv.sl(kp, m * 128, 128), lnT.sl(kp, tb, 512),
                        start=(kp == 0), stop=(kp == 7)),
                        reads=[wk] + [f"lf.lnT.t{tb // 128 + i}" for i in range(4)], writes=[pkb], tag=u("lf.smm"))
                if act is None:
                    p.v(lambda e, ps=ps, m=m, dstv=dstv: e.tensor_copy(
                        dstv.sl(m, 0, 512), ps),
                        reads=[pkb], writes=[dstk], tag=u("lf.sev"))
                else:
                    p.s(lambda e, ps=ps, m=m, dstv=dstv, act=act: e.activation(
                        dstv.sl(m, 0, 512), ps, act),
                        reads=[pkb], writes=[dstk], tag=u("lf.sact"))
        for ch in range(2):
            p.v(lambda e, ch=ch: e.tensor_tensor(
                mk.M[:, ch * 4096:(ch + 1) * 4096], mk.M[:, ch * 4096:(ch + 1) * 4096],
                mk.M[:, 8192 + ch * 4096:8192 + (ch + 1) * 4096], op=OP.mult),
                reads=["lf.aT", "lf.bT"], writes=[f"lf.g{ch}", "lf.aT"], tag=u("lf.gmul"))
            dap = gsh_dram[ch * 1024:(ch + 1) * 1024,
                           tb:tb + 512].rearrange("(j r) c -> r j c", r=128)
            sap = mk.M[:, ch * 4096:(ch + 1) * 4096].rearrange("p (j c) -> p j c", j=8)
            p.dma("scalar", lambda e, dap=dap, sap=sap: e.dma_start(out=dap, in_=sap),
                  reads=[f"lf.g{ch}", "lf.aT"], writes=["lf.gsh"], tag=u("lf.gst"))
    p.barrier()
    sw2, k_sw2 = mk.load_weight_fm("L1_ffn_sw2", dst=mk.B, dstoff=0, key="lf.sw2")
    for t in range(PT):
        sl_ = t % 2
        gt = mk.M[:, sl_ * 2048:(sl_ + 1) * 2048]
        gk = f"lf.gt{sl_}"
        sap = gsh_dram[:, t * 128:(t + 1) * 128].rearrange("(j r) c -> r j c", r=128)
        g3 = gt.rearrange("p (j c) -> p j c", j=16)
        p.dma("sync", lambda e, g3=g3, sap=sap: e.dma_start(out=g3, in_=sap),
              reads=["lf.gsh"], writes=[gk], tag=u("lf.gld"))
        for half in range(2):
            ps, pkb = mk.bank(4 + half)
            for hp in range(16):
                p.t(lambda e, ps=ps, hp=hp, half=half, gt=gt, sw2=sw2: e.matmul(
                    ps, gt[:, hp * 128:(hp + 1) * 128], sw2.sl(hp, half * 512, 512),
                    start=(hp == 0), stop=(hp == 15)),
                    reads=[gk, k_sw2], writes=[pkb], tag=u("lf.somm"))
            pr = t % 2
            stg = (F[:, pr * 1024 + half * 512:pr * 1024 + (half + 1) * 512]
                   if pr == 0 else
                   (F[:, 1024:1536] if half == 0 else F[:, 2048:2560]))
            sk = f"lf.stg{pr}{half}"
            p.v(lambda e, ps=ps, stg=stg: e.tensor_copy(stg, ps),
                reads=[pkb], writes=[sk], tag=u("lf.soev"))
            p.dma("scalar", lambda e, stg=stg, t=t, half=half: e.dma_start(
                out=acc_dram[t * 128:(t + 1) * 128, half * 512:(half + 1) * 512],
                in_=stg),
                reads=[sk], writes=[f"lf.acc_{t}_{half}"], tag=u("lf.sost"))
    p.barrier()
    hT = FMView(mk.M, 0, 8, 512)
    for e_ in range(8):
        off = (e_ % 2) * 16384
        ew1, k_e1 = mk.load_weight_fm("l1f_ew1b", queue="sync", dst=mk.B, dstoff=off,
                                      key=f"lf.ew1_{e_ % 2}", expert=e_,
                                      src_key="cast.l1f_ew1b")
        ew2, k_e2 = mk.load_weight_fm("l1f_ew2b", queue="sync", dst=mk.B,
                                      dstoff=off + 8192,
                                      key=f"lf.ew2_{e_ % 2}", expert=e_,
                                      src_key="cast.l1f_ew2b")
        for tb in range(0, T, 512):
            for m in range(8):
                ps, pkb = mk.bank(m % 2)
                for kp in range(8):
                    p.t(lambda e, ps=ps, m=m, kp=kp, tb=tb, ew1=ew1: e.matmul(
                        ps, ew1.sl(kp, m * 128, 128), lnT.sl(kp, tb, 512),
                        start=(kp == 0), stop=(kp == 7)),
                        reads=[k_e1] + [f"lf.lnT.t{tb // 128 + i}" for i in range(4)], writes=[pkb], tag=u("lf.emm"))
                p.s(lambda e, ps=ps, m=m: e.activation(
                    hT.sl(m, 0, 512), ps, AF.Gelu),
                    reads=[pkb], writes=[f"lf.hT{m}"], tag=u("lf.egelu"))
            for tl in range(4):
                t = tb // 128 + tl
                for hp in range(8):
                    for half in range(2):
                        ps, pkb = mk.bank(4 + half)
                        p.t(lambda e, ps=ps, hp=hp, tl=tl, half=half, ew2=ew2: e.matmul(
                            ps, hT.sl(hp, tl * 128, 128), ew2.sl(hp, half * 512, 512),
                            start=(hp == 0), stop=(hp == 7)),
                            reads=[f"lf.hT{hp}", k_e2], writes=[pkb], tag=u("lf.eymm"))
                for half in range(2):
                    ps, pkb = mk.bank(4 + half)
                    pr = tl % 2
                    stg = (F[:, half * 512:(half + 1) * 512] if pr == 0 else
                           (F[:, 1024:1536] if half == 0 else F[:, 2048:2560]))
                    sk = f"lf.stg{pr}{half}"
                    p.v(lambda e, ps=ps, stg=stg, t=t, e_=e_: e.tensor_scalar(
                        stg, ps, comb[:, t * 8 + e_:t * 8 + e_ + 1], None, op0=OP.mult),
                        reads=[pkb, "lf.comb"], writes=[sk], tag=u("lf.escl"))
                    p.dma("gpsimd", lambda e, stg=stg, t=t, half=half: e.dma_start(
                        out=acc_dram[t * 128:(t + 1) * 128, half * 512:(half + 1) * 512],
                        in_=stg, accum_op=OP.add),
                        reads=[sk], writes=[f"lf.acc_{t}_{half}"], tag=u("lf.eac"))
    p.barrier()
    for t in range(PT):
        z = F[:, 0:1024]
        n = F[:, 1024:2048]
        p.dma("sync", lambda e, z=z, t=t: e.dma_start(
            out=z, in_=acc_dram[t * 128:(t + 1) * 128, :]),
            reads=[f"lf.acc_{t}_0", f"lf.acc_{t}_1"], writes=["lf.z"], tag=u("lf.fld1"))
        p.dma("sync", lambda e, n=n, t=t: e.dma_start(
            out=n, in_=xin[t * 128:(t + 1) * 128, :]),
            reads=[xin_key], writes=["lf.n"], tag=u("lf.fld2"))
        p.v(lambda e, z=z, n=n: e.tensor_tensor(z, z, n, op=OP.add),
            reads=["lf.z", "lf.n"], writes=["lf.z"], tag=u("lf.fadd"))
        p.dma("scalar", lambda e, z=z, t=t: e.dma_start(
            out=xout[t * 128:(t + 1) * 128, :], in_=z),
            reads=["lf.z"], writes=[xout_key], tag=u("lf.fst"))
    p.barrier()



class _MKHelpers:
    def bTT(self, out, in0, in1, op, reads, writes, tag):
        def fn(e, out=out, in0=in0, in1=in1, op=op):
            a, b = bass.broadcast_tensor_aps(in0, in1)
            return e.tensor_tensor(out, a, b, op=op)
        return self.p.v(fn, reads=reads, writes=writes, tag=tag)

    def router_tile_hook(self, prefix, rwT, rwT_key, logits, logits_key, nE=8):
        p = self.p

        def hook(t, tf, tfk, xtk):
            psT, (pkA, pkB) = self.bank_pair(2)
            for j in range(8):
                p.t(lambda e, tf=tf, j=j, psT=psT: e.transpose(
                    psT[:, j * 128:(j + 1) * 128], tf[:, j * 128:(j + 1) * 128],
                    self.IDF[:, :]),
                    reads=[tfk, "IDF"], writes=[pkA if j < 4 else pkB],
                    tag=self.u(f"{prefix}_rtr"))
            ev = self.F32P[:, 0:1024]
            p.v(lambda e, ev=ev, psT=psT: e.tensor_copy(ev, psT[:, :]),
                reads=[pkA, pkB], writes=[xtk], tag=self.u(f"{prefix}_rtev"))
            psL, pkL = self.bank(2)
            for kp in range(8):
                p.t(lambda e, ev=ev, kp=kp, psL=psL: e.matmul(
                    psL[0:nE, 0:128], rwT[:, kp * nE:(kp + 1) * nE],
                    ev[:, kp * 128:(kp + 1) * 128],
                    start=(kp == 0), stop=(kp == 7)),
                    reads=[xtk, rwT_key], writes=[pkL], tag=self.u(f"{prefix}_rmm"))
            lt = self.F32P[0:nE, 2368:2496]
            p.v(lambda e, lt=lt, psL=psL: e.tensor_copy(lt, psL[0:nE, 0:128]),
                reads=[pkL], writes=[f"{prefix}_lt"], tag=self.u(f"{prefix}_rle"))
            psK, pkK = self.bank(3)
            p.t(lambda e, lt=lt, psK=psK: e.matmul(
                psK[:, 0:nE], lt, self.IDF[0:nE, 0:nE], start=True, stop=True),
                reads=[f"{prefix}_lt", "IDF"], writes=[pkK], tag=self.u(f"{prefix}_rtb"))
            p.v(lambda e, psK=psK, t=t: e.tensor_copy(
                logits[:, t * nE:(t + 1) * nE], psK[:, 0:nE]),
                reads=[pkK], writes=[logits_key], tag=self.u(f"{prefix}_rlc"))
        return hook

    def router_tile_hook2(self, prefix, rwT, rwT_key, logits, logits_key,
                          dstv, dst_key, nE=8):
        p = self.p
        rwb = self.M[:, 16384:16448]
        rwbk = f"{prefix}_rwb"
        p.v(lambda e, rwb=rwb: e.tensor_copy(rwb, rwT),
            reads=[rwT_key], writes=[rwbk], tag=self.u(f"{prefix}_rwc"))

        def hook(t):
            psL, pkL = self.bank(2)
            for kp in range(8):
                p.t(lambda e, kp=kp, psL=psL, t=t: e.matmul(
                    psL[0:nE, 0:128], rwb[:, kp * nE:(kp + 1) * nE],
                    dstv.sl(kp, t * 128, 128),
                    start=(kp == 0), stop=(kp == 7)),
                    reads=[f"{dst_key}.t{t}", rwbk], writes=[pkL],
                    tag=self.u(f"{prefix}_rmm"))
            lt = self.F32P[0:nE, 2368:2496]
            p.v(lambda e, lt=lt, psL=psL: e.tensor_copy(lt, psL[0:nE, 0:128]),
                reads=[pkL], writes=[f"{prefix}_lt"], tag=self.u(f"{prefix}_rle"))
            psK, pkK = self.bank(3)
            p.t(lambda e, lt=lt, psK=psK: e.matmul(
                psK[:, 0:nE], lt, self.IDF[0:nE, 0:nE], start=True, stop=True),
                reads=[f"{prefix}_lt", "IDF"], writes=[pkK],
                tag=self.u(f"{prefix}_rtb"))
            p.v(lambda e, psK=psK, t=t: e.tensor_copy(
                logits[:, t * nE:(t + 1) * nE], psK[:, 0:nE]),
                reads=[pkK], writes=[logits_key], tag=self.u(f"{prefix}_rlc"))
        return hook

    def load_rwT(self, wname, dst, dst_key, prefix, nE=8, transposed=False):
        p = self.p
        w = self.dram[wname]
        if transposed:
            src = w.rearrange("(kp p) e -> p kp e", p=128)
            d3 = dst.rearrange("p (kp e) -> p kp e", kp=8)
            p.dma("sync", lambda e: e.dma_start(out=d3, in_=src),
                  writes=[dst_key], tag=self.u(f"{prefix}_rw"))
            return
        rw = self.F32P[0:nE, 0:1024]
        p.dma("sync", lambda e: e.dma_start(out=rw, in_=w[:, :]),
              writes=[f"{prefix}_rwrows"], tag=self.u(f"{prefix}_rwld"))
        psK, pkK = self.bank(3)
        for kp in range(8):
            p.t(lambda e, rw=rw, kp=kp, psK=psK: e.matmul(
                psK[:, kp * nE:(kp + 1) * nE], rw[:, kp * 128:(kp + 1) * 128],
                self.IDF[0:nE, 0:nE], start=True, stop=(kp == 7),
                skip_group_check=True),
                reads=[f"{prefix}_rwrows", "IDF"], writes=[pkK],
                tag=self.u(f"{prefix}_rwt"))
        p.v(lambda e, psK=psK: e.tensor_copy(dst, psK[:, 0:8 * nE]),
            reads=[pkK], writes=[dst_key], tag=self.u(f"{prefix}_rwe"))

    def top2(self, prefix, scores, scores_key, NT, outs):
        p = self.p
        BIG = 1.0e6
        v3 = lambda ap: ap.rearrange("p (t e) -> p t e", t=NT)
        v2 = lambda ap: ap.rearrange("p (t one) -> p t one", one=1)
        oh1, oh1k = outs["oh1"]
        oh2, oh2k = outs["oh2"]
        m1, m1k = outs["m1"]
        m2, m2k = outs["m2"]
        tmp, tmpk = outs["tmp"]
        idx, idxk = outs["idx"]
        iotaB, iotak = outs["iotaB"]
        for rnd, (oh, ohk, m, mkk, src, srck) in enumerate(
                ((oh1, oh1k, m1, m1k, scores, scores_key),
                 (oh2, oh2k, m2, m2k, tmp, tmpk))):
            p.v(lambda e, src=src, m=m: e.tensor_reduce(
                v2(m), v3(src), axis=AX.X, op=OP.max),
                reads=[srck], writes=[mkk], tag=self.u(f"{prefix}_max{rnd}"))
            self.bTT(v3(oh), v3(src), v2(m), OP.is_equal,
                     reads=[srck, mkk], writes=[ohk], tag=self.u(f"{prefix}_eq{rnd}"))
            p.v(lambda e, oh=oh, iotaB=iotaB, tmp=tmp: e.scalar_tensor_tensor(
                tmp, oh, -BIG, iotaB, op0=OP.mult, op1=OP.add),
                reads=[ohk, iotak], writes=[tmpk], tag=self.u(f"{prefix}_msk{rnd}"))
            p.v(lambda e, tmp=tmp, idx=idx: e.tensor_reduce(
                v2(idx), v3(tmp), axis=AX.X, op=OP.min),
                reads=[tmpk], writes=[idxk], tag=self.u(f"{prefix}_min{rnd}"))
            p.v(lambda e, idx=idx: e.tensor_scalar(
                idx, idx, BIG, None, op0=OP.add),
                reads=[idxk], writes=[idxk], tag=self.u(f"{prefix}_unb{rnd}"))
            self.bTT(v3(oh), v3(iotaB), v2(idx), OP.is_equal,
                     reads=[iotak, idxk], writes=[ohk], tag=self.u(f"{prefix}_oh{rnd}"))
            if rnd == 0:
                p.v(lambda e, oh=oh, scores=scores, tmp=tmp: e.scalar_tensor_tensor(
                    tmp, oh, -BIG, scores, op0=OP.mult, op1=OP.add),
                    reads=[ohk, scores_key], writes=[tmpk], tag=self.u(f"{prefix}_m2s"))


for _n in ('bTT', 'router_tile_hook', 'router_tile_hook2', 'load_rwT', 'top2'):
    setattr(MK, _n, getattr(_MKHelpers, _n))


def ph_l2_attn(mk, xin, xin_key, xout, xout_key):
    p, T, PT = mk.p, mk.T, mk.PT
    u = mk.u
    SINV = 1.0 / 8.0
    lnT = FMView(mk.A, 0, 8, T)
    attnT_dram = mk.dram["l0_attnT"]
    mk._bcol_cursor = 0
    bqc, k_bq = mk.bias_col("L2_attn_bq")
    bkc, k_bk = mk.bias_col("L2_attn_bk")
    mk.ln_transpose(xin_key, xin, lnT, "l2.lnT", prefix="l2ln")
    p.barrier()
    M = mk.M
    qT = M[:, 0:T]
    kT = M[:, T:2 * T]
    VP = M[:, 2 * T:2 * T + PT * 130]
    p.v(lambda e: e.memset(VP[:, :], 1.0), writes=["l2.VP"], tag=u("l2vones"))
    for h2 in range(8):
        p.barrier()
        wq, k_wq = mk.load_weight_fm("L2_attn_Wq", dst=mk.WT, dstoff=0,
                                     colslice=(h2 * 128, 128), key="l2.wq")
        wk, k_wk = mk.load_weight_fm("L2_attn_Wk", dst=mk.WT, dstoff=1024,
                                     colslice=(h2 * 128, 128), key="l2.wk")
        wv, k_wv = mk.load_weight_fm("L2_attn_Wv", dst=mk.WT, dstoff=2048,
                                     colslice=(h2 * 128, 128), key="l2.wv")
        for which, wv_, wk_, bc, bk_, dst, dk in (
                ("q", wq, k_wq, bqc, k_bq, qT, "l2.qT"),
                ("k", wk, k_wk, bkc, k_bk, kT, "l2.kT")):
            for tb in range(0, T, 512):
                ps, pkb = mk.bank(0 if which == "q" else 1)
                for kp in range(8):
                    p.t(lambda e, ps=ps, kp=kp, tb=tb, wv_=wv_: e.matmul(
                        ps, wv_.sl(kp, 0, 128), lnT.sl(kp, tb, 512),
                        start=(kp == 0), stop=(kp == 7)),
                        reads=[wk_] + [f"l2.lnT.t{tb // 128 + i}" for i in range(4)], writes=[pkb], tag=u(f"l2{which}mm"))
                p.s(lambda e, ps=ps, tb=tb, bc=bc, dst=dst, h2=h2: e.activation(
                    dst[:, tb:tb + 512], ps, AF.Identity, bias=bc[:, h2:h2 + 1]),
                    reads=[pkb, bk_], writes=[dk], tag=u(f"l2{which}ev"))
        p.dma("sync", lambda e, h2=h2: e.dma_start(
            out=mk.ROWS[0:1, 128:256],
            in_=mk.dram["L2_attn_bv"][h2 * 128:(h2 + 1) * 128].rearrange(
                "(one n) -> one n", one=1)),
            writes=["ROWS2"], tag=u("l2bv"))
        for t in range(PT):
            ps, pkb = mk.bank(4)
            p.t(lambda e, ps=ps: e.matmul(
                ps[:, 0:128], mk.ONESF[:, 0:128], mk.ROWS[0:1, 128:256],
                start=True, stop=False),
                reads=["ROWS2", "ONESF"], writes=[pkb], tag=u("l2vb"))
            for kp in range(8):
                p.t(lambda e, ps=ps, kp=kp, t=t, wv=wv: e.matmul(
                    ps[:, 0:128], lnT.sl(kp, t * 128, 128), wv.sl(kp, 0, 128),
                    start=False, stop=(kp == 7)),
                    reads=[k_wv, f"l2.lnT.t{t}"], writes=[pkb], tag=u("l2vmm"))
            dap = VP[:, t * 130:(t + 1) * 130].rearrange(
                "p (two c) -> p two c", two=2)[:, :, 0:64]
            sap = ps[:, 0:128].rearrange("p (two c) -> p two c", two=2)
            p.v(lambda e, dap=dap, sap=sap: e.tensor_copy(dap, sap),
                reads=[pkb], writes=["l2.VP"], tag=u("l2vev"))
        pend = []
        pend = []
        for qb in range(0, T, 512):
            def emit_S(kc, h, qb=qb):
                par = kc % 2
                ps_s, pk_s = mk.bank(h * 2 + par)
                p.t(lambda e, ps_s=ps_s, h=h, kc=kc, qb=qb: e.matmul(
                    ps_s, kT[h * 64:(h + 1) * 64, kc * 128:(kc + 1) * 128],
                    qT[h * 64:(h + 1) * 64, qb:qb + 512], start=True, stop=True),
                    reads=["l2.kT", "l2.qT"], writes=[pk_s], tag=u("l2S"))
            for h in range(2):
                emit_S(0, h)
            for kc in range(PT):
                if kc in (2, 4) and pend:
                    pend.pop(0)()
                if kc + 1 < PT:
                    for h in range(2):
                        emit_S(kc + 1, h)
                for h in range(2):
                    par = kc % 2
                    ps_s, pk_s = mk.bank(h * 2 + par)
                    if par == 0:
                        eb = mk.E[:, h * 512:(h + 1) * 512]
                    else:
                        eb = mk.TOKB[:, h * 512:(h + 1) * 512]
                    ekey = f"l2.E{h}_{par}"
                    p.s(lambda e, ps_s=ps_s, eb=eb: e.activation(
                        eb, ps_s, AF.Exp, scale=SINV),
                        reads=[pk_s], writes=[ekey], tag=u("l2exp"))
                    ps_av, pk_av = mk.bank(4 + h)
                    p.t(lambda e, ps_av=ps_av, h=h, kc=kc, eb=eb: e.matmul(
                        ps_av[0:65, :], VP[:, kc * 130 + h * 65:kc * 130 + (h + 1) * 65],
                        eb, start=(kc == 0), stop=(kc == PT - 1)),
                        reads=["l2.VP", ekey], writes=[pk_av], tag=u("l2AV"))
            for h in range(2):
                ps_av, pk_av = mk.bank(4 + h)
                avs = mk.F32P[0:65, h * 512:(h + 1) * 512]
                p.v(lambda e, ps_av=ps_av, avs=avs: e.tensor_copy(
                    avs, ps_av[0:65, 0:512]),
                    reads=[pk_av], writes=[f"l2.avs{h}"], tag=u("l2avse"))

            def mk_out(h, qb=qb, h2=h2):
                avs = mk.F32P[0:65, h * 512:(h + 1) * 512]
                rrow = mk.F32P[0:1, 1024 + h * 512:1024 + (h + 1) * 512]
                rb = mk.F32P[0:64, 2048:2560]
                stg = mk.TOKB[0:64, 1024 + h * 512:1024 + (h + 1) * 512]
                ps_bc, pk_bc = mk.bank(2 * h + 1)

                def emit():
                    p.v(lambda e, avs=avs, rrow=rrow: e.reciprocal(
                        rrow, avs[64:65, :]),
                        reads=[f"l2.avs{h}"], writes=[f"l2.rrow{h}"],
                        tag=u("l2rec"))
                    p.t(lambda e, ps_bc=ps_bc, rrow=rrow: e.matmul(
                        ps_bc[0:64, :], mk.ONESF[0:1, 0:64], rrow,
                        start=True, stop=True),
                        reads=[f"l2.rrow{h}", "ONESF"], writes=[pk_bc],
                        tag=u("l2bc"))
                    p.v(lambda e, ps_bc=ps_bc, rb=rb: e.tensor_copy(rb, ps_bc[0:64, :]),
                        reads=[pk_bc], writes=["l2.rb"], tag=u("l2rbev"))
                    p.v(lambda e, avs=avs, rb=rb, stg=stg: e.tensor_tensor(
                        stg, avs[0:64, :], rb, op=OP.mult),
                        reads=[f"l2.avs{h}", "l2.rb"], writes=[f"l2.ostg{h}"],
                        tag=u("l2onorm"))
                    drow = (2 * h2 + h) * 64
                    p.dma("scalar", lambda e, stg=stg, drow=drow, qb=qb: e.dma_start(
                        out=attnT_dram[drow:drow + 64, qb:qb + 512], in_=stg),
                        reads=[f"l2.ostg{h}"], writes=["l2.attnT"], tag=u("l2ost"))
                return emit
            pend.append(mk_out(0))
            pend.append(mk_out(1))
        while pend:
            pend.pop(0)()
    p.barrier()
    wo, k_wo = mk.load_weight_fm("L2_attn_Wo", dst=mk.WT, dstoff=0, key="l2.wo")
    for half in range(2):
        mk.bias_row("L2_attn_bo", half * 512, 512, key="ROWS", tag=u("l2bo"))
        for t in range(PT):
            sl = t % 2
            atile = mk.M[:, sl * 1024:(sl + 1) * 1024]
            ak = f"l2.at{sl}"
            sap = attnT_dram[:, t * 128:(t + 1) * 128].rearrange("(j r) c -> r j c", r=128)
            at3 = atile.rearrange("p (j c) -> p j c", j=8)
            p.dma("sync", lambda e, at3=at3, sap=sap: e.dma_start(out=at3, in_=sap),
                  reads=["l2.attnT"], writes=[ak], tag=u("l2wold"))
            xt = mk.F32P[:, sl * 1024:sl * 1024 + 512]
            xk = f"l2.wx{sl}"
            p.dma("sync", lambda e, xt=xt, t=t, half=half: e.dma_start(
                out=xt, in_=xin[t * 128:(t + 1) * 128, half * 512:(half + 1) * 512]),
                reads=[xin_key], writes=[xk], tag=u("l2wox"))
            ps, pkb = mk.bank(2 * sl)
            p.t(lambda e, ps=ps: e.matmul(
                ps, mk.ONESF[:, 0:128], mk.ROWS[0:1, 0:512], start=True, stop=False),
                reads=["ROWS", "ONESF"], writes=[pkb], tag=u("l2wob"))
            for kp in range(8):
                p.t(lambda e, ps=ps, kp=kp, atile=atile, half=half, wo=wo: e.matmul(
                    ps, atile[:, kp * 128:(kp + 1) * 128], wo.sl(kp, half * 512, 512),
                    start=False, stop=(kp == 7)),
                    reads=[ak, "l2.wo"], writes=[pkb], tag=u("l2womm"))
            ot = mk.F32P[:, 2048:2560]
            ok = "l2.wo_ot"
            p.v(lambda e, ps=ps, xt=xt, ot=ot: e.tensor_tensor(ot, ps, xt, op=OP.add),
                reads=[pkb, xk], writes=[ok], tag=u("l2woadd"))
            p.dma("scalar", lambda e, ot=ot, t=t, half=half: e.dma_start(
                out=xout[t * 128:(t + 1) * 128, half * 512:(half + 1) * 512], in_=ot),
                reads=[ok], writes=[xout_key], tag=u("l2wost"))
    p.barrier()


def ph_l2_ffn(mk, xin, xin_key, xout, xout_key):
    p, T, PT = mk.p, mk.T, mk.PT
    u = mk.u
    F = mk.F32P
    NE8 = PT * 8
    xnT = FMView(mk.A, 0, 8, T)
    xblk_dram = mk.dram["l2_xblk"]
    acc_dram = mk.dram["l1f_acc"]
    c_dram = mk.dram["l2_c"]
    cv = mk.constval
    eps_c = cv(1e-5)
    mk._bcol_cursor = 0
    gmc, k_gm = mk.bias_col("L2_ffn_gamma")
    btc, k_bt = mk.bias_col("L2_ffn_beta")
    rwT = F[:, 2048:2112]
    mk.load_rwT("L2_ffn_Wr", rwT, "l2f.rwT", "l2f", transposed=True)
    p.barrier()
    p.dma("sync", lambda e: e.dma_start(
        out=mk.ROWS[0:1, 0:8],
        in_=mk.dram["L2_ffn_br"][0:8].rearrange("(one n) -> one n", one=1)),
        writes=["ROWS"], tag=u("l2f.brld"))
    psK, pkK = mk.bank(2)
    p.t(lambda e, psK=psK: e.matmul(
        psK[:, 0:8], mk.ONESF[:, 0:128], mk.ROWS[0:1, 0:8], start=True, stop=True),
        reads=["ROWS", "ONESF"], writes=[pkK], tag=u("l2f.bb"))
    rb = mk.STAT[:, 8:16]
    p.v(lambda e, psK=psK, rb=rb: e.tensor_copy(rb, psK[:, 0:8]),
        reads=[pkK], writes=["l2f.rbb"], tag=u("l2f.rbe"))
    p.barrier()
    logits = F[:, 2112:2112 + NE8]
    xnT = FMView(mk.A, 0, 8, T)
    hook = mk.router_tile_hook2("l2f", rwT, "l2f.rwT", logits, "l2f.logits",
                                xnT, "l2f.xnT")
    for t in range(PT):
        xt = F[:, 0:1024]
        xk = "l2f_xt"
        xb2 = F[:, 1024:2048]
        xb2k = "l2f_xb2"
        st = mk.STAT[:, 0:8]
        sk = "l2f_st"
        junk = mk.TOKB[:, 0:1024]
        p.dma("sync", lambda e, xt=xt, t=t: e.dma_start(
            out=xt, in_=xin[t * 128:(t + 1) * 128, :]),
            reads=[xin_key], writes=[xk], tag=u("l2f.ldx"))
        p.v(lambda e, xt=xt, st=st: e.tensor_reduce(st[:, 0:1], xt, axis=AX.X, op=OP.add),
            reads=[xk], writes=[sk], tag=u("l2f.s0"))
        p.v(lambda e, st=st: e.tensor_scalar(st[:, 1:2], st[:, 0:1], -1.0 / D, None, op0=OP.mult),
            reads=[sk], writes=[sk], tag=u("l2f.s1"))
        p.s(lambda e, xt=xt, st=st, junk=junk: e.activation(
            junk, xt, AF.Square, bias=st[:, 1:2], accum_out=st[:, 2:3]),
            reads=[xk, sk], writes=["l2f.junk", sk], tag=u("l2f.s2"))
        p.s(lambda e, st=st: e.activation(st[:, 3:4], st[:, 2:3], AF.Ln, scale=1.0 / D, bias=eps_c),
            reads=[sk, "cpool"], writes=[sk], tag=u("l2f.s3"))
        p.s(lambda e, st=st: e.activation(st[:, 4:5], st[:, 3:4], AF.Exp, scale=-0.5),
            reads=[sk], writes=[sk], tag=u("l2f.s4"))
        p.v(lambda e, st=st: e.tensor_tensor(st[:, 5:6], st[:, 1:2], st[:, 4:5], op=OP.mult),
            reads=[sk], writes=[sk], tag=u("l2f.s5"))
        p.s(lambda e, xt=xt, st=st, xb2=xb2: e.activation(
            xb2, xt, AF.Identity, scale=st[:, 4:5], bias=st[:, 5:6]),
            reads=[xk, sk], writes=[xb2k], tag=u("l2f.s6"))
        p.dma("scalar", lambda e, xb2=xb2, t=t: e.dma_start(
            out=xblk_dram[t * 128:(t + 1) * 128, :], in_=xb2),
            reads=[xb2k], writes=["l2f.xblk"], tag=u("l2f.stx"))
        p.v(lambda e, xb2=xb2, st=st: e.tensor_reduce(st[:, 0:1], xb2, axis=AX.X, op=OP.add),
            reads=[xb2k], writes=[sk], tag=u("l2f.t0"))
        p.v(lambda e, st=st: e.tensor_scalar(st[:, 1:2], st[:, 0:1], -1.0 / D, None, op0=OP.mult),
            reads=[sk], writes=[sk], tag=u("l2f.t1"))
        p.s(lambda e, xb2=xb2, st=st, junk=junk: e.activation(
            junk, xb2, AF.Square, bias=st[:, 1:2], accum_out=st[:, 2:3]),
            reads=[xb2k, sk], writes=["l2f.junk", sk], tag=u("l2f.t2"))
        p.s(lambda e, st=st: e.activation(st[:, 3:4], st[:, 2:3], AF.Ln, scale=1.0 / D, bias=eps_c),
            reads=[sk, "cpool"], writes=[sk], tag=u("l2f.t3"))
        p.s(lambda e, st=st: e.activation(st[:, 4:5], st[:, 3:4], AF.Exp, scale=-0.5),
            reads=[sk], writes=[sk], tag=u("l2f.t4"))
        p.v(lambda e, st=st: e.tensor_tensor(st[:, 5:6], st[:, 1:2], st[:, 4:5], op=OP.mult),
            reads=[sk], writes=[sk], tag=u("l2f.t5"))
        n32 = xt
        p.s(lambda e, xb2=xb2, st=st, n32=n32: e.activation(
            n32, xb2, AF.Identity, scale=st[:, 4:5], bias=st[:, 5:6]),
            reads=[xb2k, sk], writes=[xk], tag=u("l2f.t6"))
        nb = mk.TOKB[:, 1024:2048]
        p.v(lambda e, n32=n32, nb=nb: e.tensor_copy(nb, n32),
            reads=[xk], writes=["l2f.nb"], tag=u("l2f.cast"))
        pbf = mk.PBF[t % 2]
        pbk = f"psBF{t % 2}"
        for j in range(8):
            p.t(lambda e, nb=nb, pbf=pbf, j=j: e.transpose(
                pbf[:, j * 128:(j + 1) * 128], nb[:, j * 128:(j + 1) * 128],
                mk.IDB[:, :]),
                reads=["l2f.nb", "IDB"], writes=[pbk], tag=u("l2f.tr"))
        for j in range(8):
            p.v(lambda e, pbf=pbf, j=j, t=t: e.tensor_scalar(
                xnT.sl(j, t * 128, 128), pbf[:, j * 128:(j + 1) * 128],
                gmc[:, j:j + 1], btc[:, j:j + 1], op0=OP.mult, op1=OP.add),
                reads=[pbk, k_gm, k_bt], writes=["l2f.xnT", f"l2f.xnT.t{t}"], tag=u("l2f.ev"))
        hook(t)
    p.barrier()
    s_ = F[:, 0:NE8]
    oh1 = F[:, 512:512 + NE8]
    oh2 = F[:, 768:768 + NE8]
    tmp = F[:, 1024:1024 + NE8]
    iotaB = F[:, 1280:1280 + NE8]
    w_ = F[:, 2112:2112 + NE8]
    m1 = F[:, 1792:1792 + PT]
    m2 = F[:, 1824:1824 + PT]
    idx = F[:, 1856:1856 + PT]
    mx = F[:, 1888:1888 + PT]
    sm = F[:, 1920:1920 + PT]
    v3 = lambda ap: ap.rearrange("p (t e) -> p t e", t=PT)
    v2 = lambda ap: ap.rearrange("p (t one) -> p t one", one=1)
    if getattr(mk, "DEBUG_L2F", False):
        d = mk.dram["dbg_lg"]
        p.dma("scalar", lambda e, d=d: e.dma_start(out=d[:, 0:NE8], in_=logits),
              reads=["l2f.logits"], writes=["dbg_lg"], tag=u("dbglg"))
        d2 = mk.dram["dbg_rb"]
        p.dma("scalar", lambda e, d2=d2: e.dma_start(out=d2[:, 0:8], in_=rb),
              reads=["l2f.rbb"], writes=["dbg_rb"], tag=u("dbgrb"))
        d3 = mk.dram["dbg_bc"]
        p.dma("scalar", lambda e, d3=d3: e.dma_start(out=d3[:, :], in_=mk.BCOL[:, 0:16]),
              reads=[k_gm, k_bt], writes=["dbg_bc"], tag=u("dbgbc"))
    p.g(lambda e: e.iota(iotaB, pattern=[[0, PT], [1, 8]], base=0,
                         channel_multiplier=0, allow_small_or_imprecise_dtypes=True),
        writes=["l2f.iota"], tag=u("l2f.iota"))
    mk.bTT(v3(s_), v3(logits), rb.rearrange("p (one e) -> p one e", one=1), OP.add,
           reads=["l2f.logits", "l2f.rbb"], writes=["l2f.s"], tag=u("l2f.addb"))
    p.v(lambda e: e.tensor_reduce(v2(mx), v3(s_), axis=AX.X, op=OP.max),
        reads=["l2f.s"], writes=["l2f.mx"], tag=u("l2f.smax"))
    mk.bTT(v3(s_), v3(s_), v2(mx), OP.subtract,
           reads=["l2f.s", "l2f.mx"], writes=["l2f.s"], tag=u("l2f.ssub"))
    p.s(lambda e: e.activation(s_, s_, AF.Exp),
        reads=["l2f.s"], writes=["l2f.s"], tag=u("l2f.sexp"))
    p.v(lambda e: e.tensor_reduce(v2(sm), v3(s_), axis=AX.X, op=OP.add),
        reads=["l2f.s"], writes=["l2f.sm"], tag=u("l2f.ssum"))
    p.v(lambda e: e.reciprocal(sm, sm),
        reads=["l2f.sm"], writes=["l2f.sm"], tag=u("l2f.srec"))
    mk.bTT(v3(s_), v3(s_), v2(sm), OP.mult,
           reads=["l2f.s", "l2f.sm"], writes=["l2f.s"], tag=u("l2f.snrm"))
    mk.top2("l2f", s_, "l2f.s", PT, dict(
        oh1=(oh1, "l2f.oh1"), oh2=(oh2, "l2f.oh2"), m1=(m1, "l2f.m1"),
        m2=(m2, "l2f.m2"), tmp=(tmp, "l2f.tmp"), idx=(idx, "l2f.idx"),
        iotaB=(iotaB, "l2f.iota")))
    mk.bTT(v3(tmp), v3(oh1), v2(m1), OP.mult,
           reads=["l2f.oh1", "l2f.m1"], writes=["l2f.tmp"], tag=u("l2f.w1"))
    mk.bTT(v3(w_), v3(oh2), v2(m2), OP.mult,
           reads=["l2f.oh2", "l2f.m2"], writes=["l2f.w"], tag=u("l2f.w2"))
    p.v(lambda e: e.tensor_tensor(w_, w_, tmp, op=OP.add),
        reads=["l2f.w", "l2f.tmp"], writes=["l2f.w"], tag=u("l2f.w3"))
    if getattr(mk, "DEBUG_L2F", False):
        for nm, buf, rk in (("dbg_w", w_, "l2f.w"), ("dbg_s", s_, "l2f.s")):
            d = mk.dram[nm]
            p.dma("scalar", lambda e, d=d, buf=buf: e.dma_start(out=d[:, 0:buf.shape[1]], in_=buf),
                  reads=[rk], writes=[nm], tag=u("dbgw"))
    z0 = F[:, 0:512]
    p.v(lambda e, z0=z0: e.memset(z0, 0.0), writes=["l2f.z0", "l2f.s"], tag=u("l2f.z0"))
    for t in range(PT):
        for half in range(2):
            p.dma("scalar", lambda e, z0=z0, t=t, half=half: e.dma_start(
                out=acc_dram[t * 128:(t + 1) * 128, half * 512:(half + 1) * 512],
                in_=z0),
                reads=["l2f.z0"], writes=[f"l2f.acc_{t}_{half}"], tag=u("l2f.zst"))
    p.barrier()
    hT = FMView(mk.M, 0, 8, 512)
    for e_ in range(8):
        for hq in range(4):
            qb_ = (e_ * 4 + hq) % 2
            mk._bcol_cursor = qb_ * 8
            w1h, k_w1 = mk.load_weight_fm("l2f_w1b", queue="sync", dst=mk.B,
                                          dstoff=qb_ * 16384,
                                          colslice=(hq * 1024, 1024),
                                          key=f"l2f.w1h{qb_}", expert=e_,
                                          src_key="cast.l2f_w1b")
            w2h, k_w2 = mk.load_weight_fm("l2f_w2b", queue="sync", dst=mk.B,
                                          dstoff=qb_ * 16384 + 8192,
                                          rowslice=(hq * 1024, 1024),
                                          key=f"l2f.w2h{qb_}", expert=e_,
                                          src_key="cast.l2f_w2b")
            b1c, k_b1 = mk.bias_col_expert("L2_ffn_b1", e_, hq, width=1024,
                                           key=f"l2f.b1q{qb_}")
            rb1 = mk.TOKB[:, 1152:1160]
            p.v(lambda e, b1c=b1c, rb1=rb1: e.tensor_scalar(
                rb1, b1c, 0.0, None, op0=OP.max),
                reads=[k_b1], writes=["l2f.rb1"], tag=u("l2f.rb1"))
            psc, (pc0, pc1) = mk.bank_pair(2)
            for half in range(2):
                for hp in range(8):
                    p.t(lambda e, psc=psc, hp=hp, half=half, rb1=rb1, w2h=w2h: e.matmul(
                        psc[0:1, half * 512:(half + 1) * 512], rb1[:, hp:hp + 1],
                        w2h.sl(hp, half * 512, 512), start=(hp == 0), stop=(hp == 7),
                        skip_group_check=True),
                        reads=["l2f.rb1", k_w2], writes=[pc0 if half == 0 else pc1],
                        tag=u("l2f.cmm"))
            cstg = mk.TOKB[0:1, 0:1024]
            if hq == 0:
                p.v(lambda e, psc=psc, cstg=cstg: e.tensor_copy(cstg, psc[0:1, 0:1024]),
                    reads=[pc0, pc1], writes=["l2f.cstg"], tag=u("l2f.cev0"))
            else:
                p.v(lambda e, psc=psc, cstg=cstg: e.tensor_tensor(
                    cstg, cstg, psc[0:1, 0:1024], op=OP.add),
                    reads=[pc0, pc1, "l2f.cstg"], writes=["l2f.cstg"], tag=u("l2f.cev1"))
            for tb in range(0, T, 512):
                for m in range(8):
                    ps, pkb = mk.bank(m % 2)
                    for kp in range(8):
                        p.t(lambda e, ps=ps, m=m, kp=kp, tb=tb, w1h=w1h: e.matmul(
                            ps, w1h.sl(kp, m * 128, 128), xnT.sl(kp, tb, 512),
                            start=(kp == 0), stop=(kp == 7)),
                            reads=[k_w1] + [f"l2f.xnT.t{tb // 128 + i}" for i in range(4)], writes=[pkb], tag=u("l2f.hmm"))
                    p.s(lambda e, ps=ps, m=m, b1c=b1c: e.activation(
                        hT.sl(m, 0, 512), ps, AF.Relu, bias=b1c[:, m:m + 1]),
                        reads=[pkb, k_b1], writes=[f"l2f.hT{m}"], tag=u("l2f.relu"))
                for tl in range(4):
                    t = tb // 128 + tl
                    for hp in range(8):
                        for hd in range(2):
                            ps, pkb = mk.bank(4 + hd)
                            p.t(lambda e, ps=ps, hp=hp, tl=tl, hd=hd, w2h=w2h: e.matmul(
                                ps, hT.sl(hp, tl * 128, 128), w2h.sl(hp, hd * 512, 512),
                                start=(hp == 0), stop=(hp == 7)),
                                reads=[f"l2f.hT{hp}", k_w2], writes=[pkb], tag=u("l2f.ymm"))
                    pr = tl % 2
                    for hd in range(2):
                        ps, pkb = mk.bank(4 + hd)
                        stg = F[:, pr * 1024 + hd * 512:pr * 1024 + (hd + 1) * 512]
                        sk = f"l2f.stg{pr}{hd}"
                        p.v(lambda e, ps=ps, stg=stg, t=t, e_=e_: e.tensor_scalar(
                            stg, ps, w_[:, t * 8 + e_:t * 8 + e_ + 1], None, op0=OP.mult),
                            reads=[pkb, "l2f.w"], writes=[sk], tag=u("l2f.scl"))
                        p.dma("gpsimd", lambda e, stg=stg, t=t, hd=hd: e.dma_start(
                            out=acc_dram[t * 128:(t + 1) * 128,
                                         hd * 512:(hd + 1) * 512],
                            in_=stg, accum_op=OP.add),
                            reads=[sk], writes=[f"l2f.acc_{t}_{hd}"],
                            tag=u("l2f.ac"))
        cstg = mk.TOKB[0:1, 0:1024]
        for half in range(2):
            p.dma("sync", lambda e, half=half, e_=e_: e.dma_start(
                out=mk.ROWS[0:1, 0:512],
                in_=mk.dram["L2_ffn_b2"][e_, half * 512:(half + 1) * 512].rearrange(
                    "(one n) -> one n", one=1)),
                writes=["ROWS"], tag=u("l2f.cb2"))
            p.v(lambda e, cstg=cstg, half=half: e.tensor_tensor(
                cstg[0:1, half * 512:(half + 1) * 512],
                cstg[0:1, half * 512:(half + 1) * 512],
                mk.ROWS[0:1, 0:512], op=OP.add),
                reads=["l2f.cstg", "ROWS"], writes=["l2f.cstg"], tag=u("l2f.cba"))
        p.dma("scalar", lambda e, cstg=cstg, e_=e_: e.dma_start(
            out=c_dram[e_:e_ + 1, :], in_=cstg),
            reads=["l2f.cstg"], writes=["l2f.c"], tag=u("l2f.cst"))
    p.barrier()
    csb = mk.E[0:8, 0:1024]
    p.dma("gpsimd", lambda e, csb=csb: e.dma_start(out=csb, in_=c_dram[:, :]),
          reads=["l2f.c"], writes=["l2f.csb"], tag=u("l2f.cld"))
    one8 = mk.ROWS[0:1, 16:24]
    p.v(lambda e, one8=one8: e.memset(one8, 1.0), writes=["l2f.one8"], tag=u("l2f.o8"))
    onec = mk.TOKB[0:8, 1168:1169]
    p.v(lambda e, onec=onec: e.memset(onec, 1.0), writes=["l2f.onec"], tag=u("l2f.oc"))
    psc, (pc0, pc1) = mk.bank_pair(2)
    for half in range(2):
        p.t(lambda e, psc=psc, half=half, onec=onec, csb=csb: e.matmul(
            psc[0:1, half * 512:(half + 1) * 512], onec, csb[:, half * 512:(half + 1) * 512],
            start=True, stop=True, skip_group_check=True),
            reads=["l2f.onec", "l2f.csb"], writes=[pc0 if half == 0 else pc1],
            tag=u("l2f.ct"))
    ctrow = mk.ROWS[0:1, 0:512]
    psb2, pkb2 = mk.bank(0)
    b2sb = mk.M[0:8, 0:1024]
    p.dma("gpsimd", lambda e, b2sb=b2sb: e.dma_start(
        out=b2sb, in_=mk.dram["L2_ffn_b2"][:, :]),
        writes=["l2f.b2sb"], tag=u("l2f.b2ld"))
    cp = mk.TOKB[0:8, 0:1024]
    for half in range(2):
        p.v(lambda e, psc=psc, ctrow=ctrow, half=half: e.tensor_copy(
            ctrow, psc[0:1, half * 512:(half + 1) * 512]),
            reads=[pc0 if half == 0 else pc1], writes=["ROWS"], tag=u("l2f.cte"))
        p.t(lambda e, psb2=psb2, ctrow=ctrow: e.matmul(
            psb2[0:8, 0:512], mk.ONESF[0:1, 0:8], ctrow, start=True, stop=True),
            reads=["ROWS", "ONESF"], writes=[pkb2], tag=u("l2f.ctb"))
        p.v(lambda e, psb2=psb2, cp=cp, csb=csb, half=half: e.tensor_tensor(
            cp[0:8, half * 512:(half + 1) * 512], psb2[0:8, 0:512],
            csb[:, half * 512:(half + 1) * 512], op=OP.subtract),
            reads=[pkb2, "l2f.csb"], writes=["l2f.cp"], tag=u("l2f.cps"))
        p.v(lambda e, cp=cp, b2sb=b2sb, half=half: e.tensor_tensor(
            cp[0:8, half * 512:(half + 1) * 512],
            cp[0:8, half * 512:(half + 1) * 512],
            b2sb[:, half * 512:(half + 1) * 512], op=OP.add),
            reads=["l2f.cp", "l2f.b2sb"], writes=["l2f.cp"], tag=u("l2f.cpb"))
    for t in range(PT):
        pswt, pkwt = mk.bank(1)
        p.t(lambda e, pswt=pswt, t=t: e.matmul(
            pswt[0:8, 0:128], w_[:, t * 8:(t + 1) * 8], mk.IDF[:, :],
            start=True, stop=True),
            reads=["l2f.w", "IDF"], writes=[pkwt], tag=u("l2f.wt"))
        wtb = mk.TOKB[0:8, 1024:1152]
        p.v(lambda e, pswt=pswt, wtb=wtb: e.tensor_copy(wtb, pswt[0:8, 0:128]),
            reads=[pkwt], writes=["l2f.wtb"], tag=u("l2f.wte"))
        for half in range(2):
            z = F[:, half * 512:(half + 1) * 512]
            n = F[:, 1024 + half * 512:1024 + (half + 1) * 512]
            psx, pkx = mk.bank(4 + half)
            p.t(lambda e, psx=psx, wtb=wtb, cp=cp, half=half: e.matmul(
                psx, wtb, cp[0:8, half * 512:(half + 1) * 512],
                start=True, stop=True),
                reads=["l2f.wtb", "l2f.cp"], writes=[pkx], tag=u("l2f.corr"))
            p.dma("sync", lambda e, z=z, t=t, half=half: e.dma_start(
                out=z, in_=acc_dram[t * 128:(t + 1) * 128, half * 512:(half + 1) * 512]),
                reads=[f"l2f.acc_{t}_{half}"], writes=["l2f.z"], tag=u("l2f.fl1"))
            p.dma("sync", lambda e, n=n, t=t, half=half: e.dma_start(
                out=n, in_=xblk_dram[t * 128:(t + 1) * 128, half * 512:(half + 1) * 512]),
                reads=["l2f.xblk"], writes=["l2f.n"], tag=u("l2f.fl2"))
            p.v(lambda e, z=z, psx=psx: e.tensor_tensor(z, z, psx, op=OP.add),
                reads=["l2f.z", pkx], writes=["l2f.z"], tag=u("l2f.fc"))
            p.v(lambda e, z=z, n=n: e.tensor_tensor(z, z, n, op=OP.add),
                reads=["l2f.z", "l2f.n"], writes=["l2f.z"], tag=u("l2f.fa1"))
            p.dma("sync", lambda e, n=n, t=t, half=half: e.dma_start(
                out=n, in_=xin[t * 128:(t + 1) * 128, half * 512:(half + 1) * 512]),
                reads=[xin_key], writes=["l2f.n"], tag=u("l2f.fl3"))
            p.v(lambda e, z=z, n=n: e.tensor_tensor(z, z, n, op=OP.add),
                reads=["l2f.z", "l2f.n"], writes=["l2f.z"], tag=u("l2f.fa2"))
            p.dma("scalar", lambda e, z=z, t=t, half=half: e.dma_start(
                out=xout[t * 128:(t + 1) * 128, half * 512:(half + 1) * 512], in_=z),
                reads=["l2f.z"], writes=[xout_key], tag=u("l2f.fst"))
    p.barrier()


def _bias_col_expert(self, wname, e_, hh, width=2048, key=None):
    p = self.p
    vec = self.dram[wname]
    ncols = width // 128
    c0 = self._bcol_cursor
    assert c0 + ncols <= 48
    self._bcol_cursor += ncols
    dst = self.BCOL[:, c0:c0 + ncols]
    key = key or f"bcole_{wname}_{e_}_{hh}_{c0}"
    psum, pkey = self.bank(4)
    for h0 in range(0, width, 512):
        hn = min(512, width - h0)
        p.dma("sync", lambda e, h0=h0, hn=hn: e.dma_start(
            out=self.ROWS[0:1, 0:hn],
            in_=vec[e_, hh * width + h0:hh * width + h0 + hn].rearrange(
                "(one n) -> one n", one=1)),
            writes=["ROWS"], tag=self.u("ldbe"))
        nch = hn // 128
        for j in range(nch):
            p.t(lambda e, j=j: e.matmul(
                psum[:, j:j + 1], self.ROWS[0:1, j * 128:(j + 1) * 128],
                self.ONE11F[:, :], start=True, stop=True),
                reads=["ROWS", "ONE11F"], writes=[pkey], tag=self.u("r2ce"))
        base = h0 // 128
        p.v(lambda e, nch=nch, base=base: e.tensor_copy(
            dst[:, base:base + nch], psum[:, 0:nch]),
            reads=[pkey], writes=[key], tag=self.u("evbe"))
    return dst, key


def _bias_row_expert(self, wname, e_, hd):
    vec = self.dram[wname]
    self.p.dma("sync", lambda e: e.dma_start(
        out=self.ROWS[0:1, 0:512],
        in_=vec[e_, hd * 512:(hd + 1) * 512].rearrange("(one n) -> one n", one=1)),
        writes=["ROWS"], tag=self.u("ldbre"))


MK.bias_col_expert = _bias_col_expert
MK.bias_row_expert = _bias_row_expert


def ph_l3_attn(mk, xin, xin_key, xout, xout_key):
    p, T, PT = mk.p, mk.T, mk.PT
    u = mk.u
    F = mk.F32P
    M = mk.M
    cv = mk.constval
    eps_c = cv(1e-5)
    wt_dram = mk.dram["l3_wt"]
    bo_dram = mk.dram["l3_bo"]
    WNAMES = ["L3_attn_Wk", "L3_attn_Wv", "L3_attn_Wq", "L3_attn_Wkp",
              "L3_attn_Wqp", "L3_attn_Wo"]
    for i, nm in enumerate(WNAMES):
        p.dma("gpsimd", lambda e, i=i, nm=nm: e.dma_start(
            out=wt_dram[i], in_=mk.dram[nm][:, :]),
            writes=["l3.wbf"], tag=u("l3cast"))
    views = {}
    for i, nm in enumerate(WNAMES):
        pool = mk.A if i < 4 else mk.B
        off = (i % 4) * 8192 if i < 4 else (i - 4) * 8192
        views[nm] = FMView(pool, off, 8, 1024)
        for j in range(8):
            p.dma("sync", lambda e, i=i, j=j, pool=pool, off=off: e.dma_start(
                out=pool[:, off + j * 1024:off + (j + 1) * 1024],
                in_=wt_dram[i][:, j * 128:(j + 1) * 128], transpose=True),
                reads=["l3.wbf"], writes=[f"l3.w_{nm}"], tag=u("l3wtr"))
    wk, wv, wq, wkp, wqp, wo = (views[n] for n in WNAMES)
    p.barrier()
    mk._bcol_cursor = 40
    lnbc, k_lnb = mk.bias_col("L3_attn_ln_b")
    lnbb = mk.TOKB[:, 0:8]
    p.v(lambda e, lnbb=lnbb: e.tensor_copy(lnbb, lnbc[:, 0:8]),
        reads=[k_lnb], writes=["l3.lnbb"], tag=u("l3lnbb"))
    for half in range(2):
        psX, pkX = mk.bank(half)
        for kp in range(8):
            p.t(lambda e, kp=kp, psX=psX, half=half: e.matmul(
                psX[0:1, 0:512], lnbb[:, kp:kp + 1], wo.sl(kp, half * 512, 512),
                start=(kp == 0), stop=(kp == 7)),
                reads=["l3.lnbb", "l3.w_L3_attn_Wo"], writes=[pkX], tag=u("l3bo2"))
        mk.bias_row("L3_attn_bo", half * 512, 512, key="ROWS", tag=u("l3bor"))
        br = mk.ROWS[0:1, 0:512]
        p.v(lambda e, psX=psX, br=br: e.tensor_tensor(
            br, br, psX[0:1, 0:512], op=OP.add),
            reads=[pkX, "ROWS"], writes=["ROWS"], tag=u("l3boa"))
        p.dma("scalar", lambda e, br=br, half=half: e.dma_start(
            out=bo_dram[0:1, half * 512:(half + 1) * 512], in_=br),
            reads=["ROWS"], writes=["l3.bo"], tag=u("l3bos"))
    p.barrier()
    mk._bcol_cursor = 40
    lngc, k_lng = mk.bias_col("L3_attn_ln_g")
    for kp in range(8):
        p.v(lambda e, kp=kp: e.tensor_scalar(
            wo.sl(kp, 0, 1024), wo.sl(kp, 0, 1024), lngc[:, kp:kp + 1], None,
            op0=OP.mult),
            reads=["l3.w_L3_attn_Wo", k_lng], writes=["l3.w_L3_attn_Wo"],
            tag=u("l3fold"))
    mk._bcol_cursor = 0
    bkc, k_bk = mk.bias_col("L3_attn_bk")
    bvc, k_bv = mk.bias_col("L3_attn_bv")
    bqc, k_bq = mk.bias_col("L3_attn_bq")
    bkpc, k_bkp = mk.bias_col("L3_attn_bkp")
    bqpc, k_bqp = mk.bias_col("L3_attn_bqp")
    psc, k_ps = mk.bias_col("L3_attn_phase_scale")
    p.barrier()
    mS = dict(lnT=0, keyT=1024, valT=2048, qTT=3072, kphF=4096, qphF=5120,
              val_t=6144, kph_t=7168, qph_t=8192, sin_k=9216, cos_k=10240,
              sin_q=11264, cos_q=12288, br_=13312, bi_=14336, nT=15360)
    sl_ = lambda name: M[:, mS[name]:mS[name] + 1024]
    TRIG_TMP = M[:, 3072:4096]
    for t in range(PT):
        xt = F[:, 0:1024]
        ln = F[:, 1024:2048]
        st = mk.STAT[:, 0:8]
        junk = mk.TOKB[:, 0:1024]
        kx, lk, sk = "l3.xt", "l3.ln", "l3.st"
        p.dma("sync", lambda e, xt=xt, t=t: e.dma_start(
            out=xt, in_=xin[t * 128:(t + 1) * 128, :]),
            reads=[xin_key], writes=[kx, "l3.ot", "l3.t0", "l3.t1"], tag=u("l3ldx"))
        p.v(lambda e, xt=xt, st=st: e.tensor_reduce(st[:, 0:1], xt, axis=AX.X, op=OP.add),
            reads=[kx], writes=[sk], tag=u("l3s0"))
        p.v(lambda e, st=st: e.tensor_scalar(st[:, 1:2], st[:, 0:1], -1.0 / D, None, op0=OP.mult),
            reads=[sk], writes=[sk], tag=u("l3s1"))
        p.s(lambda e, xt=xt, st=st, junk=junk: e.activation(
            junk, xt, AF.Square, bias=st[:, 1:2], accum_out=st[:, 2:3]),
            reads=[kx, sk], writes=["l3.junk", sk], tag=u("l3s2"))
        p.s(lambda e, st=st: e.activation(st[:, 3:4], st[:, 2:3], AF.Ln, scale=1.0 / D, bias=eps_c),
            reads=[sk, "cpool"], writes=[sk], tag=u("l3s3"))
        p.s(lambda e, st=st: e.activation(st[:, 4:5], st[:, 3:4], AF.Exp, scale=-0.5),
            reads=[sk], writes=[sk], tag=u("l3s4"))
        p.v(lambda e, st=st: e.tensor_tensor(st[:, 5:6], st[:, 1:2], st[:, 4:5], op=OP.mult),
            reads=[sk], writes=[sk], tag=u("l3s5"))
        p.s(lambda e, xt=xt, st=st, ln=ln: e.activation(
            ln, xt, AF.Identity, scale=st[:, 4:5], bias=st[:, 5:6]),
            reads=[kx, sk], writes=[lk], tag=u("l3s6"))
        lnb = mk.TOKB[:, 1024:2048]
        p.v(lambda e, ln=ln, lnb=lnb: e.tensor_copy(lnb, ln),
            reads=[lk], writes=["l3.lnb"], tag=u("l3cast"))
        pbf = mk.PBF[0]
        pbk = "psBF0"
        for j in range(8):
            p.t(lambda e, lnb=lnb, pbf=pbf, j=j: e.transpose(
                pbf[:, j * 128:(j + 1) * 128], lnb[:, j * 128:(j + 1) * 128],
                mk.IDB[:, :]),
                reads=["l3.lnb", "IDB"], writes=[pbk], tag=u("l3lntr"))
        p.v(lambda e, pbf=pbf: e.tensor_copy(sl_("lnT"), pbf[:, :]),
            reads=[pbk], writes=["l3.lnT"], tag=u("l3lnev"))
        def proj(src_slot, src_key, wview, wkey, bcol, bk_, dst_slot, dst_key,
                 tanh_ps=False):
            ps0, (pA, pB) = mk.bank_pair(0)
            for m in range(8):
                pk_ = pA if m < 4 else pB
                for kp in range(8):
                    p.t(lambda e, m=m, kp=kp, ps0=ps0, wview=wview: e.matmul(
                        ps0[:, m * 128:(m + 1) * 128],
                        wview.sl(kp, m * 128, 128),
                        sl_(src_slot)[:, kp * 128:(kp + 1) * 128],
                        start=(kp == 0 and m % 4 == 0), stop=False,
                        skip_group_check=True),
                        reads=[wkey, src_key], writes=[pk_], tag=u("l3pj"))
            if not tanh_ps:
                for m in range(8):
                    pk_ = pA if m < 4 else pB
                    p.s(lambda e, m=m, ps0=ps0: e.activation(
                        sl_(dst_slot)[:, m * 128:(m + 1) * 128],
                        ps0[:, m * 128:(m + 1) * 128], AF.Identity,
                        bias=bcol[:, m:m + 1]),
                        reads=[pk_, bk_], writes=[dst_key], tag=u("l3pe"))
            else:
                for m in range(8):
                    pk_ = pA if m < 4 else pB
                    p.s(lambda e, m=m, ps0=ps0: e.activation(
                        sl_(dst_slot)[:, m * 128:(m + 1) * 128],
                        ps0[:, m * 128:(m + 1) * 128], AF.Tanh,
                        bias=bcol[:, m:m + 1]),
                        reads=[pk_, bk_], writes=[dst_key], tag=u("l3pt"))
                for m in range(8):
                    p.v(lambda e, m=m: e.tensor_scalar(
                        sl_(dst_slot)[:, m * 128:(m + 1) * 128],
                        sl_(dst_slot)[:, m * 128:(m + 1) * 128],
                        psc[:, m:m + 1], None, op0=OP.mult),
                        reads=[dst_key, k_ps], writes=[dst_key], tag=u("l3pp"))
        proj("lnT", "l3.lnT", wk, "l3.w_L3_attn_Wk", bkc, k_bk, "keyT", "l3.keyT")
        proj("lnT", "l3.lnT", wv, "l3.w_L3_attn_Wv", bvc, k_bv, "valT", "l3.valT")
        proj("lnT", "l3.lnT", wq, "l3.w_L3_attn_Wq", bqc, k_bq, "qTT", "l3.qTT")
        proj("keyT", "l3.keyT", wkp, "l3.w_L3_attn_Wkp", bkpc, k_bkp, "kphF",
             "l3.kphF", tanh_ps=True)
        proj("qTT", "l3.qTT", wqp, "l3.w_L3_attn_Wqp", bqpc, k_bqp, "qphF",
             "l3.qphF", tanh_ps=True)
        for src_slot, src_key, dst_slot, dst_key in (
                ("valT", "l3.valT", "val_t", "l3.val_t"),
                ("kphF", "l3.kphF", "kph_t", "l3.kph_t"),
                ("qphF", "l3.qphF", "qph_t", "l3.qph_t")):
            pbf2 = mk.PBF[1]
            pbk2 = "psBF1"
            for j in range(8):
                p.t(lambda e, j=j, pbf2=pbf2, src_slot=src_slot: e.transpose(
                    pbf2[:, j * 128:(j + 1) * 128],
                    sl_(src_slot)[:, j * 128:(j + 1) * 128], mk.IDB[:, :]),
                    reads=[src_key, "IDB"], writes=[pbk2], tag=u("l3tt"))
            p.v(lambda e, pbf2=pbf2, dst_slot=dst_slot: e.tensor_copy(
                sl_(dst_slot), pbf2[:, :]),
                reads=[pbk2], writes=[dst_key], tag=u("l3tte"))
        for ph_slot, ph_key in (("kph_t", "l3.kph_t"), ("qph_t", "l3.qph_t")):
            t1w = sl_("sin_k")
            t2w = sl_("cos_k")
            p.v(lambda e, ph_slot=ph_slot, t1w=t1w: e.tensor_scalar(
                t1w, sl_(ph_slot), math.pi, None, op0=OP.is_gt),
                reads=[ph_key], writes=["l3.sin_k"], tag=u("l3rr1"))
            p.v(lambda e, ph_slot=ph_slot, t2w=t2w: e.tensor_scalar(
                t2w, sl_(ph_slot), -math.pi, None, op0=OP.is_lt),
                reads=[ph_key], writes=["l3.cos_k"], tag=u("l3rr2"))
            p.v(lambda e, t1w=t1w, t2w=t2w: e.tensor_tensor(
                t1w, t1w, t2w, op=OP.subtract),
                reads=["l3.sin_k", "l3.cos_k"], writes=["l3.sin_k"], tag=u("l3rr3"))
            p.v(lambda e, t1w=t1w: e.tensor_scalar(
                t1w, t1w, -2.0 * math.pi, None, op0=OP.mult),
                reads=["l3.sin_k"], writes=["l3.sin_k"], tag=u("l3rr4"))
            p.v(lambda e, ph_slot=ph_slot, t1w=t1w: e.tensor_tensor(
                sl_(ph_slot), sl_(ph_slot), t1w, op=OP.add),
                reads=[ph_key, "l3.sin_k"], writes=[ph_key], tag=u("l3rr5"))
        for ph_slot, ph_key, snm, cnm in (("kph_t", "l3.kph_t", "sin_k", "cos_k"),
                                          ("qph_t", "l3.qph_t", "sin_q", "cos_q")):
            p.s(lambda e, ph_slot=ph_slot, snm=snm: e.activation(
                sl_(snm), sl_(ph_slot), AF.Sin),
                reads=[ph_key], writes=[f"l3.{snm}"], tag=u("l3sin"))
            p.s(lambda e, ph_slot=ph_slot, cnm=cnm: e.activation(
                sl_(cnm), sl_(ph_slot), AF.Sin, scale=0.5),
                reads=[ph_key], writes=[f"l3.{cnm}"], tag=u("l3sinh"))
            p.s(lambda e, cnm=cnm: e.activation(
                sl_(cnm), sl_(cnm), AF.Square),
                reads=[f"l3.{cnm}"], writes=[f"l3.{cnm}"], tag=u("l3sq"))
            p.v(lambda e, cnm=cnm: e.tensor_scalar(
                sl_(cnm), sl_(cnm), -2.0, 1.0, op0=OP.mult, op1=OP.add),
                reads=[f"l3.{cnm}"], writes=[f"l3.{cnm}"], tag=u("l3cos"))
        p.v(lambda e: e.tensor_tensor(sl_("br_"), sl_("val_t"), sl_("cos_k"), op=OP.mult),
            reads=["l3.val_t", "l3.cos_k"], writes=["l3.br"], tag=u("l3br"))
        p.v(lambda e: e.tensor_tensor(sl_("bi_"), sl_("val_t"), sl_("sin_k"), op=OP.mult),
            reads=["l3.val_t", "l3.sin_k"], writes=["l3.bi"], tag=u("l3bi"))
        psR, (pR0, pR1) = mk.bank_pair(1)
        psI, (pI0, pI1) = mk.bank_pair(2)
        for half in range(2):
            p.t(lambda e, half=half, psR=psR: e.matmul(
                psR[:, half * 512:(half + 1) * 512], mk.TRI[:, :],
                sl_("br_")[:, half * 512:(half + 1) * 512], start=True, stop=True),
                reads=["TRI", "l3.br"], writes=[pR0 if half == 0 else pR1],
                tag=u("l3memr"))
            p.t(lambda e, half=half, psI=psI: e.matmul(
                psI[:, half * 512:(half + 1) * 512], mk.TRI[:, :],
                sl_("bi_")[:, half * 512:(half + 1) * 512], start=True, stop=True),
                reads=["TRI", "l3.bi"], writes=[pI0 if half == 0 else pI1],
                tag=u("l3memi"))
        for pass_ in range(2):
            for half in range(2):
                t0 = F[:, 0:512]
                t1 = F[:, 512:1024]
                rh = F[:, 2048:2560]
                pRk = pR0 if half == 0 else pR1
                pIk = pI0 if half == 0 else pI1
                p.v(lambda e, half=half, t0=t0, psR=psR: e.tensor_tensor(
                    t0, psR[:, half * 512:(half + 1) * 512],
                    sl_("cos_q")[:, half * 512:(half + 1) * 512], op=OP.mult),
                    reads=[pRk, "l3.cos_q"], writes=["l3.t0", kx], tag=u("l3r0"))
                p.v(lambda e, half=half, t1=t1, psI=psI: e.tensor_tensor(
                    t1, psI[:, half * 512:(half + 1) * 512],
                    sl_("sin_q")[:, half * 512:(half + 1) * 512], op=OP.mult),
                    reads=[pIk, "l3.sin_q"], writes=["l3.t1", kx], tag=u("l3r1"))
                p.v(lambda e, t0=t0, t1=t1, rh=rh: e.tensor_tensor(
                    rh, t0, t1, op=OP.add),
                    reads=["l3.t0", "l3.t1"], writes=["l3.rh"], tag=u("l3r2"))
                if pass_ == 0:
                    p.v(lambda e, rh=rh, st=st, half=half: e.tensor_reduce(
                        st[:, 6 + half:7 + half], rh, axis=AX.X, op=OP.add),
                        reads=["l3.rh"], writes=[sk], tag=u("l3rs"))
                else:
                    nb2 = mk.TOKB[:, 1024:2048]
                    p.s(lambda e, rh=rh, st=st, half=half, nb2=nb2: e.activation(
                        nb2[:, half * 512:(half + 1) * 512], rh, AF.Identity,
                        scale=st[:, 4:5], bias=st[:, 5:6]),
                        reads=["l3.rh", sk], writes=["l3.nb2"], tag=u("l3nn"))
            if pass_ == 0:
                p.v(lambda e, st=st: e.tensor_tensor(
                    st[:, 0:1], st[:, 6:7], st[:, 7:8], op=OP.add),
                    reads=[sk], writes=[sk], tag=u("l3ra"))
                p.v(lambda e, st=st: e.tensor_scalar(
                    st[:, 1:2], st[:, 0:1], -1.0 / D, None, op0=OP.mult),
                    reads=[sk], writes=[sk], tag=u("l3rb"))
                for half in range(2):
                    t0 = F[:, 0:512]
                    t1 = F[:, 512:1024]
                    rh = F[:, 2048:2560]
                    pRk = pR0 if half == 0 else pR1
                    pIk = pI0 if half == 0 else pI1
                    p.v(lambda e, half=half, t0=t0, psR=psR: e.tensor_tensor(
                        t0, psR[:, half * 512:(half + 1) * 512],
                        sl_("cos_q")[:, half * 512:(half + 1) * 512], op=OP.mult),
                        reads=[pRk, "l3.cos_q"], writes=["l3.t0", kx], tag=u("l3q0"))
                    p.v(lambda e, half=half, t1=t1, psI=psI: e.tensor_tensor(
                        t1, psI[:, half * 512:(half + 1) * 512],
                        sl_("sin_q")[:, half * 512:(half + 1) * 512], op=OP.mult),
                        reads=[pIk, "l3.sin_q"], writes=["l3.t1", kx], tag=u("l3q1"))
                    p.v(lambda e, t0=t0, t1=t1, rh=rh: e.tensor_tensor(
                        rh, t0, t1, op=OP.add),
                        reads=["l3.t0", "l3.t1"], writes=["l3.rh"], tag=u("l3q2"))
                    p.s(lambda e, rh=rh, st=st, junk=junk, half=half: e.activation(
                        junk[:, 0:512], rh, AF.Square, bias=st[:, 1:2],
                        accum_out=st[:, 6 + half:7 + half]),
                        reads=["l3.rh", sk], writes=["l3.junk", sk], tag=u("l3qs"))
                p.v(lambda e, st=st: e.tensor_tensor(
                    st[:, 2:3], st[:, 6:7], st[:, 7:8], op=OP.add),
                    reads=[sk], writes=[sk], tag=u("l3qa"))
                p.s(lambda e, st=st: e.activation(
                    st[:, 3:4], st[:, 2:3], AF.Ln, scale=1.0 / (D * 1024.0), bias=eps_c),
                    reads=[sk, "cpool"], writes=[sk], tag=u("l3ql"))
                p.s(lambda e, st=st: e.activation(
                    st[:, 4:5], st[:, 3:4], AF.Exp, scale=-0.5),
                    reads=[sk], writes=[sk], tag=u("l3qe"))
                p.v(lambda e, st=st: e.tensor_scalar(
                    st[:, 4:5], st[:, 4:5], 1.0 / 32.0, None, op0=OP.mult),
                    reads=[sk], writes=[sk], tag=u("l3qf"))
                p.v(lambda e, st=st: e.tensor_tensor(
                    st[:, 5:6], st[:, 1:2], st[:, 4:5], op=OP.mult),
                    reads=[sk], writes=[sk], tag=u("l3qg"))
        nb2 = mk.TOKB[:, 1024:2048]
        pbf = mk.PBF[0]
        for j in range(8):
            p.t(lambda e, j=j, pbf=pbf, nb2=nb2: e.transpose(
                pbf[:, j * 128:(j + 1) * 128], nb2[:, j * 128:(j + 1) * 128],
                mk.IDB[:, :]),
                reads=["l3.nb2", "IDB"], writes=["psBF0"], tag=u("l3ntr"))
        p.v(lambda e, pbf=pbf: e.tensor_copy(sl_("nT"), pbf[:, :]),
            reads=["psBF0"], writes=["l3.nT"], tag=u("l3nev"))
        for half in range(2):
            psW, pkW = mk.bank(0 if half == 0 else 1)
            p.dma("sync", lambda e, half=half: e.dma_start(
                out=mk.ROWS[0:1, 0:512], in_=bo_dram[0:1, half * 512:(half + 1) * 512]),
                reads=["l3.bo"], writes=["ROWS"], tag=u("l3bold"))
            p.t(lambda e, psW=psW: e.matmul(
                psW, mk.ONESF[:, 0:128], mk.ROWS[0:1, 0:512], start=True, stop=False),
                reads=["ROWS", "ONESF"], writes=[pkW], tag=u("l3wb"))
            for kp in range(8):
                p.t(lambda e, kp=kp, half=half, psW=psW: e.matmul(
                    psW, sl_("nT")[:, kp * 128:(kp + 1) * 128],
                    wo.sl(kp, half * 512, 512), start=False, stop=(kp == 7)),
                    reads=["l3.nT", "l3.w_L3_attn_Wo"], writes=[pkW], tag=u("l3wmm"))
            ot = F[:, 0 + half * 512:512 + half * 512]
            p.v(lambda e, psW=psW, ot=ot, ln=ln, half=half: e.tensor_tensor(
                ot, psW, ln[:, half * 512:(half + 1) * 512], op=OP.add),
                reads=[pkW, lk], writes=["l3.ot", kx], tag=u("l3oa"))
            xrl = F[:, 2048:2560]
            p.dma("sync", lambda e, xrl=xrl, t=t, half=half: e.dma_start(
                out=xrl, in_=xin[t * 128:(t + 1) * 128, half * 512:(half + 1) * 512]),
                reads=[xin_key], writes=["l3.xrl", "l3.rh"], tag=u("l3xrl"))
            p.v(lambda e, ot=ot, xrl=xrl: e.tensor_tensor(ot, ot, xrl, op=OP.add),
                reads=["l3.ot", "l3.xrl"], writes=["l3.ot"], tag=u("l3ob"))
            p.dma("scalar", lambda e, ot=ot, t=t, half=half: e.dma_start(
                out=xout[t * 128:(t + 1) * 128, half * 512:(half + 1) * 512], in_=ot),
                reads=["l3.ot"], writes=[xout_key], tag=u("l3ost"))
    p.barrier()


def ph_l3_ffn(mk, xin, xin_key, xout, xout_key):
    p, T, PT = mk.p, mk.T, mk.PT
    u = mk.u
    F = mk.F32P
    NE8 = PT * 8
    lnT = FMView(mk.A, 0, 8, T)
    acc_dram = mk.dram["l1f_acc"]
    w1d, w2d, w3d = mk.dram["l3f_w1"], mk.dram["l3f_w2"], mk.dram["l3f_w3"]
    for e_ in range(8):
        p.dma("gpsimd", lambda e, e_=e_: e.dma_start(
            out=w1d[e_], in_=mk.dram["L3_ffn_W1"][e_]),
            writes=["l3f.wbf"], tag=u("l3fc1"))
        p.dma("gpsimd", lambda e, e_=e_: e.dma_start(
            out=w2d[e_], in_=mk.dram["L3_ffn_W2"][e_]),
            writes=["l3f.wbf"], tag=u("l3fc2"))
        p.dma("gpsimd", lambda e, e_=e_: e.dma_start(
            out=w3d[e_], in_=mk.dram["L3_ffn_W3"][e_]),
            writes=["l3f.wbf"], tag=u("l3fc3"))
    rwT = F[:, 2048:2112]
    mk.load_rwT("L3_ffn_Wg", rwT, "l3f.rwT", "l3f")
    p.barrier()
    logits = F[:, 2112:2112 + NE8]
    hook = mk.router_tile_hook2("l3f", rwT, "l3f.rwT", logits, "l3f.logits",
                                lnT, "l3f.lnT")
    mk.ln_transpose(xin_key, xin, lnT, "l3f.lnT", prefix="l3fln", hook=hook)
    p.barrier()
    oh1 = F[:, 512:512 + NE8]
    oh2 = F[:, 768:768 + NE8]
    tmp = F[:, 1024:1024 + NE8]
    iotaB = F[:, 1280:1280 + NE8]
    w_ = F[:, 2112:2112 + NE8]
    m1 = F[:, 1792:1792 + PT]
    m2 = F[:, 1824:1824 + PT]
    idx = F[:, 1856:1856 + PT]
    tt_ = F[:, 1888:1888 + PT]
    rr_ = F[:, 1920:1920 + PT]
    p2_ = F[:, 1952:1952 + PT]
    v3 = lambda ap: ap.rearrange("p (t e) -> p t e", t=PT)
    v2 = lambda ap: ap.rearrange("p (t one) -> p t one", one=1)
    p.g(lambda e: e.iota(iotaB, pattern=[[0, PT], [1, 8]], base=0,
                         channel_multiplier=0, allow_small_or_imprecise_dtypes=True),
        writes=["l3f.iota"], tag=u("l3f.iota"))
    mk.top2("l3f", logits, "l3f.logits", PT, dict(
        oh1=(oh1, "l3f.oh1"), oh2=(oh2, "l3f.oh2"), m1=(m1, "l3f.m1"),
        m2=(m2, "l3f.m2"), tmp=(tmp, "l3f.tmp"), idx=(idx, "l3f.idx"),
        iotaB=(iotaB, "l3f.iota")))
    p.v(lambda e: e.tensor_tensor(tt_, m2, m1, op=OP.subtract),
        reads=["l3f.m2", "l3f.m1"], writes=["l3f.tt"], tag=u("l3f.p0"))
    p.s(lambda e: e.activation(tt_, tt_, AF.Exp),
        reads=["l3f.tt"], writes=["l3f.tt"], tag=u("l3f.p1"))
    p.v(lambda e: e.tensor_scalar(rr_, tt_, 1.0, None, op0=OP.add),
        reads=["l3f.tt"], writes=["l3f.rr"], tag=u("l3f.p2"))
    p.v(lambda e: e.reciprocal(rr_, rr_),
        reads=["l3f.rr"], writes=["l3f.rr"], tag=u("l3f.p3"))
    p.v(lambda e: e.tensor_tensor(p2_, tt_, rr_, op=OP.mult),
        reads=["l3f.tt", "l3f.rr"], writes=["l3f.p2c"], tag=u("l3f.p4"))
    mk.bTT(v3(tmp), v3(oh1), v2(rr_), OP.mult,
           reads=["l3f.oh1", "l3f.rr"], writes=["l3f.tmp"], tag=u("l3f.w1"))
    mk.bTT(v3(w_), v3(oh2), v2(p2_), OP.mult,
           reads=["l3f.oh2", "l3f.p2c"], writes=["l3f.w", "l3f.logits"], tag=u("l3f.w2"))
    p.v(lambda e: e.tensor_tensor(w_, w_, tmp, op=OP.add),
        reads=["l3f.w", "l3f.tmp"], writes=["l3f.w"], tag=u("l3f.w3"))
    z0 = F[:, 0:512]
    p.v(lambda e, z0=z0: e.memset(z0, 0.0), writes=["l3f.z0"], tag=u("l3f.z0"))
    for t in range(PT):
        for half in range(2):
            p.dma("scalar", lambda e, z0=z0, t=t, half=half: e.dma_start(
                out=acc_dram[t * 128:(t + 1) * 128, half * 512:(half + 1) * 512],
                in_=z0),
                reads=["l3f.z0"], writes=[f"l3f.acc_{t}_{half}"], tag=u("l3f.zst"))
    p.barrier()
    s1T = FMView(mk.M, 0, 4, 512)
    h2T = FMView(mk.M, 2048, 4, 512)
    for e_ in range(8):
        for hq in range(4):
            qb_ = (e_ * 4 + hq) % 2
            off = qb_ * 12288
            w1v = FMView(mk.B, off, 8, 512)
            w2v = FMView(mk.B, off + 4096, 8, 512)
            w3v = FMView(mk.B, off + 8192, 4, 1024)
            kw1, kw2, kw3 = (f"l3f.w1v{qb_}", f"l3f.w2v{qb_}", f"l3f.w3v{qb_}")
            for j in range(8):
                p.dma("sync", lambda e, e_=e_, hq=hq, j=j, off=off: e.dma_start(
                    out=mk.B[:, off + j * 512:off + (j + 1) * 512],
                    in_=w1d[e_][hq * 512:(hq + 1) * 512, j * 128:(j + 1) * 128],
                    transpose=True),
                    reads=["l3f.wbf"], writes=[kw1], tag=u("l3f.t1"))
                p.dma("sync", lambda e, e_=e_, hq=hq, j=j, off=off: e.dma_start(
                    out=mk.B[:, off + 4096 + j * 512:off + 4096 + (j + 1) * 512],
                    in_=w2d[e_][hq * 512:(hq + 1) * 512, j * 128:(j + 1) * 128],
                    transpose=True),
                    reads=["l3f.wbf"], writes=[kw2], tag=u("l3f.t2"))
            for j in range(4):
                p.dma("sync", lambda e, e_=e_, hq=hq, j=j, off=off: e.dma_start(
                    out=mk.B[:, off + 8192 + j * 1024:off + 8192 + (j + 1) * 1024],
                    in_=w3d[e_][:, hq * 512 + j * 128:hq * 512 + (j + 1) * 128],
                    transpose=True),
                    reads=["l3f.wbf"], writes=[kw3], tag=u("l3f.t3"))
            for tb in range(0, T, 512):
                for m in range(4):
                    ps, pkb = mk.bank(m % 2)
                    for kp in range(8):
                        p.t(lambda e, ps=ps, m=m, kp=kp, tb=tb, w1v=w1v: e.matmul(
                            ps, w1v.sl(kp, m * 128, 128), lnT.sl(kp, tb, 512),
                            start=(kp == 0), stop=(kp == 7)),
                            reads=[kw1] + [f"l3f.lnT.t{tb // 128 + i}" for i in range(4)], writes=[pkb], tag=u("l3f.h1"))
                    p.s(lambda e, ps=ps, m=m: e.activation(
                        s1T.sl(m, 0, 512), ps, AF.Silu),
                        reads=[pkb], writes=[f"l3f.s1_{m}"], tag=u("l3f.silu"))
                for m in range(4):
                    ps, pkb = mk.bank(2 + m % 2)
                    for kp in range(8):
                        p.t(lambda e, ps=ps, m=m, kp=kp, tb=tb, w2v=w2v: e.matmul(
                            ps, w2v.sl(kp, m * 128, 128), lnT.sl(kp, tb, 512),
                            start=(kp == 0), stop=(kp == 7)),
                            reads=[kw2] + [f"l3f.lnT.t{tb // 128 + i}" for i in range(4)], writes=[pkb], tag=u("l3f.h2"))
                    p.v(lambda e, ps=ps, m=m: e.tensor_copy(h2T.sl(m, 0, 512), ps),
                        reads=[pkb], writes=[f"l3f.h2s{m}"], tag=u("l3f.h2e"))
                for m in range(4):
                    p.v(lambda e, m=m: e.tensor_tensor(
                        mk.M[:, m * 512:(m + 1) * 512], mk.M[:, m * 512:(m + 1) * 512],
                        mk.M[:, 2048 + m * 512:2048 + (m + 1) * 512], op=OP.mult),
                        reads=[f"l3f.s1_{m}", f"l3f.h2s{m}"], writes=[f"l3f.s1_{m}"],
                        tag=u("l3f.g"))
                for tl in range(4):
                    t = tb // 128 + tl
                    for hp in range(4):
                        for hd in range(2):
                            ps, pkb = mk.bank(4 + hd)
                            p.t(lambda e, ps=ps, hp=hp, tl=tl, hd=hd, w3v=w3v: e.matmul(
                                ps, s1T.sl(hp, tl * 128, 128), w3v.sl(hp, hd * 512, 512),
                                start=(hp == 0), stop=(hp == 3)),
                                reads=[f"l3f.s1_{hp}", kw3], writes=[pkb], tag=u("l3f.y"))
                    pr = tl % 2
                    for hd in range(2):
                        ps, pkb = mk.bank(4 + hd)
                        stg = F[:, pr * 1024 + hd * 512:pr * 1024 + (hd + 1) * 512]
                        sk = f"l3f.stg{pr}{hd}"
                        p.v(lambda e, ps=ps, stg=stg, t=t, e_=e_: e.tensor_scalar(
                            stg, ps, w_[:, t * 8 + e_:t * 8 + e_ + 1], None, op0=OP.mult),
                            reads=[pkb, "l3f.w"], writes=[sk], tag=u("l3f.scl"))
                        p.dma("gpsimd", lambda e, stg=stg, t=t, hd=hd: e.dma_start(
                            out=acc_dram[t * 128:(t + 1) * 128,
                                         hd * 512:(hd + 1) * 512],
                            in_=stg, accum_op=OP.add),
                            reads=[sk], writes=[f"l3f.acc_{t}_{hd}"],
                            tag=u("l3f.ac"))
    p.barrier()
    for t in range(PT):
        z = F[:, 0:1024]
        n = F[:, 1024:2048]
        p.dma("sync", lambda e, z=z, t=t: e.dma_start(
            out=z, in_=acc_dram[t * 128:(t + 1) * 128, :]),
            reads=[f"l3f.acc_{t}_0", f"l3f.acc_{t}_1"], writes=["l3f.z", "l3f.stg"],
            tag=u("l3f.fl1"))
        p.dma("sync", lambda e, n=n, t=t: e.dma_start(
            out=n, in_=xin[t * 128:(t + 1) * 128, :]),
            reads=[xin_key], writes=["l3f.n"], tag=u("l3f.fl2"))
        p.v(lambda e, z=z, n=n: e.tensor_tensor(z, z, n, op=OP.add),
            reads=["l3f.z", "l3f.n"], writes=["l3f.z"], tag=u("l3f.fadd"))
        p.dma("scalar", lambda e, z=z, t=t: e.dma_start(
            out=xout[t * 128:(t + 1) * 128, :], in_=z),
            reads=["l3f.z"], writes=[xout_key], tag=u("l3f.fst"))
    p.barrier()


def _fm_pass(self, wview, wkey, srcview, srckey, nm, nkp, T_, evac, tag,
             tbw=512):
    p = self.p
    u = self.u
    nTB = T_ // tbw
    for m in range(nm):
        for g0 in range(0, nTB, 3):
            gn = min(3, nTB - g0)
            par = (g0 // 3) % 2
            banks = [self.bank(par * 3 + j) for j in range(gn)]
            for kp in range(nkp):
                for j in range(gn):
                    ps, pk = banks[j]
                    tb = (g0 + j) * tbw
                    p.t(lambda e, ps=ps, m=m, kp=kp, tb=tb: e.matmul(
                        ps[:, 0:tbw], wview.sl(kp, m * 128, 128),
                        srcview.sl(kp, tb, tbw),
                        start=(kp == 0), stop=(kp == nkp - 1)),
                        reads=[wkey, srckey], writes=[pk], tag=u(tag))
            for j in range(gn):
                ps, pk = banks[j]
                evac(m, (g0 + j) * tbw, ps[:, 0:tbw], pk)


MK.fm_pass = _fm_pass
